# version 07
# baseline (speedup 1.0000x reference)
from collections import deque

import numpy as np

import concourse.bass as bass
import concourse.mybir as mybir
from concourse.bass_utils import run_bass_kernel_spmd

F32 = mybir.dt.float32
BF16 = mybir.dt.bfloat16
U8 = mybir.dt.uint8
AF = mybir.ActivationFunctionType
ALU = mybir.AluOpType

NCORES = 8
C = 2048
INTER = 4096
FF = 8192
T = 1024
KC = C // 128
KI = INTER // 128
KF = FF // 128
NT = 2
MAGIC = 12582912.0
GK = 0.7978845608028654
GC = 0.044715
QMAX = 32767.0
QMIN = -32768.0
EPS = 1e-5
JMOD = 12288
NBLK = 24

ENGS = ("sync", "gpsimd", "tensor", "vector", "scalar")
SEMS = ("sW", "sX", "sS", "sPE", "sV", "sA", "sC", "sG")


class Sched:
    def __init__(self):
        self.items = {e: [] for e in ENGS}
        self.cnt = {s: 0 for s in SEMS}
        self.last_wait = {e: {s: 0 for s in SEMS} for e in ENGS}

    def wait(self, eng, sem, val):
        assert 0 <= val <= self.cnt[sem], (eng, sem, val, self.cnt[sem])
        if val <= self.last_wait[eng][sem]:
            return
        self.last_wait[eng][sem] = val
        self.items[eng].append(lambda e, S, sem=sem, val=val: e.wait_ge(S[sem], val))

    def op(self, eng, build, inc=None, n=1):
        if inc is None:
            self.items[eng].append(lambda e, S, b=build: b(e))
            return None
        self.cnt[inc] += n
        v = self.cnt[inc]
        self.items[eng].append(lambda e, S, b=build, s=inc, m=n: b(e).then_inc(S[s], m))
        return v


def place(nc, base_addr, name, shape, dtype, offset):
    h = nc.virtual_sbuf_tensor(name, shape, dtype)
    nc._tensor(h.name, h.shape, h.dtype, type="SB")
    mloc = nc.lookup_mloc(h)
    mloc.allocated = True
    mloc.addr = base_addr + offset
    return h


def build_program(debug=False):
    nc = bass.Bass("TRN2", target_bir_lowering=False)

    d_x = nc.dram_tensor("xT", [C, T], F32, kind="ExternalInput").ap()
    d_win = nc.dram_tensor("winTM", [FF // 128, 128, C], F32, kind="ExternalInput").ap()
    d_conv = nc.dram_tensor("convTM", [KI, 128, INTER], F32, kind="ExternalInput").ap()
    d_wout = nc.dram_tensor("woutTM", [KC, 128, INTER], F32, kind="ExternalInput").ap()
    d_w1 = nc.dram_tensor("w1TM", [KF, 128, C], F32, kind="ExternalInput").ap()
    d_w2 = nc.dram_tensor("w2TM", [KC, 128, FF], F32, kind="ExternalInput").ap()
    d_wmodB = nc.dram_tensor("wmodB", [32, 128, 4096], F32, kind="ExternalInput").ap()
    d_wmodF = nc.dram_tensor("wmodF", [32, 128, 2048], F32, kind="ExternalInput").ap()
    d_c = nc.dram_tensor("cT", [128, 16], F32, kind="ExternalInput").ap()
    d_D = nc.dram_tensor("dT", [128, 32], F32, kind="ExternalInput").ap()
    d_out = nc.dram_tensor("outT", [C, T], F32, kind="ExternalOutput").ap()
    d_x2 = nc.dram_tensor("x2spill", [C, T], F32,
                          kind="ExternalOutput" if debug else "Internal").ap()
    if debug:
        d_dh = nc.dram_tensor("dbg_h", [128, 16384], F32, kind="ExternalOutput").ap()
        d_dxh = nc.dram_tensor("dbg_xh", [128, 32768], F32, kind="ExternalOutput").ap()
        d_dsz = nc.dram_tensor("dbg_sz", [128, 32768], F32, kind="ExternalOutput").ap()
        d_dyz = nc.dram_tensor("dbg_yz", [128, 32768], F32, kind="ExternalOutput").ap()
        d_dh2 = nc.dram_tensor("dbg_h2", [128, 16384], F32, kind="ExternalOutput").ap()
        d_da = nc.dram_tensor("dbg_a", [128, 65536], F32, kind="ExternalOutput").ap()
        d_dmodt = nc.dram_tensor("dbg_modt", [128, 96], F32, kind="ExternalOutput").ap()
        d_dpar = nc.dram_tensor("dbg_par", [128, 80], F32, kind="ExternalOutput").ap()
        d_dbc = nc.dram_tensor("dbg_bc", [128, 2048], F32, kind="ExternalOutput").ap()
    d_mod = nc.dram_tensor("modscr", [96, 128], F32, kind="Internal").ap()

    P = Sched()
    CP = {}

    OFF = {}
    cur = 0

    def A(name, nbytes):
        nonlocal cur
        OFF[name] = cur
        cur += (nbytes + 31) // 32 * 32

    A("wbuf", 16384)
    A("rowbuf", 2048)
    A("rowbuf2", 2048)
    A("ones_row", 512)
    A("ones_col", 32)
    A("silu_cb", 32)
    A("cd", 64)
    A("modt", 384)
    A("par", 320)
    A("magic", 32)
    A("dt", 128)
    A("m96", 512)
    A("RA", 65536)
    A("RB", 65536)
    A("RC", 32768)
    A("WM", 24576)
    assert cur <= 211000, cur

    with (
        nc.sbuf_tensor("arena", [128, cur], U8) as arena,
        nc.psum_tensor("mm_ps", [128, 2048], F32) as mm_ps,
        nc.psum_tensor("bc_ps", [128, 1024], F32) as bc_ps,
        nc.psum_tensor("row_ps", [1, 512], F32) as row_ps,
        nc.psum_tensor("mod_ps", [1, 512], F32) as mod_ps,
        nc.semaphore("sW") as _sW,
        nc.semaphore("sX") as _sX,
        nc.semaphore("sS") as _sS,
        nc.semaphore("sPE") as _sPE,
        nc.semaphore("sV") as _sV,
        nc.semaphore("sA") as _sA,
        nc.semaphore("sC") as _sC,
        nc.semaphore("sG") as _sG,
        nc.Block() as block,
    ):
        SEM = {"sW": _sW, "sX": _sX, "sS": _sS, "sPE": _sPE, "sV": _sV,
               "sA": _sA, "sC": _sC, "sG": _sG}
        base = nc.lookup_mloc(arena).addr

        wbuf = place(nc, base, "v_wbuf", [128, 8192], BF16, OFF["wbuf"])
        rstd_bc = place(nc, base, "v_rstd_bc", [128, 1024], F32, OFF["RB"] + 28672)
        mrstd_bc = place(nc, base, "v_mrstd_bc", [128, 1024], F32, OFF["RB"] + 32768)
        rowb = place(nc, base, "v_rowb", [1, 512], F32, OFF["rowbuf"])
        rowb2 = place(nc, base, "v_rowb2", [1, 512], F32, OFF["rowbuf2"])
        ones_row = place(nc, base, "v_ones_row", [1, 128], F32, OFF["ones_row"])
        ones_col = place(nc, base, "v_ones_col", [128, 16], BF16, OFF["ones_col"])
        silu_cb = place(nc, base, "v_silu_cb", [128, 16], BF16, OFF["silu_cb"])
        cd = place(nc, base, "v_cd", [128, 16], F32, OFF["cd"])
        modt = place(nc, base, "v_modt", [128, 96], F32, OFF["modt"])
        par = place(nc, base, "v_par", [128, 80], F32, OFF["par"])
        magic = place(nc, base, "v_magic", [128, 1], F32, OFF["magic"])
        Dt = place(nc, base, "v_dt", [128, 32], F32, OFF["dt"])
        m96 = place(nc, base, "v_m96", [96, 128], F32, OFF["m96"])
        RA_x = place(nc, base, "v_RA_x", [128, 16384], F32, OFF["RA"])
        RA_xh = place(nc, base, "v_RA_xh", [128, 32768], BF16, OFF["RA"])
        RA_x2 = place(nc, base, "v_RA_x2", [128, 16384], F32, OFF["RA"])
        RA_alo = place(nc, base, "v_RA_alo", [128, 32768], BF16, OFF["RA"])
        RB_sz = place(nc, base, "v_RB_sz", [128, 32768], BF16, OFF["RB"])
        RB_sum = place(nc, base, "v_RB_sum", [1, 1024], F32, OFF["RB"])
        RB_sq = place(nc, base, "v_RB_sq", [1, 1024], F32, OFF["RB"] + 4096)
        RB_rstd = place(nc, base, "v_RB_rstd", [1, 1024], F32, OFF["RB"] + 8192)
        RB_mrstd = place(nc, base, "v_RB_mrstd", [1, 1024], F32, OFF["RB"] + 12288)
        RB_htmp = place(nc, base, "v_RB_htmp", [128, 2048], F32, OFF["RB"] + 16384)
        RB_xbf = place(nc, base, "v_RB_xbf", [128, 8192], BF16, OFF["RB"] + 36864)
        RC_h = place(nc, base, "v_RC_h", [128, 16384], BF16, OFF["RC"])
        WM_b = [place(nc, base, "v_WM_b%d" % s, [128, 4096], BF16,
                      OFF["WM"] + 8192 * s) for s in range(3)]
        WM_f = [place(nc, base, "v_WM_f%d" % s, [128, 2048], F32,
                      OFF["WM"] + 8192 * s) for s in range(3)]
        WM_g = place(nc, base, "v_WM_g", [128, 2048], F32, OFF["WM"])
        WM_stcv = place(nc, base, "v_WM_stcv", [128, 1024], BF16, OFF["WM"])
        WM_out = place(nc, base, "v_WM_out", [128, 2048], F32, OFF["WM"] + 8192)
        WM_jit = place(nc, base, "v_WM_jit", [128, 2048], F32, OFF["WM"] + 16384)

        def col(t, j):
            return t[:, j : j + 1]

        bank_free = [None] * 4
        unit_free = [None] * 4
        group_ctr = [0]
        unit_ctr = [0]

        def mm_bank_ap(b):
            return mm_ps[:, 512 * b : 512 * b + 512]

        def pe_wait(sem, cnt):
            P.wait("tensor", sem, cnt)

        def stage(wdram, Ktot, Mch, m_off, rhs_ap, rhs_ready, evac, hook=None,
                  posthook=None, npieces=1):
            Kch = Ktot // 128
            upg = Ktot // 2048
            assert upg in (1, 2, 4) and upg % npieces == 0
            upp = upg // npieces
            kpp = Kch // npieces
            ring_slots = 4 // upp
            total_pieces = Mch * npieces
            piece_info = []
            issued = [0]

            def issue_piece(p):
                u0 = unit_ctr[0] % 4
                for uu in range(u0, u0 + upp):
                    if unit_free[uu] is not None:
                        P.wait("gpsimd", *unit_free[uu])
                m = p // npieces
                pc = p % npieces
                src = wdram[m_off + m, :, pc * kpp * 128 : (pc + 1) * kpp * 128]
                dst = wbuf[:, u0 * 2048 : u0 * 2048 + kpp * 128]
                wv = P.op("gpsimd", lambda e, s=src, d=dst: e.dma_start(out=d, in_=s),
                          inc="sW", n=16)
                piece_info.append((u0, wv))
                unit_ctr[0] += upp

            def ensure(upto):
                while issued[0] < min(total_pieces, upto):
                    issue_piece(issued[0])
                    issued[0] += 1

            ensure(npieces + max(0, ring_slots - npieces))
            for m in range(Mch):
                if hook is not None:
                    hook(m)
                ensure((m + 1) * npieces + max(0, ring_slots - npieces))
                g = group_ctr[0]
                group_ctr[0] += 2
                b0, b1 = g % 4, (g + 1) % 4
                for b in (b0, b1):
                    if bank_free[b] is not None:
                        pe_wait(*bank_free[b])
                banks = (b0, b1)
                v_final = None
                for k in range(Kch):
                    pc = k // kpp
                    u0, wv = piece_info[m * npieces + pc]
                    pe_wait("sW", wv)
                    lhs = wbuf[:, u0 * 2048 + (k % kpp) * 128 :
                               u0 * 2048 + (k % kpp) * 128 + 128]
                    piece_end = (k + 1) % kpp == 0
                    for n in range(NT):
                        r = rhs_ready(k, n)
                        if r is not None:
                            pe_wait(*r)
                        is_inc = piece_end and n == NT - 1
                        v = P.op("tensor",
                                 lambda e, o=mm_bank_ap(banks[n]), l=lhs,
                                 r2=rhs_ap(k, n), st=(k == 0), sp=(k == Kch - 1):
                                 e.matmul(o, l, r2, start=st, stop=sp),
                                 inc="sPE" if is_inc else None)
                        if is_inc:
                            v_final = v
                    if piece_end:
                        for uu in range(u0, u0 + upp):
                            unit_free[uu] = ("sPE", v_final)
                f0 = evac(m, 0, mm_bank_ap(b0), v_final)
                f1 = evac(m, 1, mm_bank_ap(b1), v_final)
                bank_free[b0] = f0
                bank_free[b1] = f1
                if posthook is not None:
                    posthook(m)

        P.op("vector", lambda e: e.memset(ones_col[:, :], 1.0))
        P.op("vector", lambda e: e.memset(ones_row[:, :], 1.0))
        v_const = P.op("vector", lambda e: e.memset(magic[:, :], MAGIC), inc="sV")

        x_c = P.op("sync", lambda e: e.dma_start(out=cd[:, :], in_=d_c), inc="sX", n=16)
        P.op("sync", lambda e: e.dma_start(out=Dt[:, :], in_=d_D), inc="sX", n=16)
        x_x = None
        for gq in range(4):
            src = d_x[512 * gq : 512 * gq + 512, :].rearrange("(kc p) t -> p kc t", p=128)
            dst = RA_x[:, 4096 * gq : 4096 * gq + 4096].rearrange("p (kc t) -> p kc t", t=T)
            x_x = P.op("sync", lambda e, s=src, d=dst: e.dma_start(out=d, in_=s),
                       inc="sX", n=16)

        P.wait("scalar", "sX", x_c)
        a_silu = P.op("scalar", lambda e: e.activation(cd[:, :], cd[:, :], AF.Silu),
                      inc="sA")
        P.wait("scalar", "sA", a_silu)
        a_siluc = P.op("scalar", lambda e: e.copy(silu_cb[:, :], cd[:, :]), inc="sA")

        def stats(srcf32, between=None):
            ev = None
            for n in range(NT):
                for sq in (0, 1):
                    if between is not None:
                        between()
                    av = None
                    for k in range(KC):
                        xin = srcf32[:, k * 1024 + n * 512 : k * 1024 + n * 512 + 512]
                        cv = RB_xbf[:, k * 512 : k * 512 + 512]
                        if sq == 0:
                            av = P.op("scalar", lambda e, o=cv, i=xin: e.copy(o, i),
                                      inc="sA" if k == KC - 1 else None)
                        else:
                            av = P.op("scalar",
                                      lambda e, o=cv, i=xin: e.activation(o, i, AF.Square),
                                      inc="sA" if k == KC - 1 else None)
                    P.wait("tensor", "sA", av)
                    pv = None
                    for k in range(KC):
                        cv = RB_xbf[:, k * 512 : k * 512 + 512]
                        pv = P.op("tensor",
                                  lambda e, r=cv, st=(k == 0), sp=(k == KC - 1):
                                  e.matmul(row_ps[:, :], ones_col[:, 0:1], r,
                                           start=st, stop=sp),
                                  inc="sPE" if k == KC - 1 else None)
                    P.wait("scalar", "sPE", pv)
                    P.wait("vector", "sPE", pv)
                    dst = (RB_sum if sq == 0 else RB_sq)[0:1, n * 512 : n * 512 + 512]
                    ev = P.op("vector", lambda e, d=dst: e.tensor_copy(d, row_ps[:, :]),
                              inc="sV")
                    P.wait("tensor", "sV", ev)
            return ev

        P.wait("scalar", "sX", x_x)

        PIECES = []
        for i in range(16):
            PIECES.append((256 * i, 256, "b"))
        for i in range(16):
            PIECES.append((6144 + 256 * i, 256, "b"))
        for i in range(16):
            PIECES.append((4096 + 128 * i, 128, "f"))
        for i in range(16):
            PIECES.append((10240 + 128 * i, 128, "f"))

        PIECE_SUB = []
        nb = nf = 0
        for (_j0, _w, _k) in PIECES:
            if _k == "b":
                PIECE_SUB.append(nb); nb += 1
            else:
                PIECE_SUB.append(nf); nf += 1

        mod_state = {"mm": [0, 0, 0], "store": [0, 0], "idx": 0, "last_mm": 0,
                     "dma_idx": 0, "dma_v": {}}

        def mod_piece_dma():
            i = mod_state["dma_idx"]
            if i >= len(PIECES):
                return
            mod_state["dma_idx"] += 1
            j0, wdt, kind = PIECES[i]
            s = i % 3
            if mod_state["mm"][s]:
                P.wait("gpsimd", "sPE", mod_state["mm"][s])
            if kind == "b":
                srcw = d_wmodB[PIECE_SUB[i], :, :]
                slot = WM_b[s][:, :]
            else:
                srcw = d_wmodF[PIECE_SUB[i], :, :]
                slot = WM_f[s][:, :]
            mod_state["dma_v"][i] = P.op(
                "gpsimd", lambda e, s_=srcw, d_=slot: e.dma_start(out=d_, in_=s_),
                inc="sW", n=16)

        def mod_piece():
            if mod_state["dma_idx"] <= mod_state["idx"]:
                mod_piece_dma()
            i = mod_state["idx"]
            mod_state["idx"] += 1
            j0, wdt, kind = PIECES[i]
            s = i % 3
            wv = mod_state["dma_v"][i]
            pe_wait("sW", wv)
            pe_wait("sA", a_siluc)
            pv = None
            for kc in range(KC):
                if kind == "b":
                    lhs = silu_cb[:, kc : kc + 1]
                    rhs = WM_b[s][:, kc * wdt : kc * wdt + wdt]
                else:
                    lhs = cd[:, kc : kc + 1]
                    rhs = WM_f[s][:, kc * wdt : kc * wdt + wdt]
                pv = P.op("tensor",
                          lambda e, l=lhs, r=rhs, st=(kc == 0), sp=(kc == KC - 1),
                          w_=wdt: e.matmul(mod_ps[0:1, 0:w_], l, r, start=st, stop=sp),
                          inc="sPE" if kc == KC - 1 else None)
            mod_state["mm"][s] = pv
            mod_state["last_mm"] = pv
            rs = i % 2
            rb = rowb if rs == 0 else rowb2
            if mod_state["store"][rs]:
                P.wait("vector", "sS", mod_state["store"][rs])
            P.wait("vector", "sPE", pv)
            ev = P.op("vector", lambda e, r=rb, w_=wdt: e.tensor_copy(
                r[0:1, 0:w_], mod_ps[0:1, 0:w_]), inc="sV")
            P.wait("tensor", "sV", ev)
            P.wait("sync", "sV", ev)
            J0 = j0 // 128
            nrow = wdt // 128
            sv = P.op("sync",
                      lambda e, r=rb[0:1, 0:wdt].rearrange("p (a c) -> p a c", c=128),
                      d=d_mod[J0 : J0 + nrow, :].rearrange("a c -> () a c"):
                      e.dma_start(out=d, in_=r),
                      inc="sS", n=16)
            mod_state["store"][rs] = sv

        mod_piece_dma()
        mod_piece_dma()

        def p0_between():
            for _ in range(4):
                mod_piece_dma()
                mod_piece()

        stats1 = stats(RA_x, between=p0_between)
        CP["saca_store"] = P.cnt["sS"]

        P.wait("sync", "sS", CP["saca_store"])
        r1 = P.op("sync", lambda e: e.dma_start(out=m96[0:32, :], in_=d_mod[0:32, :]),
                  inc="sX", n=16)
        P.wait("vector", "sX", r1)
        tv = None
        for pb in range(4):
            tv = P.op("vector",
                      lambda e, o=modt[32 * pb : 32 * pb + 32, 0:32],
                      i=m96[0:32, 32 * pb : 32 * pb + 32]: e.transpose(o, i),
                      inc="sV")
        P.wait("vector", "sV", tv)
        v_cap1 = P.op("vector", lambda e: e.tensor_scalar(
            par[:, 0:16], modt[:, 16:32], 1.0, None, ALU.add), inc="sV")


        def row_math():
            v1 = P.op("vector", lambda e: e.tensor_scalar(
                RB_sum[:, :], RB_sum[:, :], 1.0 / 2048.0, None, ALU.mult), inc="sV")
            v2 = P.op("vector", lambda e: e.tensor_scalar(
                RB_sq[:, :], RB_sq[:, :], 1.0 / 2048.0, EPS, ALU.mult, ALU.add),
                inc="sV")
            P.wait("vector", "sV", v1)
            v3 = P.op("vector", lambda e: e.tensor_tensor(
                RB_rstd[:, :], RB_sum[:, :], RB_sum[:, :], ALU.mult), inc="sV")
            P.wait("vector", "sV", max(v2, v3))
            v4 = P.op("vector", lambda e: e.tensor_tensor(
                RB_sq[:, :], RB_sq[:, :], RB_rstd[:, :], ALU.subtract), inc="sV")
            P.wait("scalar", "sV", v4)
            a1 = P.op("scalar", lambda e: e.activation(
                RB_rstd[:, :], RB_sq[:, :], AF.Ln), inc="sA")
            P.wait("scalar", "sA", a1)
            a2 = P.op("scalar", lambda e: e.activation(
                RB_rstd[:, :], RB_rstd[:, :], AF.Exp, scale=-0.5), inc="sA")
            P.wait("vector", "sA", a2)
            v5 = P.op("vector", lambda e: e.tensor_tensor(
                RB_mrstd[:, :], RB_sum[:, :], RB_rstd[:, :], ALU.mult), inc="sV")
            return v5

        P.wait("vector", "sV", stats1)
        rows1 = row_math()

        def bcast(src_row, dst_bc, ready_v):
            pe_wait("sV", ready_v)
            pv = None
            for n in range(NT):
                pv = P.op("tensor",
                          lambda e, r=src_row[0:1, n * 512 : n * 512 + 512],
                          o=bc_ps[:, n * 512 : n * 512 + 512]:
                          e.matmul(o, ones_row[0:1, :], r, start=True, stop=True),
                          inc="sPE")
            P.wait("vector", "sPE", pv)
            ev = P.op("vector", lambda e, d=dst_bc: e.tensor_copy(d[:, :], bc_ps[:, :]),
                      inc="sV")
            P.wait("tensor", "sV", ev)
            return ev

        bc1 = bcast(RB_rstd, rstd_bc, rows1)
        bc2 = bcast(RB_mrstd, mrstd_bc, rows1)

        def make_ln(srcf32, gp_ready=0):
            lv = []
            for k in range(KC):
                eng = "vector" if k % 2 == 0 else "gpsimd"
                sem = "sV" if k % 2 == 0 else "sG"
                ht = RB_htmp[:, (k % 2) * 1024 : (k % 2) * 1024 + 1024]
                xin = srcf32[:, k * 1024 : k * 1024 + 1024]
                if eng == "gpsimd" and gp_ready:
                    P.wait("gpsimd", *gp_ready)
                    gp_ready = 0
                t1 = P.op(eng, lambda e, o=ht, i=xin: e.tensor_tensor(
                    o, i, rstd_bc[:, :], ALU.mult), inc=sem)
                P.wait(eng, sem, t1)
                t2 = P.op(eng,
                          lambda e, o=RC_h[:, k * 1024 : k * 1024 + 1024], i=ht:
                          e.tensor_tensor(o, i, mrstd_bc[:, :], ALU.subtract),
                          inc=sem)
                lv.append((sem, t2))
            return lv

        def make_h(lv, shift_col, scale_col):
            hv = []
            for k in range(KC):
                P.wait("scalar", lv[k][0], lv[k][1])
                t3 = P.op("scalar",
                          lambda e, o=RC_h[:, k * 1024 : k * 1024 + 1024],
                          s1=scale_col(k), s2=shift_col(k):
                          e.activation(o, o, AF.Identity, bias=s2, scale=s1),
                          inc="sA")
                hv.append(t3)
            return hv

        P.wait("vector", "sV", max(bc1, bc2))
        lv1 = make_ln(RA_x, gp_ready=("sV", max(bc1, bc2)))
        P.wait("scalar", "sV", v_cap1)
        hv1 = make_h(lv1, lambda k: col(modt, k), lambda k: col(par, k))
        CP["h1_done"] = P.cnt["sA"]

        def h_rhs(k, n):
            return RC_h[:, k * 1024 + n * 512 : k * 1024 + n * 512 + 512]

        first_a1x = [True]

        def evac_a1x(m, n, ps, mmv):
            P.wait("scalar", "sPE", mmv)
            if first_a1x[0]:
                P.wait("scalar", "sA", CP["h1_done"])
                first_a1x[0] = False
            o = RA_xh[:, m * 1024 + n * 512 : m * 1024 + n * 512 + 512]
            av = P.op("scalar", lambda e, oo=o, pp=ps: e.copy(oo, pp), inc="sA")
            return ("sA", av)

        def a1_hook(m):
            if m % 2 == 0:
                mod_piece_dma()
                if mod_state["idx"] < mod_state["dma_idx"] - 1 or \
                   mod_state["dma_idx"] == len(PIECES):
                    if mod_state["idx"] < len(PIECES):
                        mod_piece()

        stage(d_win, C, 32, 0, h_rhs, lambda k, n: ("sA", hv1[k]), evac_a1x,
              hook=a1_hook)
        CP["a1x_evac"] = P.cnt["sA"]

        def evac_a1z(m, n, ps, mmv):
            P.wait("scalar", "sPE", mmv)
            o = RB_sz[:, m * 1024 + n * 512 : m * 1024 + n * 512 + 512]
            av = P.op("scalar", lambda e, oo=o, pp=ps: e.activation(oo, pp, AF.Silu),
                      inc="sA")
            return ("sA", av)

        stage(d_win, C, 32, 32, h_rhs, lambda k, n: ("sA", hv1[k]), evac_a1z,
              hook=a1_hook)
        CP["a1z_evac"] = P.cnt["sA"]


        dbg_gate = {}

        def dump(dram, sb_ap, gates):
            for sem, cnt in gates:
                P.wait("gpsimd", sem, cnt)
            return P.op("gpsimd", lambda e, d=dram, s=sb_ap: e.dma_start(out=d, in_=s),
                        inc="sS", n=16)

        if debug:
            dump(d_dh, RC_h[:, :], [("sV", CP["h1_done"])])
            dump(d_dxh, RA_xh[:, :], [("sA", CP["a1x_evac"])])
            dbg_gate["sz"] = dump(d_dsz, RB_sz[:, :], [("sA", CP["a1z_evac"])])

        def xh_rhs(k, n):
            return RA_xh[:, k * 1024 + n * 512 : k * 1024 + n * 512 + 512]

        first_a2 = [True]

        def evac_a2(m, n, ps, mmv):
            P.wait("vector", "sPE", mmv)
            if first_a2[0]:
                P.wait("vector", "sA", CP["a1z_evac"])
                if debug:
                    P.wait("vector", "sS", dbg_gate["sz"])
                first_a2[0] = False
            o = RB_sz[:, m * 1024 + n * 512 : m * 1024 + n * 512 + 512]
            vv = P.op("vector",
                      lambda e, oo=o, pp=ps, dc=col(Dt, m):
                      e.scalar_tensor_tensor(oo, pp, dc, oo, ALU.mult, ALU.mult),
                      inc="sV")
            return ("sV", vv)

        def a2_hook(m):
            if m % 2 == 0:
                mod_piece_dma()
                if mod_state["idx"] < len(PIECES):
                    mod_piece()

        stage(d_conv, INTER, KI, 0, xh_rhs, lambda k, n: ("sA", CP["a1x_evac"]),
              evac_a2, hook=a2_hook)
        CP["a2_evac"] = P.cnt["sV"]

        P.wait("sync", "sS", P.cnt["sS"])
        r2 = P.op("sync", lambda e: e.dma_start(out=m96[32:96, :], in_=d_mod[32:96, :]),
                  inc="sX", n=16)
        P.wait("vector", "sX", r2)
        tv2 = None
        for pb in range(4):
            for jb in (1, 2):
                tv2 = P.op("vector",
                           lambda e, o=modt[32 * pb : 32 * pb + 32,
                                            32 * jb : 32 * jb + 32],
                           i=m96[32 * jb : 32 * jb + 32, 32 * pb : 32 * pb + 32]:
                           e.transpose(o, i),
                           inc="sV")
        P.wait("vector", "sV", tv2)
        P.op("vector", lambda e: e.tensor_scalar(
            par[:, 32:48], modt[:, 64:80], 1.0, None, ALU.add), inc="sV")
        P.wait("scalar", "sV", tv2)
        sg1 = P.op("scalar", lambda e: e.activation(
            par[:, 64:80], modt[:, 32:48], AF.Sign), inc="sA")
        ab1 = P.op("scalar", lambda e: e.activation(
            par[:, 16:32], modt[:, 32:48], AF.Abs), inc="sA")
        P.wait("vector", "sA", ab1)
        am1 = P.op("vector", lambda e: e.tensor_scalar(
            par[:, 16:32], par[:, 16:32], 0.25, None, ALU.max), inc="sV")
        P.wait("vector", "sV", am1)
        ga_v = P.op("vector", lambda e: e.tensor_tensor(
            par[:, 16:32], par[:, 16:32], par[:, 64:80], ALU.mult), inc="sV")
        P.wait("scalar", "sV", ga_v)
        sg2 = P.op("scalar", lambda e: e.activation(
            par[:, 64:80], modt[:, 80:96], AF.Sign), inc="sA")
        ab2 = P.op("scalar", lambda e: e.activation(
            par[:, 48:64], modt[:, 80:96], AF.Abs), inc="sA")
        P.wait("vector", "sA", ab2)
        am2 = P.op("vector", lambda e: e.tensor_scalar(
            par[:, 48:64], par[:, 48:64], 0.25, 0.5, ALU.max, ALU.mult), inc="sV")
        P.wait("vector", "sV", am2)
        g2 = P.op("vector", lambda e: e.tensor_tensor(
            par[:, 48:64], par[:, 48:64], par[:, 64:80], ALU.mult), inc="sV")
        CP["params2"] = P.cnt["sV"]

        def yz_rhs(k, n):
            return RB_sz[:, k * 1024 + n * 512 : k * 1024 + n * 512 + 512]

        xjit_v = {}
        xjit_slot_free = {}

        def xjit_dma(m):
            slot = m % 2
            if slot in xjit_slot_free:
                P.wait("sync", "sV", xjit_slot_free[slot])
            P.wait("sync", "sPE", mod_state["last_mm"])
            dst = WM_jit[:, slot * 1024 : slot * 1024 + 1024]
            xjit_v[m] = P.op("sync",
                             lambda e, s=d_x[128 * m : 128 * m + 128, :], d=dst:
                             e.dma_start(out=d, in_=s),
                             inc="sX", n=16)

        xjit_dma(0)
        xjit_dma(1)

        st_rows = [row_ps[0:1, 0:512], mod_ps[0:1, 0:512],
                   bc_ps[0:1, 0:512], bc_ps[0:1, 512:1024]]
        st_cv = [WM_stcv[:, 0:512], WM_stcv[:, 512:1024]]
        st_state = {"pv": [0] * 4}

        def a3_stats(m, x2v):
            P.wait("scalar", "sV", x2v)
            if m >= 1:
                P.wait("scalar", "sPE", max(st_state["pv"]))
            cvs = []
            for n in range(NT):
                xin = RA_x2[:, m * 1024 + n * 512 : m * 1024 + n * 512 + 512]
                cvs.append(P.op("scalar", lambda e, o=st_cv[n], i=xin: e.copy(o, i),
                                inc="sA"))
            P.wait("tensor", "sA", cvs[-1])
            for n in range(NT):
                st_state["pv"][n] = P.op(
                    "tensor", lambda e, r=st_cv[n], o=st_rows[n],
                    st=(m == 0), sp=(m == KC - 1):
                    e.matmul(o, ones_col[:, 0:1], r, start=st, stop=sp),
                    inc="sPE")
            P.wait("scalar", "sPE", st_state["pv"][1])
            for n in range(NT):
                xin = RA_x2[:, m * 1024 + n * 512 : m * 1024 + n * 512 + 512]
                cvs.append(P.op("scalar",
                                lambda e, o=st_cv[n], i=xin: e.activation(o, i, AF.Square),
                                inc="sA"))
            P.wait("tensor", "sA", cvs[-1])
            for n in range(NT):
                st_state["pv"][2 + n] = P.op(
                    "tensor", lambda e, r=st_cv[n], o=st_rows[2 + n],
                    st=(m == 0), sp=(m == KC - 1):
                    e.matmul(o, ones_col[:, 0:1], r, start=st, stop=sp),
                    inc="sPE")

        def a3_post(m):
            if m + 2 < KC:
                xjit_dma(m + 2)
            a3_stats(m, a3_x2v[m])

        a3_x2v = {}

        def evac_a3(m, n, ps, mmv):
            P.wait("vector", "sPE", mmv)
            P.wait("vector", "sX", xjit_v[m])
            P.wait("vector", "sV", CP["params2"])
            slot = m % 2
            xj = WM_jit[:, slot * 1024 + n * 512 : slot * 1024 + n * 512 + 512]
            o = RA_x2[:, m * 1024 + n * 512 : m * 1024 + n * 512 + 512]
            vv = P.op("vector",
                      lambda e, oo=o, pp=ps, gc=col(par, 16 + m), xx=xj:
                      e.scalar_tensor_tensor(oo, pp, gc, xx, ALU.mult, ALU.add),
                      inc="sV")
            if n == NT - 1:
                xjit_slot_free[slot] = vv
                a3_x2v[m] = vv
            return ("sV", vv)

        stage(d_wout, INTER, KC, 0, yz_rhs, lambda k, n: ("sV", CP["a2_evac"]),
              evac_a3, posthook=a3_post)
        CP["a3_evac"] = P.cnt["sV"]
        CP["a3_mm"] = P.cnt["sPE"]

        if debug:
            dbg_gate["yz"] = dump(d_dyz, RB_sz[:, :],
                                  [("sV", CP["a2_evac"]), ("sPE", CP["a3_mm"])])
            dump(d_dbc[:, 0:1024], rstd_bc[:, :], [("sV", CP["h1_done"])])
            dump(d_dbc[:, 1024:2048], mrstd_bc[:, :], [])

        if debug:
            P.wait("scalar", "sS", dbg_gate["yz"])
            P.wait("vector", "sS", dbg_gate["yz"])
        P.wait("vector", "sPE", max(st_state["pv"]))
        for n in range(NT):
            P.op("vector", lambda e, d=RB_sum[0:1, n * 512 : n * 512 + 512],
                 s=st_rows[n]: e.tensor_copy(d, s), inc="sV")
        for n in range(NT):
            P.op("vector", lambda e, d=RB_sq[0:1, n * 512 : n * 512 + 512],
                 s=st_rows[2 + n]: e.tensor_copy(d, s), inc="sV")
        stats2 = P.cnt["sV"]
        P.wait("tensor", "sV", stats2)
        rows2 = row_math()
        bc3 = bcast(RB_rstd, rstd_bc, rows2)
        bc4 = bcast(RB_mrstd, mrstd_bc, rows2)
        P.wait("vector", "sV", max(bc3, bc4))
        lv2 = make_ln(RA_x2, gp_ready=("sV", max(bc3, bc4)))
        P.wait("scalar", "sV", CP["params2"])
        hv2 = make_h(lv2, lambda k: col(modt, 48 + k), lambda k: col(par, 32 + k))
        CP["h2_done"] = P.cnt["sA"]

        P.wait("sync", "sA", CP["h2_done"])
        spill_v = None
        for gq in range(4):
            src = RA_x2[:, 4096 * gq : 4096 * gq + 4096].rearrange(
                "p (kc t) -> p kc t", t=T)
            dst = d_x2[512 * gq : 512 * gq + 512, :].rearrange(
                "(kc p) t -> p kc t", p=128)
            spill_v = P.op("sync", lambda e, s=src, d=dst: e.dma_start(out=d, in_=s),
                           inc="sS", n=16)
        CP["spill"] = spill_v

        gelu_free = {}
        first_b1 = [True]
        a_ready = {}

        def evac_b1(m, n, ps, mmv):
            t_slot = (m * NT + n) % 2
            X = WM_g[:, t_slot * 512 : t_slot * 512 + 512]
            U = WM_g[:, 1024 + t_slot * 512 : 1024 + t_slot * 512 + 512]
            P.wait("scalar", "sPE", mmv)
            if first_b1[0]:
                P.wait("scalar", "sA", CP["h2_done"])
                P.wait("scalar", "sV", CP["a3_evac"])
                P.wait("vector", "sS", CP["spill"])
                first_b1[0] = False
            if t_slot in gelu_free:
                P.wait("scalar", "sV", gelu_free[t_slot])
            e1 = P.op("scalar", lambda e, o=X, pp=ps: e.activation(
                o, pp, AF.Identity, bias=magic[:, :], scale=10.0), inc="sA")
            P.wait("vector", "sA", e1)
            v1 = P.op("vector", lambda e, o=X: e.tensor_scalar(
                o, o, MAGIC, QMAX, ALU.subtract, ALU.min), inc="sV")
            P.wait("vector", "sV", v1)
            v2 = P.op("vector", lambda e, o=X: e.tensor_scalar(
                o, o, QMIN, 0.1, ALU.max, ALU.mult), inc="sV")
            P.wait("scalar", "sV", v2)
            s2 = P.op("scalar", lambda e, o=U, i=X: e.activation(o, i, AF.Square),
                      inc="sA")
            P.wait("vector", "sA", s2)
            v3 = P.op("vector", lambda e, o=U, i=X: e.tensor_tensor(
                o, o, i, ALU.mult), inc="sV")
            P.wait("vector", "sV", v3)
            v4 = P.op("vector", lambda e, o=U, i=X: e.scalar_tensor_tensor(
                o, o, GC, i, ALU.mult, ALU.add), inc="sV")
            P.wait("scalar", "sV", v4)
            s3 = P.op("scalar", lambda e, o=U: e.activation(o, o, AF.Tanh, scale=GK),
                      inc="sA")
            P.wait("vector", "sA", s3)
            if m < 32:
                ao = RA_alo[:, m * 1024 + n * 512 : m * 1024 + n * 512 + 512]
            else:
                ao = RB_sz[:, (m - 32) * 1024 + n * 512 : (m - 32) * 1024 + n * 512 + 512]
            v5 = P.op("vector", lambda e, o=ao, u=U, i=X: e.scalar_tensor_tensor(
                o, u, 1.0, i, ALU.add, ALU.mult), inc="sV")
            gelu_free[t_slot] = v5
            a_ready[(m, n)] = v5
            return ("sA", e1)

        P.wait("scalar", "sV", v_const)
        stage(d_w1, C, KF, 0, h_rhs, lambda k, n: ("sA", hv2[k]), evac_b1)
        CP["b1_evac"] = P.cnt["sV"]

        def a_rhs(k, n):
            if k < 32:
                return RA_alo[:, k * 1024 + n * 512 : k * 1024 + n * 512 + 512]
            return RB_sz[:, (k - 32) * 1024 + n * 512 : (k - 32) * 1024 + n * 512 + 512]

        x2jit_v = {}
        x2jit_slot_free = {}
        first_x2jit = [True]

        def x2jit_dma(m):
            slot = m % 2
            if first_x2jit[0]:
                P.wait("sync", "sS", CP["spill"])
                first_x2jit[0] = False
            if slot in x2jit_slot_free:
                P.wait("sync", "sV", x2jit_slot_free[slot])
            dst = WM_jit[:, slot * 1024 : slot * 1024 + 1024]
            x2jit_v[m] = P.op("sync",
                              lambda e, s=d_x2[128 * m : 128 * m + 128, :], d=dst:
                              e.dma_start(out=d, in_=s),
                              inc="sX", n=16)

        x2jit_dma(0)
        x2jit_dma(1)

        def b2_post(m):
            if m + 2 < KC:
                x2jit_dma(m + 2)

        out_free = {}

        def evac_b2(m, n, ps, mmv):
            slot = m % 2
            P.wait("vector", "sPE", mmv)
            P.wait("vector", "sX", x2jit_v[m])
            if slot in out_free:
                P.wait("vector", "sS", out_free[slot])
            xj = WM_jit[:, slot * 1024 + n * 512 : slot * 1024 + n * 512 + 512]
            o = WM_out[:, slot * 1024 + n * 512 : slot * 1024 + n * 512 + 512]
            vv = P.op("vector",
                      lambda e, oo=o, pp=ps, gc=col(par, 48 + m), xx=xj:
                      e.scalar_tensor_tensor(oo, pp, gc, xx, ALU.mult, ALU.add),
                      inc="sV")
            if n == NT - 1:
                x2jit_slot_free[slot] = vv
                P.wait("sync", "sV", vv)
                sv = P.op("sync",
                          lambda e, s=WM_out[:, slot * 1024 : slot * 1024 + 1024],
                          d=d_out[128 * m : 128 * m + 128, :]:
                          e.dma_start(out=d, in_=s),
                          inc="sS", n=16)
                out_free[slot] = sv
            return ("sV", vv)

        stage(d_w2, FF, KC, 0, a_rhs, lambda k, n: ("sV", a_ready[(k, n)]),
              evac_b2, posthook=b2_post, npieces=2)

        if debug:
            dump(d_dh2, RC_h[:, :], [("sV", CP["h2_done"])])
            dump(d_da[:, 0:32768], RA_alo[:, :], [("sV", CP["b1_evac"])])
            dump(d_da[:, 32768:65536], RB_sz[:, :], [])
            dump(d_dmodt, modt[:, :], [])
            dump(d_dpar, par[:, :], [])

        P.wait("sync", "sS", P.cnt["sS"])

        for engname in ENGS:
            fns = P.items[engname]
            getattr(block, engname)(lambda eng, fns=fns: [f(eng, SEM) for f in fns])

    return nc


_NC_CACHE = None


def _get_nc():
    global _NC_CACHE
    if _NC_CACHE is None:
        _NC_CACHE = build_program()
    return _NC_CACHE


def kernel(
    L0_attn_hidden_states,
    L0_attn_W_in,
    L0_attn_conv_w,
    L0_attn_W_x,
    L0_attn_W_dt,
    L0_attn_b_dt,
    L0_attn_D,
    L0_attn_W_out,
    L0_ffn_W1,
    L0_ffn_W2,
    L0_c,
    L0_wmod,
):
    f = lambda a: np.ascontiguousarray(np.asarray(a, dtype=np.float32))

    def tm(w):
        wT = np.asarray(w, dtype=np.float32).T
        K, M = wT.shape
        return np.ascontiguousarray(
            wT.reshape(K // 128, 128, M // 128, 128).transpose(2, 1, 0, 3)
            .reshape(M // 128, 128, K))

    xT = f(np.asarray(L0_attn_hidden_states, dtype=np.float32).reshape(8192, 2048).T)
    winT = tm(L0_attn_W_in)
    convT = tm(L0_attn_conv_w)
    woutT = tm(L0_attn_W_out)
    w1T = tm(L0_ffn_W1)
    w2T = tm(L0_ffn_W2)
    wmod = f(L0_wmod)
    PIECES_H = ([(256 * i, 256) for i in range(16)]
                + [(6144 + 256 * i, 256) for i in range(16)])
    PIECESF_H = ([(4096 + 128 * i, 128) for i in range(16)]
                 + [(10240 + 128 * i, 128) for i in range(16)])
    wmodB = np.stack([
        np.ascontiguousarray(
            wmod[:, j0:j0 + w].reshape(16, 128, w).transpose(1, 0, 2).reshape(128, 16 * w))
        for j0, w in PIECES_H])
    wmodF = np.stack([
        np.ascontiguousarray(
            wmod[:, j0:j0 + w].reshape(16, 128, w).transpose(1, 0, 2).reshape(128, 16 * w))
        for j0, w in PIECESF_H])
    cT = f(np.asarray(L0_c, dtype=np.float32).reshape(16, 128).T)
    dT = f(np.asarray(L0_attn_D, dtype=np.float32).reshape(32, 128).T)

    nc = _get_nc()
    in_maps = []
    for i in range(NCORES):
        in_maps.append({
            "xT": np.ascontiguousarray(xT[:, T * i : T * (i + 1)]),
            "winTM": winT,
            "convTM": convT,
            "woutTM": woutT,
            "w1TM": w1T,
            "w2TM": w2T,
            "wmodB": wmodB,
            "wmodF": wmodF,
            "cT": cT,
            "dT": dT,
        })
    res = run_bass_kernel_spmd(nc, in_maps, core_ids=list(range(NCORES)))
    out = np.concatenate([r["outT"] for r in res.results], axis=1)
    return np.ascontiguousarray(out.T).reshape(1, 8192, 2048)


# version 08
# speedup vs baseline: 1.0742x; 1.0742x over previous
from collections import deque

import numpy as np

import concourse.bass as bass
import concourse.mybir as mybir
from concourse.bass_utils import run_bass_kernel_spmd

F32 = mybir.dt.float32
BF16 = mybir.dt.bfloat16
U8 = mybir.dt.uint8
AF = mybir.ActivationFunctionType
ALU = mybir.AluOpType

NCORES = 8
C = 2048
INTER = 4096
FF = 8192
T = 1024
KC = C // 128
KI = INTER // 128
KF = FF // 128
NT = 2
MAGIC = 12582912.0
GK = 0.7978845608028654
GC = 0.044715
QMAX = 32767.0
QMIN = -32768.0
EPS = 1e-5
JMOD = 12288
NBLK = 24

ENGS = ("sync", "gpsimd", "tensor", "vector", "scalar")
SEMS = ("sW", "sX", "sS", "sPE", "sV", "sA", "sC")


class Sched:
    def __init__(self):
        self.items = {e: [] for e in ENGS}
        self.cnt = {s: 0 for s in SEMS}
        self.last_wait = {e: {s: 0 for s in SEMS} for e in ENGS}

    def wait(self, eng, sem, val):
        assert 0 <= val <= self.cnt[sem], (eng, sem, val, self.cnt[sem])
        if val <= self.last_wait[eng][sem]:
            return
        self.last_wait[eng][sem] = val
        self.items[eng].append(lambda e, S, sem=sem, val=val: e.wait_ge(S[sem], val))

    def op(self, eng, build, inc=None, n=1):
        if inc is None:
            self.items[eng].append(lambda e, S, b=build: b(e))
            return None
        self.cnt[inc] += n
        v = self.cnt[inc]
        self.items[eng].append(lambda e, S, b=build, s=inc, m=n: b(e).then_inc(S[s], m))
        return v


def place(nc, base_addr, name, shape, dtype, offset):
    h = nc.virtual_sbuf_tensor(name, shape, dtype)
    nc._tensor(h.name, h.shape, h.dtype, type="SB")
    mloc = nc.lookup_mloc(h)
    mloc.allocated = True
    mloc.addr = base_addr + offset
    return h


def build_program(debug=False):
    nc = bass.Bass("TRN2", target_bir_lowering=False)

    d_x = nc.dram_tensor("xT", [C, T], F32, kind="ExternalInput").ap()
    d_win = nc.dram_tensor("winTM", [FF // 128, 128, C], F32, kind="ExternalInput").ap()
    d_conv = nc.dram_tensor("convTM", [KI, 128, INTER], F32, kind="ExternalInput").ap()
    d_wout = nc.dram_tensor("woutTM", [KC, 128, INTER], F32, kind="ExternalInput").ap()
    d_w1 = nc.dram_tensor("w1TM", [KF, 128, C], F32, kind="ExternalInput").ap()
    d_w2 = nc.dram_tensor("w2TM", [KC, 128, FF], F32, kind="ExternalInput").ap()
    d_wmodB = nc.dram_tensor("wmodB", [32, 128, 4096], F32, kind="ExternalInput").ap()
    d_wmodF = nc.dram_tensor("wmodF", [32, 128, 2048], F32, kind="ExternalInput").ap()
    d_c = nc.dram_tensor("cT", [128, 16], F32, kind="ExternalInput").ap()
    d_D = nc.dram_tensor("dT", [128, 32], F32, kind="ExternalInput").ap()
    d_out = nc.dram_tensor("outT", [C, T], F32, kind="ExternalOutput").ap()
    d_x2 = nc.dram_tensor("x2spill", [C, T], F32,
                          kind="ExternalOutput" if debug else "Internal").ap()
    if debug:
        d_dh = nc.dram_tensor("dbg_h", [128, 16384], F32, kind="ExternalOutput").ap()
        d_dxh = nc.dram_tensor("dbg_xh", [128, 32768], F32, kind="ExternalOutput").ap()
        d_dsz = nc.dram_tensor("dbg_sz", [128, 32768], F32, kind="ExternalOutput").ap()
        d_dyz = nc.dram_tensor("dbg_yz", [128, 32768], F32, kind="ExternalOutput").ap()
        d_dh2 = nc.dram_tensor("dbg_h2", [128, 16384], F32, kind="ExternalOutput").ap()
        d_da = nc.dram_tensor("dbg_a", [128, 65536], F32, kind="ExternalOutput").ap()
        d_dmodt = nc.dram_tensor("dbg_modt", [128, 96], F32, kind="ExternalOutput").ap()
        d_dpar = nc.dram_tensor("dbg_par", [128, 80], F32, kind="ExternalOutput").ap()
        d_dbc = nc.dram_tensor("dbg_bc", [128, 2048], F32, kind="ExternalOutput").ap()
    d_mod = nc.dram_tensor("modscr", [96, 128], F32, kind="Internal").ap()

    P = Sched()
    CP = {}

    OFF = {}
    cur = 0

    def A(name, nbytes):
        nonlocal cur
        OFF[name] = cur
        cur += (nbytes + 31) // 32 * 32

    A("wbuf", 16384)
    A("rowbuf", 2048)
    A("rowbuf2", 2048)
    A("ones_row", 512)
    A("ones_col", 32)
    A("silu_cb", 32)
    A("cd", 64)
    A("modt", 384)
    A("par", 320)
    A("magic", 32)
    A("dt", 128)
    A("m96", 512)
    A("RA", 65536)
    A("RB", 65536)
    A("RC", 32768)
    A("WM", 24576)
    assert cur <= 211000, cur

    with (
        nc.sbuf_tensor("arena", [128, cur], U8) as arena,
        nc.psum_tensor("mm_ps", [128, 2048], F32) as mm_ps,
        nc.psum_tensor("bc_ps", [128, 1024], F32) as bc_ps,
        nc.psum_tensor("row_ps", [1, 512], F32) as row_ps,
        nc.psum_tensor("mod_ps", [1, 512], F32) as mod_ps,
        nc.semaphore("sW") as _sW,
        nc.semaphore("sX") as _sX,
        nc.semaphore("sS") as _sS,
        nc.semaphore("sPE") as _sPE,
        nc.semaphore("sV") as _sV,
        nc.semaphore("sA") as _sA,
        nc.semaphore("sC") as _sC,
        nc.Block() as block,
    ):
        SEM = {"sW": _sW, "sX": _sX, "sS": _sS, "sPE": _sPE, "sV": _sV,
               "sA": _sA, "sC": _sC}
        base = nc.lookup_mloc(arena).addr

        wbuf = place(nc, base, "v_wbuf", [128, 8192], BF16, OFF["wbuf"])
        rstd_bc = place(nc, base, "v_rstd_bc", [128, 1024], F32, OFF["RB"] + 28672)
        mrstd_bc = place(nc, base, "v_mrstd_bc", [128, 1024], F32, OFF["RB"] + 32768)
        rowb = place(nc, base, "v_rowb", [1, 512], F32, OFF["rowbuf"])
        rowb2 = place(nc, base, "v_rowb2", [1, 512], F32, OFF["rowbuf2"])
        ones_row = place(nc, base, "v_ones_row", [1, 128], F32, OFF["ones_row"])
        ones_col = place(nc, base, "v_ones_col", [128, 16], BF16, OFF["ones_col"])
        silu_cb = place(nc, base, "v_silu_cb", [128, 16], BF16, OFF["silu_cb"])
        cd = place(nc, base, "v_cd", [128, 16], F32, OFF["cd"])
        modt = place(nc, base, "v_modt", [128, 96], F32, OFF["modt"])
        par = place(nc, base, "v_par", [128, 80], F32, OFF["par"])
        magic = place(nc, base, "v_magic", [128, 1], F32, OFF["magic"])
        Dt = place(nc, base, "v_dt", [128, 32], F32, OFF["dt"])
        m96 = place(nc, base, "v_m96", [96, 128], F32, OFF["m96"])
        RA_x = place(nc, base, "v_RA_x", [128, 16384], F32, OFF["RA"])
        RA_xh = place(nc, base, "v_RA_xh", [128, 32768], BF16, OFF["RA"])
        RA_x2 = place(nc, base, "v_RA_x2", [128, 16384], F32, OFF["RA"])
        RA_alo = place(nc, base, "v_RA_alo", [128, 32768], BF16, OFF["RA"])
        RB_sz = place(nc, base, "v_RB_sz", [128, 32768], BF16, OFF["RB"])
        RB_sum = place(nc, base, "v_RB_sum", [1, 1024], F32, OFF["RB"])
        RB_sq = place(nc, base, "v_RB_sq", [1, 1024], F32, OFF["RB"] + 4096)
        RB_rstd = place(nc, base, "v_RB_rstd", [1, 1024], F32, OFF["RB"] + 8192)
        RB_mrstd = place(nc, base, "v_RB_mrstd", [1, 1024], F32, OFF["RB"] + 12288)
        RB_htmp = place(nc, base, "v_RB_htmp", [128, 2048], F32, OFF["RB"] + 16384)
        RB_xbf = place(nc, base, "v_RB_xbf", [128, 8192], BF16, OFF["RB"] + 36864)
        RC_h = place(nc, base, "v_RC_h", [128, 16384], BF16, OFF["RC"])
        WM_b = [place(nc, base, "v_WM_b%d" % s, [128, 4096], BF16,
                      OFF["WM"] + 8192 * s) for s in range(3)]
        WM_f = [place(nc, base, "v_WM_f%d" % s, [128, 2048], F32,
                      OFF["WM"] + 8192 * s) for s in range(3)]
        WM_g = place(nc, base, "v_WM_g", [128, 2048], F32, OFF["WM"])
        WM_out = place(nc, base, "v_WM_out", [128, 2048], F32, OFF["WM"] + 8192)
        WM_jit = place(nc, base, "v_WM_jit", [128, 2048], F32, OFF["WM"] + 16384)

        def col(t, j):
            return t[:, j : j + 1]

        bank_free = [None] * 4
        unit_free = [None] * 4
        group_ctr = [0]
        unit_ctr = [0]

        def mm_bank_ap(b):
            return mm_ps[:, 512 * b : 512 * b + 512]

        def pe_wait(sem, cnt):
            P.wait("tensor", sem, cnt)

        def stage(wdram, Ktot, Mch, m_off, rhs_ap, rhs_ready, evac, hook=None,
                  posthook=None, npieces=1):
            Kch = Ktot // 128
            upg = Ktot // 2048
            assert upg in (1, 2, 4) and upg % npieces == 0
            upp = upg // npieces
            kpp = Kch // npieces
            ring_slots = 4 // upp
            total_pieces = Mch * npieces
            piece_info = []
            issued = [0]

            def issue_piece(p):
                u0 = unit_ctr[0] % 4
                for uu in range(u0, u0 + upp):
                    if unit_free[uu] is not None:
                        P.wait("gpsimd", *unit_free[uu])
                m = p // npieces
                pc = p % npieces
                src = wdram[m_off + m, :, pc * kpp * 128 : (pc + 1) * kpp * 128]
                dst = wbuf[:, u0 * 2048 : u0 * 2048 + kpp * 128]
                wv = P.op("gpsimd", lambda e, s=src, d=dst: e.dma_start(out=d, in_=s),
                          inc="sW", n=16)
                piece_info.append((u0, wv))
                unit_ctr[0] += upp

            def ensure(upto):
                while issued[0] < min(total_pieces, upto):
                    issue_piece(issued[0])
                    issued[0] += 1

            ensure(npieces + max(0, ring_slots - npieces))
            for m in range(Mch):
                if hook is not None:
                    hook(m)
                ensure((m + 1) * npieces + max(0, ring_slots - npieces))
                g = group_ctr[0]
                group_ctr[0] += 2
                b0, b1 = g % 4, (g + 1) % 4
                for b in (b0, b1):
                    if bank_free[b] is not None:
                        pe_wait(*bank_free[b])
                banks = (b0, b1)
                v_final = None
                for k in range(Kch):
                    pc = k // kpp
                    u0, wv = piece_info[m * npieces + pc]
                    pe_wait("sW", wv)
                    lhs = wbuf[:, u0 * 2048 + (k % kpp) * 128 :
                               u0 * 2048 + (k % kpp) * 128 + 128]
                    piece_end = (k + 1) % kpp == 0
                    for n in range(NT):
                        r = rhs_ready(k, n)
                        if r is not None:
                            pe_wait(*r)
                        is_inc = piece_end and n == NT - 1
                        v = P.op("tensor",
                                 lambda e, o=mm_bank_ap(banks[n]), l=lhs,
                                 r2=rhs_ap(k, n), st=(k == 0), sp=(k == Kch - 1):
                                 e.matmul(o, l, r2, start=st, stop=sp),
                                 inc="sPE" if is_inc else None)
                        if is_inc:
                            v_final = v
                    if piece_end:
                        for uu in range(u0, u0 + upp):
                            unit_free[uu] = ("sPE", v_final)
                assert b1 == b0 + 1
                f0 = evac(m, mm_ps[:, 512 * b0 : 512 * b0 + 1024], v_final)
                bank_free[b0] = f0
                bank_free[b1] = f0
                if posthook is not None:
                    posthook(m)

        P.op("vector", lambda e: e.memset(ones_col[:, :], 1.0))
        P.op("vector", lambda e: e.memset(ones_row[:, :], 1.0))
        v_const = P.op("vector", lambda e: e.memset(magic[:, :], MAGIC), inc="sV")

        x_c = P.op("sync", lambda e: e.dma_start(out=cd[:, :], in_=d_c), inc="sX", n=16)
        P.op("sync", lambda e: e.dma_start(out=Dt[:, :], in_=d_D), inc="sX", n=16)
        x_x = None
        for gq in range(4):
            src = d_x[512 * gq : 512 * gq + 512, :].rearrange("(kc p) t -> p kc t", p=128)
            dst = RA_x[:, 4096 * gq : 4096 * gq + 4096].rearrange("p (kc t) -> p kc t", t=T)
            x_x = P.op("sync", lambda e, s=src, d=dst: e.dma_start(out=d, in_=s),
                       inc="sX", n=16)

        P.wait("scalar", "sX", x_c)
        a_silu = P.op("scalar", lambda e: e.activation(cd[:, :], cd[:, :], AF.Silu),
                      inc="sA")
        P.wait("scalar", "sA", a_silu)
        a_siluc = P.op("scalar", lambda e: e.copy(silu_cb[:, :], cd[:, :]), inc="sA")

        def stats(srcf32):
            ev = None
            for n in range(NT):
                for sq in (0, 1):
                    av = None
                    for k in range(KC):
                        xin = srcf32[:, k * 1024 + n * 512 : k * 1024 + n * 512 + 512]
                        cv = RB_xbf[:, k * 512 : k * 512 + 512]
                        if sq == 0:
                            av = P.op("scalar", lambda e, o=cv, i=xin: e.copy(o, i),
                                      inc="sA" if k == KC - 1 else None)
                        else:
                            av = P.op("scalar",
                                      lambda e, o=cv, i=xin: e.activation(o, i, AF.Square),
                                      inc="sA" if k == KC - 1 else None)
                    P.wait("tensor", "sA", av)
                    pv = None
                    for k in range(KC):
                        cv = RB_xbf[:, k * 512 : k * 512 + 512]
                        pv = P.op("tensor",
                                  lambda e, r=cv, st=(k == 0), sp=(k == KC - 1):
                                  e.matmul(row_ps[:, :], ones_col[:, 0:1], r,
                                           start=st, stop=sp),
                                  inc="sPE" if k == KC - 1 else None)
                    P.wait("scalar", "sPE", pv)
                    P.wait("vector", "sPE", pv)
                    dst = (RB_sum if sq == 0 else RB_sq)[0:1, n * 512 : n * 512 + 512]
                    ev = P.op("vector", lambda e, d=dst: e.tensor_copy(d, row_ps[:, :]),
                              inc="sV")
                    P.wait("tensor", "sV", ev)
            return ev

        P.wait("scalar", "sX", x_x)
        stats1 = stats(RA_x)

        PIECES = []
        for i in range(16):
            PIECES.append((256 * i, 256, "b"))
        for i in range(16):
            PIECES.append((6144 + 256 * i, 256, "b"))
        for i in range(16):
            PIECES.append((4096 + 128 * i, 128, "f"))
        for i in range(16):
            PIECES.append((10240 + 128 * i, 128, "f"))

        PIECE_SUB = []
        nb = nf = 0
        for (_j0, _w, _k) in PIECES:
            if _k == "b":
                PIECE_SUB.append(nb); nb += 1
            else:
                PIECE_SUB.append(nf); nf += 1

        mod_state = {"mm": [0, 0, 0], "store": [0, 0], "idx": 0, "last_mm": 0,
                     "dma_idx": 0, "dma_v": {}}

        def mod_piece_dma():
            i = mod_state["dma_idx"]
            if i >= len(PIECES):
                return
            mod_state["dma_idx"] += 1
            j0, wdt, kind = PIECES[i]
            s = i % 3
            if mod_state["mm"][s]:
                P.wait("gpsimd", "sPE", mod_state["mm"][s])
            if kind == "b":
                srcw = d_wmodB[PIECE_SUB[i], :, :]
                slot = WM_b[s][:, :]
            else:
                srcw = d_wmodF[PIECE_SUB[i], :, :]
                slot = WM_f[s][:, :]
            mod_state["dma_v"][i] = P.op(
                "gpsimd", lambda e, s_=srcw, d_=slot: e.dma_start(out=d_, in_=s_),
                inc="sW", n=16)

        def mod_piece():
            if mod_state["dma_idx"] <= mod_state["idx"]:
                mod_piece_dma()
            i = mod_state["idx"]
            mod_state["idx"] += 1
            j0, wdt, kind = PIECES[i]
            s = i % 3
            wv = mod_state["dma_v"][i]
            pe_wait("sW", wv)
            pe_wait("sA", a_siluc)
            pv = None
            for kc in range(KC):
                if kind == "b":
                    lhs = silu_cb[:, kc : kc + 1]
                    rhs = WM_b[s][:, kc * wdt : kc * wdt + wdt]
                else:
                    lhs = cd[:, kc : kc + 1]
                    rhs = WM_f[s][:, kc * wdt : kc * wdt + wdt]
                pv = P.op("tensor",
                          lambda e, l=lhs, r=rhs, st=(kc == 0), sp=(kc == KC - 1),
                          w_=wdt: e.matmul(mod_ps[0:1, 0:w_], l, r, start=st, stop=sp),
                          inc="sPE" if kc == KC - 1 else None)
            mod_state["mm"][s] = pv
            mod_state["last_mm"] = pv
            rs = i % 2
            rb = rowb if rs == 0 else rowb2
            if mod_state["store"][rs]:
                P.wait("vector", "sS", mod_state["store"][rs])
            P.wait("vector", "sPE", pv)
            ev = P.op("vector", lambda e, r=rb, w_=wdt: e.tensor_copy(
                r[0:1, 0:w_], mod_ps[0:1, 0:w_]), inc="sV")
            P.wait("tensor", "sV", ev)
            P.wait("sync", "sV", ev)
            J0 = j0 // 128
            nrow = wdt // 128
            sv = P.op("sync",
                      lambda e, r=rb[0:1, 0:wdt].rearrange("p (a c) -> p a c", c=128),
                      d=d_mod[J0 : J0 + nrow, :].rearrange("a c -> () a c"):
                      e.dma_start(out=d, in_=r),
                      inc="sS", n=16)
            mod_state["store"][rs] = sv

        mod_piece_dma()
        mod_piece_dma()
        for _ in range(16):
            mod_piece_dma()
            mod_piece()
        CP["saca_store"] = P.cnt["sS"]

        P.wait("sync", "sS", CP["saca_store"])
        r1 = P.op("sync", lambda e: e.dma_start(out=m96[0:32, :], in_=d_mod[0:32, :]),
                  inc="sX", n=16)
        P.wait("vector", "sX", r1)
        tv = None
        for pb in range(4):
            tv = P.op("vector",
                      lambda e, o=modt[32 * pb : 32 * pb + 32, 0:32],
                      i=m96[0:32, 32 * pb : 32 * pb + 32]: e.transpose(o, i),
                      inc="sV")
        P.wait("vector", "sV", tv)
        v_cap1 = P.op("vector", lambda e: e.tensor_scalar(
            par[:, 0:16], modt[:, 16:32], 1.0, None, ALU.add), inc="sV")


        def row_math():
            v1 = P.op("vector", lambda e: e.tensor_scalar(
                RB_sum[:, :], RB_sum[:, :], 1.0 / 2048.0, None, ALU.mult), inc="sV")
            v2 = P.op("vector", lambda e: e.tensor_scalar(
                RB_sq[:, :], RB_sq[:, :], 1.0 / 2048.0, EPS, ALU.mult, ALU.add),
                inc="sV")
            P.wait("vector", "sV", v1)
            v3 = P.op("vector", lambda e: e.tensor_tensor(
                RB_rstd[:, :], RB_sum[:, :], RB_sum[:, :], ALU.mult), inc="sV")
            P.wait("vector", "sV", max(v2, v3))
            v4 = P.op("vector", lambda e: e.tensor_tensor(
                RB_sq[:, :], RB_sq[:, :], RB_rstd[:, :], ALU.subtract), inc="sV")
            P.wait("scalar", "sV", v4)
            a1 = P.op("scalar", lambda e: e.activation(
                RB_rstd[:, :], RB_sq[:, :], AF.Ln), inc="sA")
            P.wait("scalar", "sA", a1)
            a2 = P.op("scalar", lambda e: e.activation(
                RB_rstd[:, :], RB_rstd[:, :], AF.Exp, scale=-0.5), inc="sA")
            P.wait("vector", "sA", a2)
            v5 = P.op("vector", lambda e: e.tensor_tensor(
                RB_mrstd[:, :], RB_sum[:, :], RB_rstd[:, :], ALU.mult), inc="sV")
            return v5

        P.wait("vector", "sV", stats1)
        rows1 = row_math()

        def bcast(src_row, dst_bc, ready_v):
            pe_wait("sV", ready_v)
            pv = None
            for n in range(NT):
                pv = P.op("tensor",
                          lambda e, r=src_row[0:1, n * 512 : n * 512 + 512],
                          o=bc_ps[:, n * 512 : n * 512 + 512]:
                          e.matmul(o, ones_row[0:1, :], r, start=True, stop=True),
                          inc="sPE")
            P.wait("vector", "sPE", pv)
            ev = P.op("vector", lambda e, d=dst_bc: e.tensor_copy(d[:, :], bc_ps[:, :]),
                      inc="sV")
            P.wait("tensor", "sV", ev)
            return ev

        bc1 = bcast(RB_rstd, rstd_bc, rows1)
        bc2 = bcast(RB_mrstd, mrstd_bc, rows1)

        def make_ln(srcf32):
            lv = []
            for k in range(KC):
                ht = RB_htmp[:, (k % 2) * 1024 : (k % 2) * 1024 + 1024]
                xin = srcf32[:, k * 1024 : k * 1024 + 1024]
                t1 = P.op("vector", lambda e, o=ht, i=xin: e.tensor_tensor(
                    o, i, rstd_bc[:, :], ALU.mult), inc="sV")
                P.wait("vector", "sV", t1)
                t2 = P.op("vector",
                          lambda e, o=RC_h[:, k * 1024 : k * 1024 + 1024], i=ht:
                          e.tensor_tensor(o, i, mrstd_bc[:, :], ALU.subtract),
                          inc="sV")
                lv.append(t2)
            return lv

        def make_h(lv, shift_col, scale_col):
            hv = []
            for k in range(KC):
                P.wait("scalar", "sV", lv[k])
                t3 = P.op("scalar",
                          lambda e, o=RC_h[:, k * 1024 : k * 1024 + 1024],
                          s1=scale_col(k), s2=shift_col(k):
                          e.activation(o, o, AF.Identity, bias=s2, scale=s1),
                          inc="sA")
                hv.append(t3)
            return hv

        P.wait("vector", "sV", max(bc1, bc2))
        lv1 = make_ln(RA_x)
        P.wait("scalar", "sV", v_cap1)
        hv1 = make_h(lv1, lambda k: col(modt, k), lambda k: col(par, k))
        CP["h1_done"] = P.cnt["sA"]

        def h_rhs(k, n):
            return RC_h[:, k * 1024 + n * 512 : k * 1024 + n * 512 + 512]

        first_a1x = [True]

        def evac_a1x(m, ps, mmv):
            P.wait("scalar", "sPE", mmv)
            if first_a1x[0]:
                P.wait("scalar", "sA", CP["h1_done"])
                first_a1x[0] = False
            o = RA_xh[:, m * 1024 : m * 1024 + 1024]
            av = P.op("scalar", lambda e, oo=o, pp=ps: e.copy(oo, pp), inc="sA")
            return ("sA", av)

        def a1_hook(m):
            if m % 2 == 0:
                mod_piece_dma()
                if mod_state["idx"] < mod_state["dma_idx"] - 1 or \
                   mod_state["dma_idx"] == len(PIECES):
                    if mod_state["idx"] < len(PIECES):
                        mod_piece()

        stage(d_win, C, 32, 0, h_rhs, lambda k, n: ("sA", hv1[k]), evac_a1x,
              hook=a1_hook)
        CP["a1x_evac"] = P.cnt["sA"]

        def evac_a1z(m, ps, mmv):
            P.wait("scalar", "sPE", mmv)
            o = RB_sz[:, m * 1024 : m * 1024 + 1024]
            av = P.op("scalar", lambda e, oo=o, pp=ps: e.activation(oo, pp, AF.Silu),
                      inc="sA")
            return ("sA", av)

        stage(d_win, C, 32, 32, h_rhs, lambda k, n: ("sA", hv1[k]), evac_a1z,
              hook=a1_hook)
        CP["a1z_evac"] = P.cnt["sA"]


        dbg_gate = {}

        def dump(dram, sb_ap, gates):
            for sem, cnt in gates:
                P.wait("gpsimd", sem, cnt)
            return P.op("gpsimd", lambda e, d=dram, s=sb_ap: e.dma_start(out=d, in_=s),
                        inc="sS", n=16)

        if debug:
            dump(d_dh, RC_h[:, :], [("sV", CP["h1_done"])])
            dump(d_dxh, RA_xh[:, :], [("sA", CP["a1x_evac"])])
            dbg_gate["sz"] = dump(d_dsz, RB_sz[:, :], [("sA", CP["a1z_evac"])])

        def xh_rhs(k, n):
            return RA_xh[:, k * 1024 + n * 512 : k * 1024 + n * 512 + 512]

        first_a2 = [True]

        def evac_a2(m, ps, mmv):
            P.wait("vector", "sPE", mmv)
            if first_a2[0]:
                P.wait("vector", "sA", CP["a1z_evac"])
                if debug:
                    P.wait("vector", "sS", dbg_gate["sz"])
                first_a2[0] = False
            o = RB_sz[:, m * 1024 : m * 1024 + 1024]
            vv = P.op("vector",
                      lambda e, oo=o, pp=ps, dc=col(Dt, m):
                      e.scalar_tensor_tensor(oo, pp, dc, oo, ALU.mult, ALU.mult),
                      inc="sV")
            return ("sV", vv)

        def a2_hook(m):
            if m % 2 == 0:
                mod_piece_dma()
                if mod_state["idx"] < len(PIECES):
                    mod_piece()

        stage(d_conv, INTER, KI, 0, xh_rhs, lambda k, n: ("sA", CP["a1x_evac"]),
              evac_a2, hook=a2_hook)
        CP["a2_evac"] = P.cnt["sV"]

        P.wait("sync", "sS", P.cnt["sS"])
        r2 = P.op("sync", lambda e: e.dma_start(out=m96[32:96, :], in_=d_mod[32:96, :]),
                  inc="sX", n=16)
        P.wait("vector", "sX", r2)
        tv2 = None
        for pb in range(4):
            for jb in (1, 2):
                tv2 = P.op("vector",
                           lambda e, o=modt[32 * pb : 32 * pb + 32,
                                            32 * jb : 32 * jb + 32],
                           i=m96[32 * jb : 32 * jb + 32, 32 * pb : 32 * pb + 32]:
                           e.transpose(o, i),
                           inc="sV")
        P.wait("vector", "sV", tv2)
        P.op("vector", lambda e: e.tensor_scalar(
            par[:, 32:48], modt[:, 64:80], 1.0, None, ALU.add), inc="sV")
        P.wait("scalar", "sV", tv2)
        sg1 = P.op("scalar", lambda e: e.activation(
            par[:, 64:80], modt[:, 32:48], AF.Sign), inc="sA")
        ab1 = P.op("scalar", lambda e: e.activation(
            par[:, 16:32], modt[:, 32:48], AF.Abs), inc="sA")
        P.wait("vector", "sA", ab1)
        am1 = P.op("vector", lambda e: e.tensor_scalar(
            par[:, 16:32], par[:, 16:32], 0.25, None, ALU.max), inc="sV")
        P.wait("vector", "sV", am1)
        ga_v = P.op("vector", lambda e: e.tensor_tensor(
            par[:, 16:32], par[:, 16:32], par[:, 64:80], ALU.mult), inc="sV")
        P.wait("scalar", "sV", ga_v)
        sg2 = P.op("scalar", lambda e: e.activation(
            par[:, 64:80], modt[:, 80:96], AF.Sign), inc="sA")
        ab2 = P.op("scalar", lambda e: e.activation(
            par[:, 48:64], modt[:, 80:96], AF.Abs), inc="sA")
        P.wait("vector", "sA", ab2)
        am2 = P.op("vector", lambda e: e.tensor_scalar(
            par[:, 48:64], par[:, 48:64], 0.25, 0.5, ALU.max, ALU.mult), inc="sV")
        P.wait("vector", "sV", am2)
        g2 = P.op("vector", lambda e: e.tensor_tensor(
            par[:, 48:64], par[:, 48:64], par[:, 64:80], ALU.mult), inc="sV")
        CP["params2"] = P.cnt["sV"]

        def yz_rhs(k, n):
            return RB_sz[:, k * 1024 + n * 512 : k * 1024 + n * 512 + 512]

        xjit_v = {}
        xjit_slot_free = {}

        def xjit_dma(m):
            slot = m % 2
            if slot in xjit_slot_free:
                P.wait("sync", "sV", xjit_slot_free[slot])
            P.wait("sync", "sPE", mod_state["last_mm"])
            dst = WM_jit[:, slot * 1024 : slot * 1024 + 1024]
            xjit_v[m] = P.op("sync",
                             lambda e, s=d_x[128 * m : 128 * m + 128, :], d=dst:
                             e.dma_start(out=d, in_=s),
                             inc="sX", n=16)

        xjit_dma(0)
        xjit_dma(1)

        def a3_post(m):
            if m + 2 < KC:
                xjit_dma(m + 2)

        def evac_a3(m, ps, mmv):
            P.wait("vector", "sPE", mmv)
            P.wait("vector", "sX", xjit_v[m])
            P.wait("vector", "sV", CP["params2"])
            slot = m % 2
            xj = WM_jit[:, slot * 1024 : slot * 1024 + 1024]
            o = RA_x2[:, m * 1024 : m * 1024 + 1024]
            vv = P.op("vector",
                      lambda e, oo=o, pp=ps, gc=col(par, 16 + m), xx=xj:
                      e.scalar_tensor_tensor(oo, pp, gc, xx, ALU.mult, ALU.add),
                      inc="sV")
            xjit_slot_free[slot] = vv
            return ("sV", vv)

        stage(d_wout, INTER, KC, 0, yz_rhs, lambda k, n: ("sV", CP["a2_evac"]),
              evac_a3, posthook=a3_post)
        CP["a3_evac"] = P.cnt["sV"]
        CP["a3_mm"] = P.cnt["sPE"]

        if debug:
            dbg_gate["yz"] = dump(d_dyz, RB_sz[:, :],
                                  [("sV", CP["a2_evac"]), ("sPE", CP["a3_mm"])])
            dump(d_dbc[:, 0:1024], rstd_bc[:, :], [("sV", CP["h1_done"])])
            dump(d_dbc[:, 1024:2048], mrstd_bc[:, :], [])

        P.wait("scalar", "sPE", CP["a3_mm"])
        P.wait("scalar", "sV", CP["a3_evac"])
        if debug:
            P.wait("scalar", "sS", dbg_gate["yz"])
            P.wait("vector", "sS", dbg_gate["yz"])
        stats2 = stats(RA_x2)
        P.wait("vector", "sV", stats2)
        rows2 = row_math()
        bc3 = bcast(RB_rstd, rstd_bc, rows2)
        bc4 = bcast(RB_mrstd, mrstd_bc, rows2)
        P.wait("vector", "sV", max(bc3, bc4))
        lv2 = make_ln(RA_x2)
        P.wait("scalar", "sV", CP["params2"])
        hv2 = make_h(lv2, lambda k: col(modt, 48 + k), lambda k: col(par, 32 + k))
        CP["h2_done"] = P.cnt["sA"]

        P.wait("sync", "sA", CP["h2_done"])
        spill_v = None
        for gq in range(4):
            src = RA_x2[:, 4096 * gq : 4096 * gq + 4096].rearrange(
                "p (kc t) -> p kc t", t=T)
            dst = d_x2[512 * gq : 512 * gq + 512, :].rearrange(
                "(kc p) t -> p kc t", p=128)
            spill_v = P.op("sync", lambda e, s=src, d=dst: e.dma_start(out=d, in_=s),
                           inc="sS", n=16)
        CP["spill"] = spill_v

        gelu_free = {}
        first_b1 = [True]
        a_ready = {}
        GX = [WM_g[:, 0:1024], WM_jit[:, 0:1024]]
        GU = [WM_g[:, 1024:2048], WM_jit[:, 1024:2048]]

        def evac_b1(m, ps, mmv):
            t_slot = m % 2
            X = GX[t_slot]
            U = GU[t_slot]
            P.wait("scalar", "sPE", mmv)
            if first_b1[0]:
                P.wait("scalar", "sA", CP["h2_done"])
                P.wait("scalar", "sV", CP["a3_evac"])
                P.wait("vector", "sS", CP["spill"])
                first_b1[0] = False
            if t_slot in gelu_free:
                P.wait("scalar", "sV", gelu_free[t_slot])
            e1 = P.op("scalar", lambda e, o=X, pp=ps: e.activation(
                o, pp, AF.Identity, bias=magic[:, :], scale=10.0), inc="sA")
            P.wait("vector", "sA", e1)
            v1 = P.op("vector", lambda e, o=X: e.tensor_scalar(
                o, o, MAGIC, QMAX, ALU.subtract, ALU.min), inc="sV")
            P.wait("vector", "sV", v1)
            v2 = P.op("vector", lambda e, o=X: e.tensor_scalar(
                o, o, QMIN, 0.1, ALU.max, ALU.mult), inc="sV")
            P.wait("scalar", "sV", v2)
            s2 = P.op("scalar", lambda e, o=U, i=X: e.activation(o, i, AF.Square),
                      inc="sA")
            P.wait("vector", "sA", s2)
            v3 = P.op("vector", lambda e, o=U, i=X: e.tensor_tensor(
                o, o, i, ALU.mult), inc="sV")
            P.wait("vector", "sV", v3)
            v4 = P.op("vector", lambda e, o=U, i=X: e.scalar_tensor_tensor(
                o, o, GC, i, ALU.mult, ALU.add), inc="sV")
            P.wait("scalar", "sV", v4)
            s3 = P.op("scalar", lambda e, o=U: e.activation(o, o, AF.Tanh, scale=GK),
                      inc="sA")
            P.wait("vector", "sA", s3)
            if m < 32:
                ao = RA_alo[:, m * 1024 : m * 1024 + 1024]
            else:
                ao = RB_sz[:, (m - 32) * 1024 : (m - 32) * 1024 + 1024]
            v5 = P.op("vector", lambda e, o=ao, u=U, i=X: e.scalar_tensor_tensor(
                o, u, 1.0, i, ALU.add, ALU.mult), inc="sV")
            gelu_free[t_slot] = v5
            a_ready[m] = v5
            return ("sA", e1)

        P.wait("scalar", "sV", v_const)
        stage(d_w1, C, KF, 0, h_rhs, lambda k, n: ("sA", hv2[k]), evac_b1)
        CP["b1_evac"] = P.cnt["sV"]

        def a_rhs(k, n):
            if k < 32:
                return RA_alo[:, k * 1024 + n * 512 : k * 1024 + n * 512 + 512]
            return RB_sz[:, (k - 32) * 1024 + n * 512 : (k - 32) * 1024 + n * 512 + 512]

        x2jit_v = {}
        x2jit_slot_free = {}
        first_x2jit = [True]

        def x2jit_dma(m):
            slot = m % 2
            if first_x2jit[0]:
                P.wait("sync", "sS", CP["spill"])
                P.wait("sync", "sV", CP["b1_evac"])
                first_x2jit[0] = False
            if slot in x2jit_slot_free:
                P.wait("sync", "sV", x2jit_slot_free[slot])
            dst = WM_jit[:, slot * 1024 : slot * 1024 + 1024]
            x2jit_v[m] = P.op("sync",
                              lambda e, s=d_x2[128 * m : 128 * m + 128, :], d=dst:
                              e.dma_start(out=d, in_=s),
                              inc="sX", n=16)

        x2jit_dma(0)
        x2jit_dma(1)

        def b2_post(m):
            if m + 2 < KC:
                x2jit_dma(m + 2)

        out_free = {}

        def evac_b2(m, ps, mmv):
            slot = m % 2
            P.wait("vector", "sPE", mmv)
            P.wait("vector", "sX", x2jit_v[m])
            if slot in out_free:
                P.wait("vector", "sS", out_free[slot])
            xj = WM_jit[:, slot * 1024 : slot * 1024 + 1024]
            o = WM_out[:, slot * 1024 : slot * 1024 + 1024]
            vv = P.op("vector",
                      lambda e, oo=o, pp=ps, gc=col(par, 48 + m), xx=xj:
                      e.scalar_tensor_tensor(oo, pp, gc, xx, ALU.mult, ALU.add),
                      inc="sV")
            x2jit_slot_free[slot] = vv
            P.wait("sync", "sV", vv)
            sv = P.op("sync",
                      lambda e, s=WM_out[:, slot * 1024 : slot * 1024 + 1024],
                      d=d_out[128 * m : 128 * m + 128, :]:
                      e.dma_start(out=d, in_=s),
                      inc="sS", n=16)
            out_free[slot] = sv
            return ("sV", vv)

        stage(d_w2, FF, KC, 0, a_rhs, lambda k, n: ("sV", a_ready[k]),
              evac_b2, posthook=b2_post, npieces=2)

        if debug:
            dump(d_dh2, RC_h[:, :], [("sV", CP["h2_done"])])
            dump(d_da[:, 0:32768], RA_alo[:, :], [("sV", CP["b1_evac"])])
            dump(d_da[:, 32768:65536], RB_sz[:, :], [])
            dump(d_dmodt, modt[:, :], [])
            dump(d_dpar, par[:, :], [])

        P.wait("sync", "sS", P.cnt["sS"])

        for engname in ENGS:
            fns = P.items[engname]
            getattr(block, engname)(lambda eng, fns=fns: [f(eng, SEM) for f in fns])

    return nc


_NC_CACHE = None


def _get_nc():
    global _NC_CACHE
    if _NC_CACHE is None:
        _NC_CACHE = build_program()
    return _NC_CACHE


def kernel(
    L0_attn_hidden_states,
    L0_attn_W_in,
    L0_attn_conv_w,
    L0_attn_W_x,
    L0_attn_W_dt,
    L0_attn_b_dt,
    L0_attn_D,
    L0_attn_W_out,
    L0_ffn_W1,
    L0_ffn_W2,
    L0_c,
    L0_wmod,
):
    f = lambda a: np.ascontiguousarray(np.asarray(a, dtype=np.float32))

    def tm(w):
        wT = np.asarray(w, dtype=np.float32).T
        K, M = wT.shape
        return np.ascontiguousarray(
            wT.reshape(K // 128, 128, M // 128, 128).transpose(2, 1, 0, 3)
            .reshape(M // 128, 128, K))

    xT = f(np.asarray(L0_attn_hidden_states, dtype=np.float32).reshape(8192, 2048).T)
    winT = tm(L0_attn_W_in)
    convT = tm(L0_attn_conv_w)
    woutT = tm(L0_attn_W_out)
    w1T = tm(L0_ffn_W1)
    w2T = tm(L0_ffn_W2)
    wmod = f(L0_wmod)
    PIECES_H = ([(256 * i, 256) for i in range(16)]
                + [(6144 + 256 * i, 256) for i in range(16)])
    PIECESF_H = ([(4096 + 128 * i, 128) for i in range(16)]
                 + [(10240 + 128 * i, 128) for i in range(16)])
    wmodB = np.stack([
        np.ascontiguousarray(
            wmod[:, j0:j0 + w].reshape(16, 128, w).transpose(1, 0, 2).reshape(128, 16 * w))
        for j0, w in PIECES_H])
    wmodF = np.stack([
        np.ascontiguousarray(
            wmod[:, j0:j0 + w].reshape(16, 128, w).transpose(1, 0, 2).reshape(128, 16 * w))
        for j0, w in PIECESF_H])
    cT = f(np.asarray(L0_c, dtype=np.float32).reshape(16, 128).T)
    dT = f(np.asarray(L0_attn_D, dtype=np.float32).reshape(32, 128).T)

    nc = _get_nc()
    in_maps = []
    for i in range(NCORES):
        in_maps.append({
            "xT": np.ascontiguousarray(xT[:, T * i : T * (i + 1)]),
            "winTM": winT,
            "convTM": convT,
            "woutTM": woutT,
            "w1TM": w1T,
            "w2TM": w2T,
            "wmodB": wmodB,
            "wmodF": wmodF,
            "cT": cT,
            "dT": dT,
        })
    res = run_bass_kernel_spmd(nc, in_maps, core_ids=list(range(NCORES)))
    out = np.concatenate([r["outT"] for r in res.results], axis=1)
    return np.ascontiguousarray(out.T).reshape(1, 8192, 2048)


# version 09
# speedup vs baseline: 1.0992x; 1.0233x over previous
from collections import deque

import numpy as np

import concourse.bass as bass
import concourse.mybir as mybir
from concourse.bass_utils import run_bass_kernel_spmd

F32 = mybir.dt.float32
BF16 = mybir.dt.bfloat16
U8 = mybir.dt.uint8
AF = mybir.ActivationFunctionType
ALU = mybir.AluOpType

NCORES = 8
C = 2048
INTER = 4096
FF = 8192
T = 1024
KC = C // 128
KI = INTER // 128
KF = FF // 128
NT = 2
MAGIC = 12582912.0
GK = 0.7978845608028654
GC = 0.044715
QMAX = 32767.0
QMIN = -32768.0
EPS = 1e-5
JMOD = 12288
NBLK = 24

ENGS = ("sync", "gpsimd", "tensor", "vector", "scalar")
SEMS = ("sW", "sX", "sS", "sPE", "sV", "sA", "sC")


class Sched:
    def __init__(self):
        self.items = {e: [] for e in ENGS}
        self.cnt = {s: 0 for s in SEMS}
        self.last_wait = {e: {s: 0 for s in SEMS} for e in ENGS}

    def wait(self, eng, sem, val):
        assert 0 <= val <= self.cnt[sem], (eng, sem, val, self.cnt[sem])
        if val <= self.last_wait[eng][sem]:
            return
        self.last_wait[eng][sem] = val
        self.items[eng].append(lambda e, S, sem=sem, val=val: e.wait_ge(S[sem], val))

    def op(self, eng, build, inc=None, n=1):
        if inc is None:
            self.items[eng].append(lambda e, S, b=build: b(e))
            return None
        self.cnt[inc] += n
        v = self.cnt[inc]
        self.items[eng].append(lambda e, S, b=build, s=inc, m=n: b(e).then_inc(S[s], m))
        return v


def place(nc, base_addr, name, shape, dtype, offset):
    h = nc.virtual_sbuf_tensor(name, shape, dtype)
    nc._tensor(h.name, h.shape, h.dtype, type="SB")
    mloc = nc.lookup_mloc(h)
    mloc.allocated = True
    mloc.addr = base_addr + offset
    return h


def build_program(debug=False):
    nc = bass.Bass("TRN2", target_bir_lowering=False)

    d_x = nc.dram_tensor("xT", [C, T], F32, kind="ExternalInput").ap()
    d_win = nc.dram_tensor("winTM", [FF // 128, 128, C], F32, kind="ExternalInput").ap()
    d_conv = nc.dram_tensor("convTM", [KI, 128, INTER], F32, kind="ExternalInput").ap()
    d_wout = nc.dram_tensor("woutTM", [KC, 128, INTER], F32, kind="ExternalInput").ap()
    d_w1 = nc.dram_tensor("w1TM", [KF, 128, C], F32, kind="ExternalInput").ap()
    d_w2 = nc.dram_tensor("w2TM", [KC, 128, FF], F32, kind="ExternalInput").ap()
    d_wmodB = nc.dram_tensor("wmodB", [32, 128, 4096], F32, kind="ExternalInput").ap()
    d_wmodF = nc.dram_tensor("wmodF", [32, 128, 2048], F32, kind="ExternalInput").ap()
    d_c = nc.dram_tensor("cT", [128, 16], F32, kind="ExternalInput").ap()
    d_D = nc.dram_tensor("dT", [128, 32], F32, kind="ExternalInput").ap()
    d_out = nc.dram_tensor("outT", [C, T], F32, kind="ExternalOutput").ap()
    d_x2 = nc.dram_tensor("x2spill", [C, T], F32,
                          kind="ExternalOutput" if debug else "Internal").ap()
    if debug:
        d_dh = nc.dram_tensor("dbg_h", [128, 16384], F32, kind="ExternalOutput").ap()
        d_dxh = nc.dram_tensor("dbg_xh", [128, 32768], F32, kind="ExternalOutput").ap()
        d_dsz = nc.dram_tensor("dbg_sz", [128, 32768], F32, kind="ExternalOutput").ap()
        d_dyz = nc.dram_tensor("dbg_yz", [128, 32768], F32, kind="ExternalOutput").ap()
        d_dh2 = nc.dram_tensor("dbg_h2", [128, 16384], F32, kind="ExternalOutput").ap()
        d_da = nc.dram_tensor("dbg_a", [128, 65536], F32, kind="ExternalOutput").ap()
        d_dmodt = nc.dram_tensor("dbg_modt", [128, 96], F32, kind="ExternalOutput").ap()
        d_dpar = nc.dram_tensor("dbg_par", [128, 80], F32, kind="ExternalOutput").ap()
        d_dbc = nc.dram_tensor("dbg_bc", [128, 2048], F32, kind="ExternalOutput").ap()
    d_mod = nc.dram_tensor("modscr", [96, 128], F32, kind="Internal").ap()

    P = Sched()
    CP = {}

    OFF = {}
    cur = 0

    def A(name, nbytes):
        nonlocal cur
        OFF[name] = cur
        cur += (nbytes + 31) // 32 * 32

    A("wbuf", 16384)
    A("rowbuf", 2048)
    A("rowbuf2", 2048)
    A("ones_row", 512)
    A("ones_col", 32)
    A("silu_cb", 32)
    A("cd", 64)
    A("modt", 384)
    A("par", 320)
    A("magic", 32)
    A("dt", 128)
    A("m96", 512)
    A("RA", 65536)
    A("RB", 65536)
    A("RC", 32768)
    A("WM", 24576)
    assert cur <= 211000, cur

    with (
        nc.sbuf_tensor("arena", [128, cur], U8) as arena,
        nc.psum_tensor("mm_ps", [128, 2048], F32) as mm_ps,
        nc.psum_tensor("bc_ps", [128, 1024], F32) as bc_ps,
        nc.psum_tensor("row_ps", [1, 512], F32) as row_ps,
        nc.psum_tensor("mod_ps", [1, 512], F32) as mod_ps,
        nc.semaphore("sW") as _sW,
        nc.semaphore("sX") as _sX,
        nc.semaphore("sS") as _sS,
        nc.semaphore("sPE") as _sPE,
        nc.semaphore("sV") as _sV,
        nc.semaphore("sA") as _sA,
        nc.semaphore("sC") as _sC,
        nc.Block() as block,
    ):
        SEM = {"sW": _sW, "sX": _sX, "sS": _sS, "sPE": _sPE, "sV": _sV,
               "sA": _sA, "sC": _sC}
        base = nc.lookup_mloc(arena).addr

        wbuf = place(nc, base, "v_wbuf", [128, 8192], BF16, OFF["wbuf"])
        rstd_bc = place(nc, base, "v_rstd_bc", [128, 1024], F32, OFF["RB"] + 28672)
        mrstd_bc = place(nc, base, "v_mrstd_bc", [128, 1024], F32, OFF["RB"] + 32768)
        rowb = place(nc, base, "v_rowb", [1, 512], F32, OFF["rowbuf"])
        rowb2 = place(nc, base, "v_rowb2", [1, 512], F32, OFF["rowbuf2"])
        ones_row = place(nc, base, "v_ones_row", [1, 128], F32, OFF["ones_row"])
        ones_col = place(nc, base, "v_ones_col", [128, 16], BF16, OFF["ones_col"])
        silu_cb = place(nc, base, "v_silu_cb", [128, 16], BF16, OFF["silu_cb"])
        cd = place(nc, base, "v_cd", [128, 16], F32, OFF["cd"])
        modt = place(nc, base, "v_modt", [128, 96], F32, OFF["modt"])
        par = place(nc, base, "v_par", [128, 80], F32, OFF["par"])
        magic = place(nc, base, "v_magic", [128, 1], F32, OFF["magic"])
        Dt = place(nc, base, "v_dt", [128, 32], F32, OFF["dt"])
        m96 = place(nc, base, "v_m96", [96, 128], F32, OFF["m96"])
        RA_x = place(nc, base, "v_RA_x", [128, 16384], F32, OFF["RA"])
        RA_xh = place(nc, base, "v_RA_xh", [128, 32768], BF16, OFF["RA"])
        RA_x2 = place(nc, base, "v_RA_x2", [128, 16384], F32, OFF["RA"])
        RA_alo = place(nc, base, "v_RA_alo", [128, 32768], BF16, OFF["RA"])
        RB_sz = place(nc, base, "v_RB_sz", [128, 32768], BF16, OFF["RB"])
        RB_sum = place(nc, base, "v_RB_sum", [1, 1024], F32, OFF["RB"])
        RB_sq = place(nc, base, "v_RB_sq", [1, 1024], F32, OFF["RB"] + 4096)
        RB_rstd = place(nc, base, "v_RB_rstd", [1, 1024], F32, OFF["RB"] + 8192)
        RB_mrstd = place(nc, base, "v_RB_mrstd", [1, 1024], F32, OFF["RB"] + 12288)
        RB_htmp = place(nc, base, "v_RB_htmp", [128, 2048], F32, OFF["RB"] + 16384)
        RB_xbf = place(nc, base, "v_RB_xbf", [128, 8192], BF16, OFF["RB"] + 36864)
        RC_h = place(nc, base, "v_RC_h", [128, 16384], BF16, OFF["RC"])
        WM_b = [place(nc, base, "v_WM_b%d" % s, [128, 4096], BF16,
                      OFF["WM"] + 8192 * s) for s in range(3)]
        WM_f = [place(nc, base, "v_WM_f%d" % s, [128, 2048], F32,
                      OFF["WM"] + 8192 * s) for s in range(3)]
        WM_g = place(nc, base, "v_WM_g", [128, 2048], F32, OFF["WM"])
        WM_st = place(nc, base, "v_WM_st", [128, 4096], BF16, OFF["WM"])
        WM_out = place(nc, base, "v_WM_out", [128, 2048], F32, OFF["WM"] + 8192)
        WM_jit = place(nc, base, "v_WM_jit", [128, 2048], F32, OFF["WM"] + 16384)

        def col(t, j):
            return t[:, j : j + 1]

        bank_free = [None] * 4
        unit_free = [None] * 4
        group_ctr = [0]
        unit_ctr = [0]

        def mm_bank_ap(b):
            return mm_ps[:, 512 * b : 512 * b + 512]

        def pe_wait(sem, cnt):
            P.wait("tensor", sem, cnt)

        def stage(wdram, Ktot, Mch, m_off, rhs_ap, rhs_ready, evac, hook=None,
                  posthook=None, npieces=1):
            Kch = Ktot // 128
            upg = Ktot // 2048
            assert upg in (1, 2, 4) and upg % npieces == 0
            upp = upg // npieces
            kpp = Kch // npieces
            ring_slots = 4 // upp
            total_pieces = Mch * npieces
            piece_info = []
            issued = [0]

            def issue_piece(p):
                u0 = unit_ctr[0] % 4
                for uu in range(u0, u0 + upp):
                    if unit_free[uu] is not None:
                        P.wait("gpsimd", *unit_free[uu])
                m = p // npieces
                pc = p % npieces
                src = wdram[m_off + m, :, pc * kpp * 128 : (pc + 1) * kpp * 128]
                dst = wbuf[:, u0 * 2048 : u0 * 2048 + kpp * 128]
                wv = P.op("gpsimd", lambda e, s=src, d=dst: e.dma_start(out=d, in_=s),
                          inc="sW", n=16)
                piece_info.append((u0, wv))
                unit_ctr[0] += upp

            def ensure(upto):
                while issued[0] < min(total_pieces, upto):
                    issue_piece(issued[0])
                    issued[0] += 1

            ensure(npieces + max(0, ring_slots - npieces))
            for m in range(Mch):
                if hook is not None:
                    hook(m)
                ensure((m + 1) * npieces + max(0, ring_slots - npieces))
                g = group_ctr[0]
                group_ctr[0] += 2
                b0, b1 = g % 4, (g + 1) % 4
                for b in (b0, b1):
                    if bank_free[b] is not None:
                        pe_wait(*bank_free[b])
                banks = (b0, b1)
                v_final = None
                for k in range(Kch):
                    pc = k // kpp
                    u0, wv = piece_info[m * npieces + pc]
                    pe_wait("sW", wv)
                    lhs = wbuf[:, u0 * 2048 + (k % kpp) * 128 :
                               u0 * 2048 + (k % kpp) * 128 + 128]
                    piece_end = (k + 1) % kpp == 0
                    for n in range(NT):
                        r = rhs_ready(k, n)
                        if r is not None:
                            pe_wait(*r)
                        is_inc = piece_end and n == NT - 1
                        v = P.op("tensor",
                                 lambda e, o=mm_bank_ap(banks[n]), l=lhs,
                                 r2=rhs_ap(k, n), st=(k == 0), sp=(k == Kch - 1):
                                 e.matmul(o, l, r2, start=st, stop=sp),
                                 inc="sPE" if is_inc else None)
                        if is_inc:
                            v_final = v
                    if piece_end:
                        for uu in range(u0, u0 + upp):
                            unit_free[uu] = ("sPE", v_final)
                assert b1 == b0 + 1
                f0 = evac(m, mm_ps[:, 512 * b0 : 512 * b0 + 1024], v_final)
                bank_free[b0] = f0
                bank_free[b1] = f0
                if posthook is not None:
                    posthook(m)

        P.op("vector", lambda e: e.memset(ones_col[:, :], 1.0))
        P.op("vector", lambda e: e.memset(ones_row[:, :], 1.0))
        v_const = P.op("vector", lambda e: e.memset(magic[:, :], MAGIC), inc="sV")

        x_c = P.op("sync", lambda e: e.dma_start(out=cd[:, :], in_=d_c), inc="sX", n=16)
        P.op("sync", lambda e: e.dma_start(out=Dt[:, :], in_=d_D), inc="sX", n=16)
        x_x = None
        for gq in range(4):
            src = d_x[512 * gq : 512 * gq + 512, :].rearrange("(kc p) t -> p kc t", p=128)
            dst = RA_x[:, 4096 * gq : 4096 * gq + 4096].rearrange("p (kc t) -> p kc t", t=T)
            x_x = P.op("sync", lambda e, s=src, d=dst: e.dma_start(out=d, in_=s),
                       inc="sX", n=16)

        P.wait("scalar", "sX", x_c)
        a_silu = P.op("scalar", lambda e: e.activation(cd[:, :], cd[:, :], AF.Silu),
                      inc="sA")
        P.wait("scalar", "sA", a_silu)
        a_siluc = P.op("scalar", lambda e: e.copy(silu_cb[:, :], cd[:, :]), inc="sA")

        def stats(srcf32, between=None):
            ev = None
            for n in range(NT):
                for sq in (0, 1):
                    av = None
                    for k in range(KC):
                        xin = srcf32[:, k * 1024 + n * 512 : k * 1024 + n * 512 + 512]
                        cv = RB_xbf[:, k * 512 : k * 512 + 512]
                        if sq == 0:
                            av = P.op("scalar", lambda e, o=cv, i=xin: e.copy(o, i),
                                      inc="sA" if k == KC - 1 else None)
                        else:
                            av = P.op("scalar",
                                      lambda e, o=cv, i=xin: e.activation(o, i, AF.Square),
                                      inc="sA" if k == KC - 1 else None)
                    if between is not None:
                        between()
                    P.wait("tensor", "sA", av)
                    pv = None
                    for k in range(KC):
                        cv = RB_xbf[:, k * 512 : k * 512 + 512]
                        pv = P.op("tensor",
                                  lambda e, r=cv, st=(k == 0), sp=(k == KC - 1):
                                  e.matmul(row_ps[:, :], ones_col[:, 0:1], r,
                                           start=st, stop=sp),
                                  inc="sPE" if k == KC - 1 else None)
                    P.wait("scalar", "sPE", pv)
                    P.wait("vector", "sPE", pv)
                    dst = (RB_sum if sq == 0 else RB_sq)[0:1, n * 512 : n * 512 + 512]
                    ev = P.op("vector", lambda e, d=dst: e.tensor_copy(d, row_ps[:, :]),
                              inc="sV")
                    P.wait("tensor", "sV", ev)
            return ev

        P.wait("scalar", "sX", x_x)

        PIECES = []
        for i in range(16):
            PIECES.append((256 * i, 256, "b"))
        for i in range(16):
            PIECES.append((6144 + 256 * i, 256, "b"))
        for i in range(16):
            PIECES.append((4096 + 128 * i, 128, "f"))
        for i in range(16):
            PIECES.append((10240 + 128 * i, 128, "f"))

        PIECE_SUB = []
        nb = nf = 0
        for (_j0, _w, _k) in PIECES:
            if _k == "b":
                PIECE_SUB.append(nb); nb += 1
            else:
                PIECE_SUB.append(nf); nf += 1

        mod_state = {"mm": [0, 0, 0], "store": [0, 0], "idx": 0, "last_mm": 0,
                     "dma_idx": 0, "dma_v": {}}

        def mod_piece_dma():
            i = mod_state["dma_idx"]
            if i >= len(PIECES):
                return
            mod_state["dma_idx"] += 1
            j0, wdt, kind = PIECES[i]
            s = i % 3
            if mod_state["mm"][s]:
                P.wait("gpsimd", "sPE", mod_state["mm"][s])
            if kind == "b":
                srcw = d_wmodB[PIECE_SUB[i], :, :]
                slot = WM_b[s][:, :]
            else:
                srcw = d_wmodF[PIECE_SUB[i], :, :]
                slot = WM_f[s][:, :]
            mod_state["dma_v"][i] = P.op(
                "gpsimd", lambda e, s_=srcw, d_=slot: e.dma_start(out=d_, in_=s_),
                inc="sW", n=16)

        def mod_piece():
            if mod_state["dma_idx"] <= mod_state["idx"]:
                mod_piece_dma()
            i = mod_state["idx"]
            mod_state["idx"] += 1
            j0, wdt, kind = PIECES[i]
            s = i % 3
            wv = mod_state["dma_v"][i]
            pe_wait("sW", wv)
            pe_wait("sA", a_siluc)
            pv = None
            for kc in range(KC):
                if kind == "b":
                    lhs = silu_cb[:, kc : kc + 1]
                    rhs = WM_b[s][:, kc * wdt : kc * wdt + wdt]
                else:
                    lhs = cd[:, kc : kc + 1]
                    rhs = WM_f[s][:, kc * wdt : kc * wdt + wdt]
                pv = P.op("tensor",
                          lambda e, l=lhs, r=rhs, st=(kc == 0), sp=(kc == KC - 1),
                          w_=wdt: e.matmul(mod_ps[0:1, 0:w_], l, r, start=st, stop=sp),
                          inc="sPE" if kc == KC - 1 else None)
            mod_state["mm"][s] = pv
            mod_state["last_mm"] = pv
            rs = i % 2
            rb = rowb if rs == 0 else rowb2
            if mod_state["store"][rs]:
                P.wait("scalar", "sS", mod_state["store"][rs])
            P.wait("scalar", "sPE", pv)
            ev = P.op("scalar", lambda e, r=rb, w_=wdt: e.copy(
                r[0:1, 0:w_], mod_ps[0:1, 0:w_]), inc="sA")
            P.wait("tensor", "sA", ev)
            P.wait("sync", "sA", ev)
            J0 = j0 // 128
            nrow = wdt // 128
            sv = P.op("sync",
                      lambda e, r=rb[0:1, 0:wdt].rearrange("p (a c) -> p a c", c=128),
                      d=d_mod[J0 : J0 + nrow, :].rearrange("a c -> () a c"):
                      e.dma_start(out=d, in_=r),
                      inc="sS", n=16)
            mod_state["store"][rs] = sv

        mod_piece_dma()
        mod_piece_dma()

        def p0_between():
            for _ in range(4):
                mod_piece_dma()
                mod_piece()

        stats1 = stats(RA_x, between=p0_between)
        CP["saca_store"] = P.cnt["sS"]



        def row_math():
            v1 = P.op("vector", lambda e: e.tensor_scalar(
                RB_sum[:, :], RB_sum[:, :], 1.0 / 2048.0, None, ALU.mult), inc="sV")
            v2 = P.op("vector", lambda e: e.tensor_scalar(
                RB_sq[:, :], RB_sq[:, :], 1.0 / 2048.0, EPS, ALU.mult, ALU.add),
                inc="sV")
            P.wait("vector", "sV", v1)
            v3 = P.op("vector", lambda e: e.tensor_tensor(
                RB_rstd[:, :], RB_sum[:, :], RB_sum[:, :], ALU.mult), inc="sV")
            P.wait("vector", "sV", max(v2, v3))
            v4 = P.op("vector", lambda e: e.tensor_tensor(
                RB_sq[:, :], RB_sq[:, :], RB_rstd[:, :], ALU.subtract), inc="sV")
            P.wait("scalar", "sV", v4)
            a1 = P.op("scalar", lambda e: e.activation(
                RB_rstd[:, :], RB_sq[:, :], AF.Ln), inc="sA")
            P.wait("scalar", "sA", a1)
            a2 = P.op("scalar", lambda e: e.activation(
                RB_rstd[:, :], RB_rstd[:, :], AF.Exp, scale=-0.5), inc="sA")
            P.wait("vector", "sA", a2)
            v5 = P.op("vector", lambda e: e.tensor_tensor(
                RB_mrstd[:, :], RB_sum[:, :], RB_rstd[:, :], ALU.mult), inc="sV")
            return v5

        P.wait("vector", "sV", stats1)
        rows1 = row_math()

        def bcast(src_row, dst_bc, ready_v):
            pe_wait("sV", ready_v)
            pv = None
            for n in range(NT):
                pv = P.op("tensor",
                          lambda e, r=src_row[0:1, n * 512 : n * 512 + 512],
                          o=bc_ps[:, n * 512 : n * 512 + 512]:
                          e.matmul(o, ones_row[0:1, :], r, start=True, stop=True),
                          inc="sPE")
            P.wait("vector", "sPE", pv)
            ev = P.op("vector", lambda e, d=dst_bc: e.tensor_copy(d[:, :], bc_ps[:, :]),
                      inc="sV")
            P.wait("tensor", "sV", ev)
            return ev

        bc1 = bcast(RB_rstd, rstd_bc, rows1)
        bc2 = bcast(RB_mrstd, mrstd_bc, rows1)

        def make_ln(srcf32):
            lv = []
            for k in range(KC):
                ht = RB_htmp[:, (k % 2) * 1024 : (k % 2) * 1024 + 1024]
                xin = srcf32[:, k * 1024 : k * 1024 + 1024]
                t1 = P.op("vector", lambda e, o=ht, i=xin: e.tensor_tensor(
                    o, i, rstd_bc[:, :], ALU.mult), inc="sV")
                P.wait("vector", "sV", t1)
                t2 = P.op("vector",
                          lambda e, o=RC_h[:, k * 1024 : k * 1024 + 1024], i=ht:
                          e.tensor_tensor(o, i, mrstd_bc[:, :], ALU.subtract),
                          inc="sV")
                lv.append(t2)
            return lv

        def make_h(lv, shift_col, scale_col):
            hv = []
            for k in range(KC):
                P.wait("scalar", "sV", lv[k])
                t3 = P.op("scalar",
                          lambda e, o=RC_h[:, k * 1024 : k * 1024 + 1024],
                          s1=scale_col(k), s2=shift_col(k):
                          e.activation(o, o, AF.Identity, bias=s2, scale=s1),
                          inc="sA")
                hv.append(t3)
            return hv

        P.wait("vector", "sV", max(bc1, bc2))
        lv1 = make_ln(RA_x)
        P.wait("sync", "sS", CP["saca_store"])
        r1 = P.op("sync", lambda e: e.dma_start(out=m96[0:32, :], in_=d_mod[0:32, :]),
                  inc="sX", n=16)
        P.wait("vector", "sX", r1)
        tv = None
        for pb in range(4):
            tv = P.op("vector",
                      lambda e, o=modt[32 * pb : 32 * pb + 32, 0:32],
                      i=m96[0:32, 32 * pb : 32 * pb + 32]: e.transpose(o, i),
                      inc="sV")
        P.wait("vector", "sV", tv)
        v_cap1 = P.op("vector", lambda e: e.tensor_scalar(
            par[:, 0:16], modt[:, 16:32], 1.0, None, ALU.add), inc="sV")
        P.wait("scalar", "sV", v_cap1)
        hv1 = make_h(lv1, lambda k: col(modt, k), lambda k: col(par, k))
        CP["h1_done"] = P.cnt["sA"]

        def h_rhs(k, n):
            return RC_h[:, k * 1024 + n * 512 : k * 1024 + n * 512 + 512]

        first_a1x = [True]

        def evac_a1x(m, ps, mmv):
            P.wait("scalar", "sPE", mmv)
            if first_a1x[0]:
                P.wait("scalar", "sA", CP["h1_done"])
                first_a1x[0] = False
            o = RA_xh[:, m * 1024 : m * 1024 + 1024]
            av = P.op("scalar", lambda e, oo=o, pp=ps: e.copy(oo, pp), inc="sA")
            return ("sA", av)

        def a1_hook(m):
            if m % 2 == 0:
                mod_piece_dma()
                if mod_state["idx"] < mod_state["dma_idx"] - 1 or \
                   mod_state["dma_idx"] == len(PIECES):
                    if mod_state["idx"] < len(PIECES):
                        mod_piece()

        stage(d_win, C, 32, 0, h_rhs, lambda k, n: ("sA", hv1[k]), evac_a1x,
              hook=a1_hook)
        CP["a1x_evac"] = P.cnt["sA"]

        def evac_a1z(m, ps, mmv):
            P.wait("scalar", "sPE", mmv)
            o = RB_sz[:, m * 1024 : m * 1024 + 1024]
            av = P.op("scalar", lambda e, oo=o, pp=ps: e.activation(oo, pp, AF.Silu),
                      inc="sA")
            return ("sA", av)

        stage(d_win, C, 32, 32, h_rhs, lambda k, n: ("sA", hv1[k]), evac_a1z,
              hook=a1_hook)
        CP["a1z_evac"] = P.cnt["sA"]


        dbg_gate = {}

        def dump(dram, sb_ap, gates):
            for sem, cnt in gates:
                P.wait("gpsimd", sem, cnt)
            return P.op("gpsimd", lambda e, d=dram, s=sb_ap: e.dma_start(out=d, in_=s),
                        inc="sS", n=16)

        if debug:
            dump(d_dh, RC_h[:, :], [("sV", CP["h1_done"])])
            dump(d_dxh, RA_xh[:, :], [("sA", CP["a1x_evac"])])
            dbg_gate["sz"] = dump(d_dsz, RB_sz[:, :], [("sA", CP["a1z_evac"])])

        def xh_rhs(k, n):
            return RA_xh[:, k * 1024 + n * 512 : k * 1024 + n * 512 + 512]

        first_a2 = [True]

        def evac_a2(m, ps, mmv):
            P.wait("vector", "sPE", mmv)
            if first_a2[0]:
                P.wait("vector", "sA", CP["a1z_evac"])
                if debug:
                    P.wait("vector", "sS", dbg_gate["sz"])
                first_a2[0] = False
            o = RB_sz[:, m * 1024 : m * 1024 + 1024]
            vv = P.op("vector",
                      lambda e, oo=o, pp=ps, dc=col(Dt, m):
                      e.scalar_tensor_tensor(oo, pp, dc, oo, ALU.mult, ALU.mult),
                      inc="sV")
            return ("sV", vv)

        def a2_hook(m):
            if m % 2 == 0:
                mod_piece_dma()
                if mod_state["idx"] < len(PIECES):
                    mod_piece()

        stage(d_conv, INTER, KI, 0, xh_rhs, lambda k, n: ("sA", CP["a1x_evac"]),
              evac_a2, hook=a2_hook)
        CP["a2_evac"] = P.cnt["sV"]

        P.wait("sync", "sS", P.cnt["sS"])
        r2 = P.op("sync", lambda e: e.dma_start(out=m96[32:96, :], in_=d_mod[32:96, :]),
                  inc="sX", n=16)
        P.wait("vector", "sX", r2)
        tv2 = None
        for pb in range(4):
            for jb in (1, 2):
                tv2 = P.op("vector",
                           lambda e, o=modt[32 * pb : 32 * pb + 32,
                                            32 * jb : 32 * jb + 32],
                           i=m96[32 * jb : 32 * jb + 32, 32 * pb : 32 * pb + 32]:
                           e.transpose(o, i),
                           inc="sV")
        P.wait("vector", "sV", tv2)
        P.op("vector", lambda e: e.tensor_scalar(
            par[:, 32:48], modt[:, 64:80], 1.0, None, ALU.add), inc="sV")
        P.wait("scalar", "sV", tv2)
        sg1 = P.op("scalar", lambda e: e.activation(
            par[:, 64:80], modt[:, 32:48], AF.Sign), inc="sA")
        ab1 = P.op("scalar", lambda e: e.activation(
            par[:, 16:32], modt[:, 32:48], AF.Abs), inc="sA")
        P.wait("vector", "sA", ab1)
        am1 = P.op("vector", lambda e: e.tensor_scalar(
            par[:, 16:32], par[:, 16:32], 0.25, None, ALU.max), inc="sV")
        P.wait("vector", "sV", am1)
        ga_v = P.op("vector", lambda e: e.tensor_tensor(
            par[:, 16:32], par[:, 16:32], par[:, 64:80], ALU.mult), inc="sV")
        P.wait("scalar", "sV", ga_v)
        sg2 = P.op("scalar", lambda e: e.activation(
            par[:, 64:80], modt[:, 80:96], AF.Sign), inc="sA")
        ab2 = P.op("scalar", lambda e: e.activation(
            par[:, 48:64], modt[:, 80:96], AF.Abs), inc="sA")
        P.wait("vector", "sA", ab2)
        am2 = P.op("vector", lambda e: e.tensor_scalar(
            par[:, 48:64], par[:, 48:64], 0.25, 0.5, ALU.max, ALU.mult), inc="sV")
        P.wait("vector", "sV", am2)
        g2 = P.op("vector", lambda e: e.tensor_tensor(
            par[:, 48:64], par[:, 48:64], par[:, 64:80], ALU.mult), inc="sV")
        CP["params2"] = P.cnt["sV"]

        def yz_rhs(k, n):
            return RB_sz[:, k * 1024 + n * 512 : k * 1024 + n * 512 + 512]

        xjit_v = {}
        xjit_slot_free = {}

        def xjit_dma(m):
            slot = m % 2
            if slot in xjit_slot_free:
                P.wait("sync", "sV", xjit_slot_free[slot])
            P.wait("sync", "sPE", mod_state["last_mm"])
            dst = WM_jit[:, slot * 1024 : slot * 1024 + 1024]
            xjit_v[m] = P.op("sync",
                             lambda e, s=d_x[128 * m : 128 * m + 128, :], d=dst:
                             e.dma_start(out=d, in_=s),
                             inc="sX", n=16)

        xjit_dma(0)
        xjit_dma(1)

        st_rows = [row_ps[0:1, 0:512], mod_ps[0:1, 0:512],
                   bc_ps[0:1, 0:512], bc_ps[0:1, 512:1024]]
        st_cv = [[WM_st[:, 0:512], WM_st[:, 512:1024]],
                 [WM_st[:, 1024:1536], WM_st[:, 1536:2048]]]
        st_sq = [[WM_st[:, 2048:2560], WM_st[:, 2560:3072]],
                 [WM_st[:, 3072:3584], WM_st[:, 3584:4096]]]
        st_state = {"pv": [0, 0, 0, 0], "cv": {}}

        def a3_stats_convs(m):
            P.wait("scalar", "sV", a3_x2v[m])
            if m >= 2:
                P.wait("scalar", "sPE", max(st_state["pv"]))
            last = None
            for n in range(NT):
                xin = RA_x2[:, m * 1024 + n * 512 : m * 1024 + n * 512 + 512]
                P.op("scalar", lambda e, o=st_cv[m % 2][n], i=xin: e.copy(o, i))
                last = P.op("scalar",
                            lambda e, o=st_sq[m % 2][n], i=xin:
                            e.activation(o, i, AF.Square),
                            inc="sA" if n == NT - 1 else None)
            st_state["cv"][m] = last

        def a3_stats_mms(m):
            P.wait("tensor", "sA", st_state["cv"][m])
            for n in range(NT):
                st_state["pv"][n] = P.op(
                    "tensor", lambda e, r=st_cv[m % 2][n], o=st_rows[n],
                    st=(m == 0), sp=(m == KC - 1):
                    e.matmul(o, ones_col[:, 0:1], r, start=st, stop=sp),
                    inc="sPE")
            for n in range(NT):
                st_state["pv"][2 + n] = P.op(
                    "tensor", lambda e, r=st_sq[m % 2][n], o=st_rows[2 + n],
                    st=(m == 0), sp=(m == KC - 1):
                    e.matmul(o, ones_col[:, 0:1], r, start=st, stop=sp),
                    inc="sPE")

        def a3_post(m):
            if m + 2 < KC:
                xjit_dma(m + 2)
            a3_stats_convs(m)
            if m >= 1:
                a3_stats_mms(m - 1)
            if m == KC - 1:
                a3_stats_mms(m)

        a3_x2v = {}

        def evac_a3(m, ps, mmv):
            P.wait("vector", "sPE", mmv)
            P.wait("vector", "sX", xjit_v[m])
            P.wait("vector", "sV", CP["params2"])
            slot = m % 2
            xj = WM_jit[:, slot * 1024 : slot * 1024 + 1024]
            o = RA_x2[:, m * 1024 : m * 1024 + 1024]
            vv = P.op("vector",
                      lambda e, oo=o, pp=ps, gc=col(par, 16 + m), xx=xj:
                      e.scalar_tensor_tensor(oo, pp, gc, xx, ALU.mult, ALU.add),
                      inc="sV")
            xjit_slot_free[slot] = vv
            a3_x2v[m] = vv
            return ("sV", vv)

        stage(d_wout, INTER, KC, 0, yz_rhs, lambda k, n: ("sV", CP["a2_evac"]),
              evac_a3, posthook=a3_post)
        CP["a3_evac"] = P.cnt["sV"]
        CP["a3_mm"] = P.cnt["sPE"]

        if debug:
            dbg_gate["yz"] = dump(d_dyz, RB_sz[:, :],
                                  [("sV", CP["a2_evac"]), ("sPE", CP["a3_mm"])])
            dump(d_dbc[:, 0:1024], rstd_bc[:, :], [("sV", CP["h1_done"])])
            dump(d_dbc[:, 1024:2048], mrstd_bc[:, :], [])

        if debug:
            P.wait("scalar", "sS", dbg_gate["yz"])
            P.wait("vector", "sS", dbg_gate["yz"])
        P.wait("vector", "sPE", max(st_state["pv"]))
        for nn in range(NT):
            P.op("vector", lambda e, d=RB_sum[0:1, nn * 512 : nn * 512 + 512],
                 s=st_rows[nn]: e.tensor_copy(d, s), inc="sV")
        for nn in range(NT):
            P.op("vector", lambda e, d=RB_sq[0:1, nn * 512 : nn * 512 + 512],
                 s=st_rows[2 + nn]: e.tensor_copy(d, s), inc="sV")
        stats2 = P.cnt["sV"]
        P.wait("tensor", "sV", stats2)
        P.wait("vector", "sV", stats2)
        rows2 = row_math()
        bc3 = bcast(RB_rstd, rstd_bc, rows2)
        bc4 = bcast(RB_mrstd, mrstd_bc, rows2)
        P.wait("vector", "sV", max(bc3, bc4))
        lv2 = make_ln(RA_x2)
        P.wait("scalar", "sV", CP["params2"])
        hv2 = make_h(lv2, lambda k: col(modt, 48 + k), lambda k: col(par, 32 + k))
        CP["h2_done"] = P.cnt["sA"]

        P.wait("sync", "sA", CP["h2_done"])
        spill_v = None
        for gq in range(4):
            src = RA_x2[:, 4096 * gq : 4096 * gq + 4096].rearrange(
                "p (kc t) -> p kc t", t=T)
            dst = d_x2[512 * gq : 512 * gq + 512, :].rearrange(
                "(kc p) t -> p kc t", p=128)
            spill_v = P.op("sync", lambda e, s=src, d=dst: e.dma_start(out=d, in_=s),
                           inc="sS", n=16)
        CP["spill"] = spill_v

        gelu_free = {}
        first_b1 = [True]
        a_ready = {}
        GX = [WM_g[:, 0:1024], WM_jit[:, 0:1024]]
        GU = [WM_g[:, 1024:2048], WM_jit[:, 1024:2048]]

        def evac_b1(m, ps, mmv):
            t_slot = m % 2
            X = GX[t_slot]
            U = GU[t_slot]
            P.wait("scalar", "sPE", mmv)
            if first_b1[0]:
                P.wait("scalar", "sA", CP["h2_done"])
                P.wait("scalar", "sV", CP["a3_evac"])
                P.wait("vector", "sS", CP["spill"])
                first_b1[0] = False
            if t_slot in gelu_free:
                P.wait("scalar", "sV", gelu_free[t_slot])
            e1 = P.op("scalar", lambda e, o=X, pp=ps: e.activation(
                o, pp, AF.Identity, bias=magic[:, :], scale=10.0), inc="sA")
            P.wait("vector", "sA", e1)
            v1 = P.op("vector", lambda e, o=X: e.tensor_scalar(
                o, o, MAGIC, QMAX, ALU.subtract, ALU.min), inc="sV")
            P.wait("vector", "sV", v1)
            v2 = P.op("vector", lambda e, o=X: e.tensor_scalar(
                o, o, QMIN, 0.1, ALU.max, ALU.mult), inc="sV")
            P.wait("scalar", "sV", v2)
            s2 = P.op("scalar", lambda e, o=U, i=X: e.activation(o, i, AF.Square),
                      inc="sA")
            P.wait("vector", "sA", s2)
            v3 = P.op("vector", lambda e, o=U, i=X: e.tensor_tensor(
                o, o, i, ALU.mult), inc="sV")
            P.wait("vector", "sV", v3)
            v4 = P.op("vector", lambda e, o=U, i=X: e.scalar_tensor_tensor(
                o, o, GC, i, ALU.mult, ALU.add), inc="sV")
            P.wait("scalar", "sV", v4)
            s3 = P.op("scalar", lambda e, o=U: e.activation(o, o, AF.Tanh, scale=GK),
                      inc="sA")
            P.wait("vector", "sA", s3)
            if m < 32:
                ao = RA_alo[:, m * 1024 : m * 1024 + 1024]
            else:
                ao = RB_sz[:, (m - 32) * 1024 : (m - 32) * 1024 + 1024]
            v5 = P.op("vector", lambda e, o=ao, u=U, i=X: e.scalar_tensor_tensor(
                o, u, 1.0, i, ALU.add, ALU.mult), inc="sV")
            gelu_free[t_slot] = v5
            a_ready[m] = v5
            return ("sA", e1)

        P.wait("scalar", "sV", v_const)
        stage(d_w1, C, KF, 0, h_rhs, lambda k, n: ("sA", hv2[k]), evac_b1)
        CP["b1_evac"] = P.cnt["sV"]

        def a_rhs(k, n):
            if k < 32:
                return RA_alo[:, k * 1024 + n * 512 : k * 1024 + n * 512 + 512]
            return RB_sz[:, (k - 32) * 1024 + n * 512 : (k - 32) * 1024 + n * 512 + 512]

        x2jit_v = {}
        x2jit_slot_free = {}
        first_x2jit = [True]

        def x2jit_dma(m):
            slot = m % 2
            if first_x2jit[0]:
                P.wait("sync", "sS", CP["spill"])
                P.wait("sync", "sV", CP["b1_evac"])
                first_x2jit[0] = False
            if slot in x2jit_slot_free:
                P.wait("sync", "sV", x2jit_slot_free[slot])
            dst = WM_jit[:, slot * 1024 : slot * 1024 + 1024]
            x2jit_v[m] = P.op("sync",
                              lambda e, s=d_x2[128 * m : 128 * m + 128, :], d=dst:
                              e.dma_start(out=d, in_=s),
                              inc="sX", n=16)

        x2jit_dma(0)
        x2jit_dma(1)

        def b2_post(m):
            if m + 2 < KC:
                x2jit_dma(m + 2)

        out_free = {}

        def evac_b2(m, ps, mmv):
            slot = m % 2
            P.wait("vector", "sPE", mmv)
            P.wait("vector", "sX", x2jit_v[m])
            if slot in out_free:
                P.wait("vector", "sS", out_free[slot])
            xj = WM_jit[:, slot * 1024 : slot * 1024 + 1024]
            o = WM_out[:, slot * 1024 : slot * 1024 + 1024]
            vv = P.op("vector",
                      lambda e, oo=o, pp=ps, gc=col(par, 48 + m), xx=xj:
                      e.scalar_tensor_tensor(oo, pp, gc, xx, ALU.mult, ALU.add),
                      inc="sV")
            x2jit_slot_free[slot] = vv
            P.wait("sync", "sV", vv)
            sv = P.op("sync",
                      lambda e, s=WM_out[:, slot * 1024 : slot * 1024 + 1024],
                      d=d_out[128 * m : 128 * m + 128, :]:
                      e.dma_start(out=d, in_=s),
                      inc="sS", n=16)
            out_free[slot] = sv
            return ("sV", vv)

        stage(d_w2, FF, KC, 0, a_rhs, lambda k, n: ("sV", a_ready[k]),
              evac_b2, posthook=b2_post, npieces=2)

        if debug:
            dump(d_dh2, RC_h[:, :], [("sV", CP["h2_done"])])
            dump(d_da[:, 0:32768], RA_alo[:, :], [("sV", CP["b1_evac"])])
            dump(d_da[:, 32768:65536], RB_sz[:, :], [])
            dump(d_dmodt, modt[:, :], [])
            dump(d_dpar, par[:, :], [])

        P.wait("sync", "sS", P.cnt["sS"])

        for engname in ENGS:
            fns = P.items[engname]
            getattr(block, engname)(lambda eng, fns=fns: [f(eng, SEM) for f in fns])

    return nc


_NC_CACHE = None


def _get_nc():
    global _NC_CACHE
    if _NC_CACHE is None:
        _NC_CACHE = build_program()
    return _NC_CACHE


def kernel(
    L0_attn_hidden_states,
    L0_attn_W_in,
    L0_attn_conv_w,
    L0_attn_W_x,
    L0_attn_W_dt,
    L0_attn_b_dt,
    L0_attn_D,
    L0_attn_W_out,
    L0_ffn_W1,
    L0_ffn_W2,
    L0_c,
    L0_wmod,
):
    f = lambda a: np.ascontiguousarray(np.asarray(a, dtype=np.float32))

    def tm(w):
        wT = np.asarray(w, dtype=np.float32).T
        K, M = wT.shape
        return np.ascontiguousarray(
            wT.reshape(K // 128, 128, M // 128, 128).transpose(2, 1, 0, 3)
            .reshape(M // 128, 128, K))

    xT = f(np.asarray(L0_attn_hidden_states, dtype=np.float32).reshape(8192, 2048).T)
    winT = tm(L0_attn_W_in)
    convT = tm(L0_attn_conv_w)
    woutT = tm(L0_attn_W_out)
    w1T = tm(L0_ffn_W1)
    w2T = tm(L0_ffn_W2)
    wmod = f(L0_wmod)
    PIECES_H = ([(256 * i, 256) for i in range(16)]
                + [(6144 + 256 * i, 256) for i in range(16)])
    PIECESF_H = ([(4096 + 128 * i, 128) for i in range(16)]
                 + [(10240 + 128 * i, 128) for i in range(16)])
    wmodB = np.stack([
        np.ascontiguousarray(
            wmod[:, j0:j0 + w].reshape(16, 128, w).transpose(1, 0, 2).reshape(128, 16 * w))
        for j0, w in PIECES_H])
    wmodF = np.stack([
        np.ascontiguousarray(
            wmod[:, j0:j0 + w].reshape(16, 128, w).transpose(1, 0, 2).reshape(128, 16 * w))
        for j0, w in PIECESF_H])
    cT = f(np.asarray(L0_c, dtype=np.float32).reshape(16, 128).T)
    dT = f(np.asarray(L0_attn_D, dtype=np.float32).reshape(32, 128).T)

    nc = _get_nc()
    in_maps = []
    for i in range(NCORES):
        in_maps.append({
            "xT": np.ascontiguousarray(xT[:, T * i : T * (i + 1)]),
            "winTM": winT,
            "convTM": convT,
            "woutTM": woutT,
            "w1TM": w1T,
            "w2TM": w2T,
            "wmodB": wmodB,
            "wmodF": wmodF,
            "cT": cT,
            "dT": dT,
        })
    res = run_bass_kernel_spmd(nc, in_maps, core_ids=list(range(NCORES)))
    out = np.concatenate([r["outT"] for r in res.results], axis=1)
    return np.ascontiguousarray(out.T).reshape(1, 8192, 2048)


# version 12
# speedup vs baseline: 1.1081x; 1.0081x over previous
from collections import deque

import numpy as np

import concourse.bass as bass
import concourse.mybir as mybir
from concourse.bass_utils import run_bass_kernel_spmd

F32 = mybir.dt.float32
BF16 = mybir.dt.bfloat16
U8 = mybir.dt.uint8
AF = mybir.ActivationFunctionType
ALU = mybir.AluOpType

NCORES = 8
C = 2048
INTER = 4096
FF = 8192
T = 1024
KC = C // 128
KI = INTER // 128
KF = FF // 128
NT = 2
MAGIC = 12582912.0
GK = 0.7978845608028654
GC = 0.044715
QMAX = 32767.0
QMIN = -32768.0
EPS = 1e-5
JMOD = 12288
NBLK = 24

ENGS = ("sync", "gpsimd", "tensor", "vector", "scalar")
SEMS = ("sW", "sX", "sS", "sPE", "sV", "sA", "sC")


class Sched:
    def __init__(self):
        self.items = {e: [] for e in ENGS}
        self.cnt = {s: 0 for s in SEMS}
        self.last_wait = {e: {s: 0 for s in SEMS} for e in ENGS}

    def wait(self, eng, sem, val):
        assert 0 <= val <= self.cnt[sem], (eng, sem, val, self.cnt[sem])
        if val <= self.last_wait[eng][sem]:
            return
        self.last_wait[eng][sem] = val
        self.items[eng].append(lambda e, S, sem=sem, val=val: e.wait_ge(S[sem], val))

    def op(self, eng, build, inc=None, n=1):
        if inc is None:
            self.items[eng].append(lambda e, S, b=build: b(e))
            return None
        self.cnt[inc] += n
        v = self.cnt[inc]
        self.items[eng].append(lambda e, S, b=build, s=inc, m=n: b(e).then_inc(S[s], m))
        return v


def place(nc, base_addr, name, shape, dtype, offset):
    h = nc.virtual_sbuf_tensor(name, shape, dtype)
    nc._tensor(h.name, h.shape, h.dtype, type="SB")
    mloc = nc.lookup_mloc(h)
    mloc.allocated = True
    mloc.addr = base_addr + offset
    return h


def build_program(debug=False):
    nc = bass.Bass("TRN2", target_bir_lowering=False)

    d_x = nc.dram_tensor("xT", [C, T], F32, kind="ExternalInput").ap()
    d_win = nc.dram_tensor("winTM", [FF // 128, 128, C], F32, kind="ExternalInput").ap()
    d_conv = nc.dram_tensor("convTM", [KI, 128, INTER], F32, kind="ExternalInput").ap()
    d_wout = nc.dram_tensor("woutTM", [KC, 128, INTER], F32, kind="ExternalInput").ap()
    d_w1 = nc.dram_tensor("w1TM", [KF, 128, C], F32, kind="ExternalInput").ap()
    d_w2 = nc.dram_tensor("w2TM", [KC, 128, FF], F32, kind="ExternalInput").ap()
    d_wmodB = nc.dram_tensor("wmodB", [32, 128, 4096], F32, kind="ExternalInput").ap()
    d_wmodF = nc.dram_tensor("wmodF", [32, 128, 2048], F32, kind="ExternalInput").ap()
    d_c = nc.dram_tensor("cT", [128, 16], F32, kind="ExternalInput").ap()
    d_D = nc.dram_tensor("dT", [128, 32], F32, kind="ExternalInput").ap()
    d_out = nc.dram_tensor("outT", [C, T], F32, kind="ExternalOutput").ap()
    d_x2 = nc.dram_tensor("x2spill", [C, T], F32,
                          kind="ExternalOutput" if debug else "Internal").ap()
    if debug:
        d_dh = nc.dram_tensor("dbg_h", [128, 16384], F32, kind="ExternalOutput").ap()
        d_dxh = nc.dram_tensor("dbg_xh", [128, 32768], F32, kind="ExternalOutput").ap()
        d_dsz = nc.dram_tensor("dbg_sz", [128, 32768], F32, kind="ExternalOutput").ap()
        d_dyz = nc.dram_tensor("dbg_yz", [128, 32768], F32, kind="ExternalOutput").ap()
        d_dh2 = nc.dram_tensor("dbg_h2", [128, 16384], F32, kind="ExternalOutput").ap()
        d_da = nc.dram_tensor("dbg_a", [128, 65536], F32, kind="ExternalOutput").ap()
        d_dmodt = nc.dram_tensor("dbg_modt", [128, 96], F32, kind="ExternalOutput").ap()
        d_dpar = nc.dram_tensor("dbg_par", [128, 80], F32, kind="ExternalOutput").ap()
        d_dbc = nc.dram_tensor("dbg_bc", [128, 2048], F32, kind="ExternalOutput").ap()
    d_mod = nc.dram_tensor("modscr", [96, 128], F32, kind="Internal").ap()

    P = Sched()
    CP = {}

    OFF = {}
    cur = 0

    def A(name, nbytes):
        nonlocal cur
        OFF[name] = cur
        cur += (nbytes + 31) // 32 * 32

    A("wbuf", 16384)
    A("rowbuf", 2048)
    A("rowbuf2", 2048)
    A("ones_row", 512)
    A("ones_col", 32)
    A("silu_cb", 32)
    A("cd", 64)
    A("modt", 384)
    A("par", 320)
    A("magic", 32)
    A("dt", 128)
    A("m96", 512)
    A("RA", 65536)
    A("RB", 65536)
    A("RC", 32768)
    A("WM", 24576)
    assert cur <= 211000, cur

    with (
        nc.sbuf_tensor("arena", [128, cur], U8) as arena,
        nc.psum_tensor("mm_ps", [128, 2048], F32) as mm_ps,
        nc.psum_tensor("bc_ps", [128, 1024], F32) as bc_ps,
        nc.psum_tensor("row_ps", [1, 512], F32) as row_ps,
        nc.psum_tensor("mod_ps", [1, 512], F32) as mod_ps,
        nc.semaphore("sW") as _sW,
        nc.semaphore("sX") as _sX,
        nc.semaphore("sS") as _sS,
        nc.semaphore("sPE") as _sPE,
        nc.semaphore("sV") as _sV,
        nc.semaphore("sA") as _sA,
        nc.semaphore("sC") as _sC,
        nc.Block() as block,
    ):
        SEM = {"sW": _sW, "sX": _sX, "sS": _sS, "sPE": _sPE, "sV": _sV,
               "sA": _sA, "sC": _sC}
        base = nc.lookup_mloc(arena).addr

        wbuf = place(nc, base, "v_wbuf", [128, 8192], BF16, OFF["wbuf"])
        rstd_bc = place(nc, base, "v_rstd_bc", [128, 1024], F32, OFF["RB"] + 28672)
        mrstd_bc = place(nc, base, "v_mrstd_bc", [128, 1024], F32, OFF["RB"] + 32768)
        rowb = place(nc, base, "v_rowb", [1, 512], F32, OFF["rowbuf"])
        rowb2 = place(nc, base, "v_rowb2", [1, 512], F32, OFF["rowbuf2"])
        ones_row = place(nc, base, "v_ones_row", [1, 128], F32, OFF["ones_row"])
        ones_col = place(nc, base, "v_ones_col", [128, 16], BF16, OFF["ones_col"])
        silu_cb = place(nc, base, "v_silu_cb", [128, 16], BF16, OFF["silu_cb"])
        cd = place(nc, base, "v_cd", [128, 16], F32, OFF["cd"])
        modt = place(nc, base, "v_modt", [128, 96], F32, OFF["modt"])
        par = place(nc, base, "v_par", [128, 80], F32, OFF["par"])
        magic = place(nc, base, "v_magic", [128, 1], F32, OFF["magic"])
        Dt = place(nc, base, "v_dt", [128, 32], F32, OFF["dt"])
        m96 = place(nc, base, "v_m96", [96, 128], F32, OFF["m96"])
        RA_x = place(nc, base, "v_RA_x", [128, 16384], F32, OFF["RA"])
        RA_xh = place(nc, base, "v_RA_xh", [128, 32768], BF16, OFF["RA"])
        RA_x2 = place(nc, base, "v_RA_x2", [128, 16384], F32, OFF["RA"])
        RA_alo = place(nc, base, "v_RA_alo", [128, 32768], BF16, OFF["RA"])
        RB_sz = place(nc, base, "v_RB_sz", [128, 32768], BF16, OFF["RB"])
        RB_sum = place(nc, base, "v_RB_sum", [1, 1024], F32, OFF["RB"])
        RB_sq = place(nc, base, "v_RB_sq", [1, 1024], F32, OFF["RB"] + 4096)
        RB_rstd = place(nc, base, "v_RB_rstd", [1, 1024], F32, OFF["RB"] + 8192)
        RB_mrstd = place(nc, base, "v_RB_mrstd", [1, 1024], F32, OFF["RB"] + 12288)
        RB_htmp = place(nc, base, "v_RB_htmp", [128, 2048], F32, OFF["RB"] + 16384)
        RB_xbf = place(nc, base, "v_RB_xbf", [128, 8192], BF16, OFF["RB"] + 36864)
        RC_h = place(nc, base, "v_RC_h", [128, 16384], BF16, OFF["RC"])
        WM_b = [place(nc, base, "v_WM_b%d" % s, [128, 4096], BF16,
                      OFF["WM"] + 8192 * s) for s in range(3)]
        WM_f = [place(nc, base, "v_WM_f%d" % s, [128, 2048], F32,
                      OFF["WM"] + 8192 * s) for s in range(3)]
        WM_g = place(nc, base, "v_WM_g", [128, 2048], F32, OFF["WM"])
        WM_st = place(nc, base, "v_WM_st", [128, 4096], BF16, OFF["WM"])
        WM_out = place(nc, base, "v_WM_out", [128, 2048], F32, OFF["WM"] + 8192)
        WM_jit = place(nc, base, "v_WM_jit", [128, 2048], F32, OFF["WM"] + 16384)

        def col(t, j):
            return t[:, j : j + 1]

        bank_free = [None] * 4
        unit_free = [None] * 4
        group_ctr = [0]
        unit_ctr = [0]

        def mm_bank_ap(b):
            return mm_ps[:, 512 * b : 512 * b + 512]

        def pe_wait(sem, cnt):
            P.wait("tensor", sem, cnt)

        def stage(wdram, Ktot, Mch, m_off, rhs_ap, rhs_ready, evac, hook=None,
                  posthook=None, npieces=1):
            Kch = Ktot // 128
            upg = Ktot // 2048
            assert upg in (1, 2, 4) and upg % npieces == 0
            upp = upg // npieces
            kpp = Kch // npieces
            ring_slots = 4 // upp
            total_pieces = Mch * npieces
            piece_info = []
            issued = [0]

            def issue_piece(p):
                u0 = unit_ctr[0] % 4
                for uu in range(u0, u0 + upp):
                    if unit_free[uu] is not None:
                        P.wait("gpsimd", *unit_free[uu])
                m = p // npieces
                pc = p % npieces
                src = wdram[m_off + m, :, pc * kpp * 128 : (pc + 1) * kpp * 128]
                dst = wbuf[:, u0 * 2048 : u0 * 2048 + kpp * 128]
                wv = P.op("gpsimd", lambda e, s=src, d=dst: e.dma_start(out=d, in_=s),
                          inc="sW", n=16)
                piece_info.append((u0, wv))
                unit_ctr[0] += upp

            def ensure(upto):
                while issued[0] < min(total_pieces, upto):
                    issue_piece(issued[0])
                    issued[0] += 1

            ensure(npieces + max(0, ring_slots - npieces))
            for m in range(Mch):
                if hook is not None:
                    hook(m)
                ensure((m + 1) * npieces + max(0, ring_slots - npieces))
                g = group_ctr[0]
                group_ctr[0] += 2
                b0, b1 = g % 4, (g + 1) % 4
                for b in (b0, b1):
                    if bank_free[b] is not None:
                        pe_wait(*bank_free[b])
                banks = (b0, b1)
                v_final = None
                for k in range(Kch):
                    pc = k // kpp
                    u0, wv = piece_info[m * npieces + pc]
                    pe_wait("sW", wv)
                    lhs = wbuf[:, u0 * 2048 + (k % kpp) * 128 :
                               u0 * 2048 + (k % kpp) * 128 + 128]
                    piece_end = (k + 1) % kpp == 0
                    for n in range(NT):
                        r = rhs_ready(k, n)
                        if r is not None:
                            pe_wait(*r)
                        is_inc = piece_end and n == NT - 1
                        v = P.op("tensor",
                                 lambda e, o=mm_bank_ap(banks[n]), l=lhs,
                                 r2=rhs_ap(k, n), st=(k == 0), sp=(k == Kch - 1):
                                 e.matmul(o, l, r2, start=st, stop=sp),
                                 inc="sPE" if is_inc else None)
                        if is_inc:
                            v_final = v
                    if piece_end:
                        for uu in range(u0, u0 + upp):
                            unit_free[uu] = ("sPE", v_final)
                assert b1 == b0 + 1
                f0 = evac(m, mm_ps[:, 512 * b0 : 512 * b0 + 1024], v_final)
                bank_free[b0] = f0
                bank_free[b1] = f0
                if posthook is not None:
                    posthook(m)

        P.op("vector", lambda e: e.memset(ones_col[:, :], 1.0))
        P.op("vector", lambda e: e.memset(ones_row[:, :], 1.0))
        v_const = P.op("vector", lambda e: e.memset(magic[:, :], MAGIC), inc="sV")

        x_c = P.op("sync", lambda e: e.dma_start(out=cd[:, :], in_=d_c), inc="sX", n=16)
        P.op("sync", lambda e: e.dma_start(out=Dt[:, :], in_=d_D), inc="sX", n=16)
        x_x = None
        for gq in range(4):
            src = d_x[512 * gq : 512 * gq + 512, :].rearrange("(kc p) t -> p kc t", p=128)
            dst = RA_x[:, 4096 * gq : 4096 * gq + 4096].rearrange("p (kc t) -> p kc t", t=T)
            x_x = P.op("sync", lambda e, s=src, d=dst: e.dma_start(out=d, in_=s),
                       inc="sX", n=16)

        P.wait("scalar", "sX", x_c)
        a_silu = P.op("scalar", lambda e: e.activation(cd[:, :], cd[:, :], AF.Silu),
                      inc="sA")
        P.wait("scalar", "sA", a_silu)
        a_siluc = P.op("scalar", lambda e: e.copy(silu_cb[:, :], cd[:, :]), inc="sA")

        def stats(srcf32, between=None):
            ev = None
            for n in range(NT):
                for sq in (0, 1):
                    av = None
                    for k in range(KC):
                        xin = srcf32[:, k * 1024 + n * 512 : k * 1024 + n * 512 + 512]
                        cv = RB_xbf[:, k * 512 : k * 512 + 512]
                        if sq == 0:
                            av = P.op("scalar", lambda e, o=cv, i=xin: e.copy(o, i),
                                      inc="sA" if k == KC - 1 else None)
                        else:
                            av = P.op("scalar",
                                      lambda e, o=cv, i=xin: e.activation(o, i, AF.Square),
                                      inc="sA" if k == KC - 1 else None)
                    if between is not None:
                        between()
                    P.wait("tensor", "sA", av)
                    pv = None
                    for k in range(KC):
                        cv = RB_xbf[:, k * 512 : k * 512 + 512]
                        pv = P.op("tensor",
                                  lambda e, r=cv, st=(k == 0), sp=(k == KC - 1):
                                  e.matmul(row_ps[:, :], ones_col[:, 0:1], r,
                                           start=st, stop=sp),
                                  inc="sPE" if k == KC - 1 else None)
                    P.wait("scalar", "sPE", pv)
                    P.wait("vector", "sPE", pv)
                    dst = (RB_sum if sq == 0 else RB_sq)[0:1, n * 512 : n * 512 + 512]
                    ev = P.op("vector", lambda e, d=dst: e.tensor_copy(d, row_ps[:, :]),
                              inc="sV")
                    P.wait("tensor", "sV", ev)
            return ev

        P.wait("scalar", "sX", x_x)

        PIECES = []
        for i in range(16):
            PIECES.append((256 * i, 256, "b"))
        for i in range(16):
            PIECES.append((6144 + 256 * i, 256, "b"))
        for i in range(16):
            PIECES.append((4096 + 128 * i, 128, "f"))
        for i in range(16):
            PIECES.append((10240 + 128 * i, 128, "f"))

        PIECE_SUB = []
        nb = nf = 0
        for (_j0, _w, _k) in PIECES:
            if _k == "b":
                PIECE_SUB.append(nb); nb += 1
            else:
                PIECE_SUB.append(nf); nf += 1

        mod_state = {"mm": [0, 0, 0], "store": [0, 0], "idx": 0, "last_mm": 0,
                     "dma_idx": 0, "dma_v": {}}

        def mod_piece_dma():
            i = mod_state["dma_idx"]
            if i >= len(PIECES):
                return
            mod_state["dma_idx"] += 1
            j0, wdt, kind = PIECES[i]
            s = i % 3
            if mod_state["mm"][s]:
                P.wait("gpsimd", "sPE", mod_state["mm"][s])
            if kind == "b":
                srcw = d_wmodB[PIECE_SUB[i], :, :]
                slot = WM_b[s][:, :]
            else:
                srcw = d_wmodF[PIECE_SUB[i], :, :]
                slot = WM_f[s][:, :]
            mod_state["dma_v"][i] = P.op(
                "gpsimd", lambda e, s_=srcw, d_=slot: e.dma_start(out=d_, in_=s_),
                inc="sW", n=16)

        def mod_piece():
            if mod_state["dma_idx"] <= mod_state["idx"]:
                mod_piece_dma()
            i = mod_state["idx"]
            mod_state["idx"] += 1
            j0, wdt, kind = PIECES[i]
            s = i % 3
            wv = mod_state["dma_v"][i]
            pe_wait("sW", wv)
            pe_wait("sA", a_siluc)
            pv = None
            for kc in range(KC):
                if kind == "b":
                    lhs = silu_cb[:, kc : kc + 1]
                    rhs = WM_b[s][:, kc * wdt : kc * wdt + wdt]
                else:
                    lhs = cd[:, kc : kc + 1]
                    rhs = WM_f[s][:, kc * wdt : kc * wdt + wdt]
                pv = P.op("tensor",
                          lambda e, l=lhs, r=rhs, st=(kc == 0), sp=(kc == KC - 1),
                          w_=wdt: e.matmul(mod_ps[0:1, 0:w_], l, r, start=st, stop=sp),
                          inc="sPE" if kc == KC - 1 else None)
            mod_state["mm"][s] = pv
            mod_state["last_mm"] = pv
            rs = i % 2
            rb = rowb if rs == 0 else rowb2
            if mod_state["store"][rs]:
                P.wait("scalar", "sS", mod_state["store"][rs])
            P.wait("scalar", "sPE", pv)
            ev = P.op("scalar", lambda e, r=rb, w_=wdt: e.copy(
                r[0:1, 0:w_], mod_ps[0:1, 0:w_]), inc="sA")
            P.wait("tensor", "sA", ev)
            P.wait("sync", "sA", ev)
            J0 = j0 // 128
            nrow = wdt // 128
            sv = P.op("sync",
                      lambda e, r=rb[0:1, 0:wdt].rearrange("p (a c) -> p a c", c=128),
                      d=d_mod[J0 : J0 + nrow, :].rearrange("a c -> () a c"):
                      e.dma_start(out=d, in_=r),
                      inc="sS", n=16)
            mod_state["store"][rs] = sv

        mod_piece_dma()
        mod_piece_dma()

        def p0_between():
            for _ in range(4):
                mod_piece_dma()
                mod_piece()

        stats1 = stats(RA_x, between=p0_between)
        CP["saca_store"] = P.cnt["sS"]



        def row_math():
            v1 = P.op("vector", lambda e: e.tensor_scalar(
                RB_sum[:, :], RB_sum[:, :], 1.0 / 2048.0, None, ALU.mult), inc="sV")
            v2 = P.op("vector", lambda e: e.tensor_scalar(
                RB_sq[:, :], RB_sq[:, :], 1.0 / 2048.0, EPS, ALU.mult, ALU.add),
                inc="sV")
            P.wait("vector", "sV", v1)
            v3 = P.op("vector", lambda e: e.tensor_tensor(
                RB_rstd[:, :], RB_sum[:, :], RB_sum[:, :], ALU.mult), inc="sV")
            P.wait("vector", "sV", max(v2, v3))
            v4 = P.op("vector", lambda e: e.tensor_tensor(
                RB_sq[:, :], RB_sq[:, :], RB_rstd[:, :], ALU.subtract), inc="sV")
            P.wait("scalar", "sV", v4)
            a1 = P.op("scalar", lambda e: e.activation(
                RB_rstd[:, :], RB_sq[:, :], AF.Ln), inc="sA")
            P.wait("scalar", "sA", a1)
            a2 = P.op("scalar", lambda e: e.activation(
                RB_rstd[:, :], RB_rstd[:, :], AF.Exp, scale=-0.5), inc="sA")
            P.wait("vector", "sA", a2)
            v5 = P.op("vector", lambda e: e.tensor_tensor(
                RB_mrstd[:, :], RB_sum[:, :], RB_rstd[:, :], ALU.mult), inc="sV")
            return v5

        P.wait("vector", "sV", stats1)
        rows1 = row_math()

        def bcast(src_row, dst_bc, ready_v):
            pe_wait("sV", ready_v)
            pv = None
            for n in range(NT):
                pv = P.op("tensor",
                          lambda e, r=src_row[0:1, n * 512 : n * 512 + 512],
                          o=bc_ps[:, n * 512 : n * 512 + 512]:
                          e.matmul(o, ones_row[0:1, :], r, start=True, stop=True),
                          inc="sPE")
            P.wait("vector", "sPE", pv)
            ev = P.op("vector", lambda e, d=dst_bc: e.tensor_copy(d[:, :], bc_ps[:, :]),
                      inc="sV")
            P.wait("tensor", "sV", ev)
            return ev

        bc1 = bcast(RB_rstd, rstd_bc, rows1)
        bc2 = bcast(RB_mrstd, mrstd_bc, rows1)

        def make_ln(srcf32):
            lv = []
            for k in range(KC):
                ht = RB_htmp[:, (k % 2) * 1024 : (k % 2) * 1024 + 1024]
                xin = srcf32[:, k * 1024 : k * 1024 + 1024]
                t1 = P.op("vector", lambda e, o=ht, i=xin: e.tensor_tensor(
                    o, i, rstd_bc[:, :], ALU.mult), inc="sV")
                P.wait("vector", "sV", t1)
                t2 = P.op("vector",
                          lambda e, o=RC_h[:, k * 1024 : k * 1024 + 1024], i=ht:
                          e.tensor_tensor(o, i, mrstd_bc[:, :], ALU.subtract),
                          inc="sV")
                lv.append(t2)
            return lv

        def make_h(lv, shift_col, scale_col):
            hv = []
            for k in range(KC):
                P.wait("scalar", "sV", lv[k])
                t3 = P.op("scalar",
                          lambda e, o=RC_h[:, k * 1024 : k * 1024 + 1024],
                          s1=scale_col(k), s2=shift_col(k):
                          e.activation(o, o, AF.Identity, bias=s2, scale=s1),
                          inc="sA")
                hv.append(t3)
            return hv

        P.wait("vector", "sV", max(bc1, bc2))
        lv1 = make_ln(RA_x)
        P.wait("sync", "sS", CP["saca_store"])
        r1 = P.op("sync", lambda e: e.dma_start(out=m96[0:32, :], in_=d_mod[0:32, :]),
                  inc="sX", n=16)
        P.wait("vector", "sX", r1)
        tv = None
        for pb in range(4):
            tv = P.op("vector",
                      lambda e, o=modt[32 * pb : 32 * pb + 32, 0:32],
                      i=m96[0:32, 32 * pb : 32 * pb + 32]: e.transpose(o, i),
                      inc="sV")
        P.wait("vector", "sV", tv)
        v_cap1 = P.op("vector", lambda e: e.tensor_scalar(
            par[:, 0:16], modt[:, 16:32], 1.0, None, ALU.add), inc="sV")
        P.wait("scalar", "sV", v_cap1)
        hv1 = make_h(lv1, lambda k: col(modt, k), lambda k: col(par, k))
        CP["h1_done"] = P.cnt["sA"]

        def h_rhs(k, n):
            return RC_h[:, k * 1024 + n * 512 : k * 1024 + n * 512 + 512]

        first_a1x = [True]

        def evac_a1x(m, ps, mmv):
            P.wait("scalar", "sPE", mmv)
            if first_a1x[0]:
                P.wait("scalar", "sA", CP["h1_done"])
                first_a1x[0] = False
            o = RA_xh[:, m * 1024 : m * 1024 + 1024]
            av = P.op("scalar", lambda e, oo=o, pp=ps: e.copy(oo, pp), inc="sA")
            return ("sA", av)

        def a1_hook(m):
            if m % 2 == 0:
                mod_piece_dma()
                if mod_state["idx"] < mod_state["dma_idx"] - 1 or \
                   mod_state["dma_idx"] == len(PIECES):
                    if mod_state["idx"] < len(PIECES):
                        mod_piece()

        stage(d_win, C, 32, 0, h_rhs, lambda k, n: ("sA", hv1[k]), evac_a1x,
              hook=a1_hook)
        CP["a1x_evac"] = P.cnt["sA"]

        def evac_a1z(m, ps, mmv):
            P.wait("scalar", "sPE", mmv)
            o = RB_sz[:, m * 1024 : m * 1024 + 1024]
            av = P.op("scalar", lambda e, oo=o, pp=ps: e.activation(oo, pp, AF.Silu),
                      inc="sA")
            return ("sA", av)

        stage(d_win, C, 32, 32, h_rhs, lambda k, n: ("sA", hv1[k]), evac_a1z,
              hook=a1_hook)
        CP["a1z_evac"] = P.cnt["sA"]


        dbg_gate = {}

        def dump(dram, sb_ap, gates):
            for sem, cnt in gates:
                P.wait("gpsimd", sem, cnt)
            return P.op("gpsimd", lambda e, d=dram, s=sb_ap: e.dma_start(out=d, in_=s),
                        inc="sS", n=16)

        if debug:
            dump(d_dh, RC_h[:, :], [("sV", CP["h1_done"])])
            dump(d_dxh, RA_xh[:, :], [("sA", CP["a1x_evac"])])
            dbg_gate["sz"] = dump(d_dsz, RB_sz[:, :], [("sA", CP["a1z_evac"])])

        def xh_rhs(k, n):
            return RA_xh[:, k * 1024 + n * 512 : k * 1024 + n * 512 + 512]

        first_a2 = [True]

        def evac_a2(m, ps, mmv):
            P.wait("vector", "sPE", mmv)
            if first_a2[0]:
                P.wait("vector", "sA", CP["a1z_evac"])
                if debug:
                    P.wait("vector", "sS", dbg_gate["sz"])
                first_a2[0] = False
            o = RB_sz[:, m * 1024 : m * 1024 + 1024]
            vv = P.op("vector",
                      lambda e, oo=o, pp=ps, dc=col(Dt, m):
                      e.scalar_tensor_tensor(oo, pp, dc, oo, ALU.mult, ALU.mult),
                      inc="sV")
            return ("sV", vv)

        def a2_hook(m):
            if m % 2 == 0:
                mod_piece_dma()
                if mod_state["idx"] < len(PIECES):
                    mod_piece()

        stage(d_conv, INTER, KI, 0, xh_rhs, lambda k, n: ("sA", CP["a1x_evac"]),
              evac_a2, hook=a2_hook)
        CP["a2_evac"] = P.cnt["sV"]

        P.wait("sync", "sS", P.cnt["sS"])
        r2 = P.op("sync", lambda e: e.dma_start(out=m96[32:96, :], in_=d_mod[32:96, :]),
                  inc="sX", n=16)
        P.wait("vector", "sX", r2)
        tv2 = None
        for pb in range(4):
            for jb in (1, 2):
                tv2 = P.op("vector",
                           lambda e, o=modt[32 * pb : 32 * pb + 32,
                                            32 * jb : 32 * jb + 32],
                           i=m96[32 * jb : 32 * jb + 32, 32 * pb : 32 * pb + 32]:
                           e.transpose(o, i),
                           inc="sV")
        P.wait("vector", "sV", tv2)
        P.op("vector", lambda e: e.tensor_scalar(
            par[:, 32:48], modt[:, 64:80], 1.0, None, ALU.add), inc="sV")
        P.wait("scalar", "sV", tv2)
        sg1 = P.op("scalar", lambda e: e.activation(
            par[:, 64:80], modt[:, 32:48], AF.Sign), inc="sA")
        ab1 = P.op("scalar", lambda e: e.activation(
            par[:, 16:32], modt[:, 32:48], AF.Abs), inc="sA")
        P.wait("vector", "sA", ab1)
        am1 = P.op("vector", lambda e: e.tensor_scalar(
            par[:, 16:32], par[:, 16:32], 0.25, None, ALU.max), inc="sV")
        P.wait("vector", "sV", am1)
        ga_v = P.op("vector", lambda e: e.tensor_tensor(
            par[:, 16:32], par[:, 16:32], par[:, 64:80], ALU.mult), inc="sV")
        P.wait("scalar", "sV", ga_v)
        sg2 = P.op("scalar", lambda e: e.activation(
            par[:, 64:80], modt[:, 80:96], AF.Sign), inc="sA")
        ab2 = P.op("scalar", lambda e: e.activation(
            par[:, 48:64], modt[:, 80:96], AF.Abs), inc="sA")
        P.wait("vector", "sA", ab2)
        am2 = P.op("vector", lambda e: e.tensor_scalar(
            par[:, 48:64], par[:, 48:64], 0.25, 0.5, ALU.max, ALU.mult), inc="sV")
        P.wait("vector", "sV", am2)
        g2 = P.op("vector", lambda e: e.tensor_tensor(
            par[:, 48:64], par[:, 48:64], par[:, 64:80], ALU.mult), inc="sV")
        CP["params2"] = P.cnt["sV"]

        def yz_rhs(k, n):
            return RB_sz[:, k * 1024 + n * 512 : k * 1024 + n * 512 + 512]

        xjit_v = {}
        xjit_slot_free = {}

        def xjit_dma(m):
            slot = m % 2
            if slot in xjit_slot_free:
                P.wait("sync", "sV", xjit_slot_free[slot])
            P.wait("sync", "sPE", mod_state["last_mm"])
            dst = WM_jit[:, slot * 1024 : slot * 1024 + 1024]
            xjit_v[m] = P.op("sync",
                             lambda e, s=d_x[128 * m : 128 * m + 128, :], d=dst:
                             e.dma_start(out=d, in_=s),
                             inc="sX", n=16)

        xjit_dma(0)
        xjit_dma(1)

        st_rows = [row_ps[0:1, 0:512], mod_ps[0:1, 0:512],
                   bc_ps[0:1, 0:512], bc_ps[0:1, 512:1024]]
        st_cv = [[WM_st[:, 0:512], WM_st[:, 512:1024]],
                 [WM_st[:, 1024:1536], WM_st[:, 1536:2048]]]
        st_sq = [[WM_st[:, 2048:2560], WM_st[:, 2560:3072]],
                 [WM_st[:, 3072:3584], WM_st[:, 3584:4096]]]
        st_state = {"pv": [0, 0, 0, 0], "cv": {}}

        def a3_stats_convs(m):
            P.wait("scalar", "sV", a3_x2v[m])
            if m >= 2:
                P.wait("scalar", "sPE", max(st_state["pv"]))
            last = None
            for n in range(NT):
                xin = RA_x2[:, m * 1024 + n * 512 : m * 1024 + n * 512 + 512]
                P.op("scalar", lambda e, o=st_cv[m % 2][n], i=xin: e.copy(o, i))
                last = P.op("scalar",
                            lambda e, o=st_sq[m % 2][n], i=xin:
                            e.activation(o, i, AF.Square),
                            inc="sA" if n == NT - 1 else None)
            st_state["cv"][m] = last

        def a3_stats_mms(m):
            P.wait("tensor", "sA", st_state["cv"][m])
            for n in range(NT):
                st_state["pv"][n] = P.op(
                    "tensor", lambda e, r=st_cv[m % 2][n], o=st_rows[n],
                    st=(m == 0), sp=(m == KC - 1):
                    e.matmul(o, ones_col[:, 0:1], r, start=st, stop=sp),
                    inc="sPE")
            for n in range(NT):
                st_state["pv"][2 + n] = P.op(
                    "tensor", lambda e, r=st_sq[m % 2][n], o=st_rows[2 + n],
                    st=(m == 0), sp=(m == KC - 1):
                    e.matmul(o, ones_col[:, 0:1], r, start=st, stop=sp),
                    inc="sPE")

        def a3_post(m):
            if m + 2 < KC:
                xjit_dma(m + 2)
            a3_stats_convs(m)
            if m >= 1:
                a3_stats_mms(m - 1)
            if m == KC - 1:
                a3_stats_mms(m)

        a3_x2v = {}

        def evac_a3(m, ps, mmv):
            P.wait("vector", "sPE", mmv)
            P.wait("vector", "sX", xjit_v[m])
            P.wait("vector", "sV", CP["params2"])
            slot = m % 2
            xj = WM_jit[:, slot * 1024 : slot * 1024 + 1024]
            o = RA_x2[:, m * 1024 : m * 1024 + 1024]
            vv = P.op("vector",
                      lambda e, oo=o, pp=ps, gc=col(par, 16 + m), xx=xj:
                      e.scalar_tensor_tensor(oo, pp, gc, xx, ALU.mult, ALU.add),
                      inc="sV")
            xjit_slot_free[slot] = vv
            a3_x2v[m] = vv
            return ("sV", vv)

        stage(d_wout, INTER, KC, 0, yz_rhs, lambda k, n: ("sV", CP["a2_evac"]),
              evac_a3, posthook=a3_post)
        CP["a3_evac"] = P.cnt["sV"]
        CP["a3_mm"] = P.cnt["sPE"]

        if debug:
            dbg_gate["yz"] = dump(d_dyz, RB_sz[:, :],
                                  [("sV", CP["a2_evac"]), ("sPE", CP["a3_mm"])])
            dump(d_dbc[:, 0:1024], rstd_bc[:, :], [("sV", CP["h1_done"])])
            dump(d_dbc[:, 1024:2048], mrstd_bc[:, :], [])

        if debug:
            P.wait("scalar", "sS", dbg_gate["yz"])
            P.wait("vector", "sS", dbg_gate["yz"])
        P.wait("vector", "sPE", max(st_state["pv"]))
        for nn in range(NT):
            P.op("vector", lambda e, d=RB_sum[0:1, nn * 512 : nn * 512 + 512],
                 s=st_rows[nn]: e.tensor_copy(d, s), inc="sV")
        for nn in range(NT):
            P.op("vector", lambda e, d=RB_sq[0:1, nn * 512 : nn * 512 + 512],
                 s=st_rows[2 + nn]: e.tensor_copy(d, s), inc="sV")
        stats2 = P.cnt["sV"]
        P.wait("tensor", "sV", stats2)
        P.wait("vector", "sV", stats2)
        rows2 = row_math()
        bc3 = bcast(RB_rstd, rstd_bc, rows2)
        bc4 = bcast(RB_mrstd, mrstd_bc, rows2)
        P.wait("vector", "sV", max(bc3, bc4))
        lv2 = make_ln(RA_x2)
        P.wait("scalar", "sV", CP["params2"])
        hv2 = make_h(lv2, lambda k: col(modt, 48 + k), lambda k: col(par, 32 + k))
        CP["h2_done"] = P.cnt["sA"]

        spill_v = None
        for gq in range(4):
            P.wait("sync", "sV", a3_x2v[4 * gq + 3])
            src = RA_x2[:, 4096 * gq : 4096 * gq + 4096].rearrange(
                "p (kc t) -> p kc t", t=T)
            dst = d_x2[512 * gq : 512 * gq + 512, :].rearrange(
                "(kc p) t -> p kc t", p=128)
            spill_v = P.op("sync", lambda e, s=src, d=dst: e.dma_start(out=d, in_=s),
                           inc="sS", n=16)
        CP["spill"] = spill_v

        gelu_free = {}
        first_b1 = [True]
        a_ready = {}
        GX = [WM_g[:, 0:1024], WM_jit[:, 0:1024]]
        GU = [WM_g[:, 1024:2048], WM_jit[:, 1024:2048]]

        def evac_b1(m, ps, mmv):
            t_slot = m % 2
            X = GX[t_slot]
            U = GU[t_slot]
            P.wait("scalar", "sPE", mmv)
            if first_b1[0]:
                P.wait("scalar", "sA", CP["h2_done"])
                P.wait("scalar", "sV", CP["a3_evac"])
                P.wait("vector", "sS", CP["spill"])
                first_b1[0] = False
            if t_slot in gelu_free:
                P.wait("scalar", "sV", gelu_free[t_slot])
            e1 = P.op("scalar", lambda e, o=X, pp=ps: e.activation(
                o, pp, AF.Identity, bias=magic[:, :], scale=10.0), inc="sA")
            P.wait("vector", "sA", e1)
            v1 = P.op("vector", lambda e, o=X: e.tensor_scalar(
                o, o, MAGIC, QMAX, ALU.subtract, ALU.min), inc="sV")
            P.wait("vector", "sV", v1)
            v2 = P.op("vector", lambda e, o=X: e.tensor_scalar(
                o, o, QMIN, 0.1, ALU.max, ALU.mult), inc="sV")
            P.wait("scalar", "sV", v2)
            s2 = P.op("scalar", lambda e, o=U, i=X: e.activation(o, i, AF.Square),
                      inc="sA")
            P.wait("vector", "sA", s2)
            v3 = P.op("vector", lambda e, o=U, i=X: e.tensor_tensor(
                o, o, i, ALU.mult), inc="sV")
            P.wait("vector", "sV", v3)
            v4 = P.op("vector", lambda e, o=U, i=X: e.scalar_tensor_tensor(
                o, o, GC, i, ALU.mult, ALU.add), inc="sV")
            P.wait("scalar", "sV", v4)
            s3 = P.op("scalar", lambda e, o=U: e.activation(o, o, AF.Tanh, scale=GK),
                      inc="sA")
            P.wait("vector", "sA", s3)
            if m < 32:
                ao = RA_alo[:, m * 1024 : m * 1024 + 1024]
            else:
                ao = RB_sz[:, (m - 32) * 1024 : (m - 32) * 1024 + 1024]
            v5 = P.op("vector", lambda e, o=ao, u=U, i=X: e.scalar_tensor_tensor(
                o, u, 1.0, i, ALU.add, ALU.mult), inc="sV")
            gelu_free[t_slot] = v5
            a_ready[m] = v5
            return ("sA", e1)

        P.wait("scalar", "sV", v_const)
        stage(d_w1, C, KF, 0, h_rhs, lambda k, n: ("sA", hv2[k]), evac_b1)
        CP["b1_evac"] = P.cnt["sV"]

        def a_rhs(k, n):
            if k < 32:
                return RA_alo[:, k * 1024 + n * 512 : k * 1024 + n * 512 + 512]
            return RB_sz[:, (k - 32) * 1024 + n * 512 : (k - 32) * 1024 + n * 512 + 512]

        x2jit_v = {}
        x2jit_slot_free = {}
        first_x2jit = [True]

        def x2jit_dma(m):
            slot = m % 2
            if first_x2jit[0]:
                P.wait("sync", "sS", CP["spill"])
                P.wait("sync", "sV", CP["b1_evac"])
                first_x2jit[0] = False
            if slot in x2jit_slot_free:
                P.wait("sync", "sV", x2jit_slot_free[slot])
            dst = WM_jit[:, slot * 1024 : slot * 1024 + 1024]
            x2jit_v[m] = P.op("sync",
                              lambda e, s=d_x2[128 * m : 128 * m + 128, :], d=dst:
                              e.dma_start(out=d, in_=s),
                              inc="sX", n=16)

        x2jit_dma(0)
        x2jit_dma(1)

        def b2_post(m):
            if m + 2 < KC:
                x2jit_dma(m + 2)

        out_free = {}

        def evac_b2(m, ps, mmv):
            slot = m % 2
            P.wait("vector", "sPE", mmv)
            P.wait("vector", "sX", x2jit_v[m])
            if slot in out_free:
                P.wait("vector", "sS", out_free[slot])
            xj = WM_jit[:, slot * 1024 : slot * 1024 + 1024]
            o = WM_out[:, slot * 1024 : slot * 1024 + 1024]
            vv = P.op("vector",
                      lambda e, oo=o, pp=ps, gc=col(par, 48 + m), xx=xj:
                      e.scalar_tensor_tensor(oo, pp, gc, xx, ALU.mult, ALU.add),
                      inc="sV")
            x2jit_slot_free[slot] = vv
            P.wait("sync", "sV", vv)
            sv = P.op("sync",
                      lambda e, s=WM_out[:, slot * 1024 : slot * 1024 + 1024],
                      d=d_out[128 * m : 128 * m + 128, :]:
                      e.dma_start(out=d, in_=s),
                      inc="sS", n=16)
            out_free[slot] = sv
            return ("sV", vv)

        stage(d_w2, FF, KC, 0, a_rhs, lambda k, n: ("sV", a_ready[k]),
              evac_b2, posthook=b2_post, npieces=2)

        if debug:
            dump(d_dh2, RC_h[:, :], [("sV", CP["h2_done"])])
            dump(d_da[:, 0:32768], RA_alo[:, :], [("sV", CP["b1_evac"])])
            dump(d_da[:, 32768:65536], RB_sz[:, :], [])
            dump(d_dmodt, modt[:, :], [])
            dump(d_dpar, par[:, :], [])

        P.wait("sync", "sS", P.cnt["sS"])

        for engname in ENGS:
            fns = P.items[engname]
            getattr(block, engname)(lambda eng, fns=fns: [f(eng, SEM) for f in fns])

    return nc


_NC_CACHE = None


def _get_nc():
    global _NC_CACHE
    if _NC_CACHE is None:
        _NC_CACHE = build_program()
    return _NC_CACHE


def kernel(
    L0_attn_hidden_states,
    L0_attn_W_in,
    L0_attn_conv_w,
    L0_attn_W_x,
    L0_attn_W_dt,
    L0_attn_b_dt,
    L0_attn_D,
    L0_attn_W_out,
    L0_ffn_W1,
    L0_ffn_W2,
    L0_c,
    L0_wmod,
):
    f = lambda a: np.ascontiguousarray(np.asarray(a, dtype=np.float32))

    def tm(w):
        wT = np.asarray(w, dtype=np.float32).T
        K, M = wT.shape
        return np.ascontiguousarray(
            wT.reshape(K // 128, 128, M // 128, 128).transpose(2, 1, 0, 3)
            .reshape(M // 128, 128, K))

    xT = f(np.asarray(L0_attn_hidden_states, dtype=np.float32).reshape(8192, 2048).T)
    winT = tm(L0_attn_W_in)
    convT = tm(L0_attn_conv_w)
    woutT = tm(L0_attn_W_out)
    w1T = tm(L0_ffn_W1)
    w2T = tm(L0_ffn_W2)
    wmod = f(L0_wmod)
    PIECES_H = ([(256 * i, 256) for i in range(16)]
                + [(6144 + 256 * i, 256) for i in range(16)])
    PIECESF_H = ([(4096 + 128 * i, 128) for i in range(16)]
                 + [(10240 + 128 * i, 128) for i in range(16)])
    wmodB = np.stack([
        np.ascontiguousarray(
            wmod[:, j0:j0 + w].reshape(16, 128, w).transpose(1, 0, 2).reshape(128, 16 * w))
        for j0, w in PIECES_H])
    wmodF = np.stack([
        np.ascontiguousarray(
            wmod[:, j0:j0 + w].reshape(16, 128, w).transpose(1, 0, 2).reshape(128, 16 * w))
        for j0, w in PIECESF_H])
    cT = f(np.asarray(L0_c, dtype=np.float32).reshape(16, 128).T)
    dT = f(np.asarray(L0_attn_D, dtype=np.float32).reshape(32, 128).T)

    nc = _get_nc()
    in_maps = []
    for i in range(NCORES):
        in_maps.append({
            "xT": np.ascontiguousarray(xT[:, T * i : T * (i + 1)]),
            "winTM": winT,
            "convTM": convT,
            "woutTM": woutT,
            "w1TM": w1T,
            "w2TM": w2T,
            "wmodB": wmodB,
            "wmodF": wmodF,
            "cT": cT,
            "dT": dT,
        })
    res = run_bass_kernel_spmd(nc, in_maps, core_ids=list(range(NCORES)))
    out = np.concatenate([r["outT"] for r in res.results], axis=1)
    return np.ascontiguousarray(out.T).reshape(1, 8192, 2048)


# version 13
# speedup vs baseline: 1.1164x; 1.0075x over previous
from collections import deque

import numpy as np

import concourse.bass as bass
import concourse.mybir as mybir
from concourse.bass_utils import run_bass_kernel_spmd

F32 = mybir.dt.float32
BF16 = mybir.dt.bfloat16
U8 = mybir.dt.uint8
AF = mybir.ActivationFunctionType
ALU = mybir.AluOpType

NCORES = 8
C = 2048
INTER = 4096
FF = 8192
T = 1024
KC = C // 128
KI = INTER // 128
KF = FF // 128
NT = 2
MAGIC = 12582912.0
GK = 0.7978845608028654
GC = 0.044715
QMAX = 32767.0
QMIN = -32768.0
EPS = 1e-5
JMOD = 12288
NBLK = 24

ENGS = ("sync", "gpsimd", "tensor", "vector", "scalar")
SEMS = ("sW", "sX", "sS", "sPE", "sV", "sA", "sC")


class Sched:
    def __init__(self):
        from collections import defaultdict
        self.items = {e: [] for e in ENGS}
        self.cnt = defaultdict(int)
        self.last_wait = {e: defaultdict(int) for e in ENGS}

    def wait(self, eng, sem, val):
        assert 0 <= val <= self.cnt[sem], (eng, sem, val, self.cnt[sem])
        if val <= self.last_wait[eng][sem]:
            return
        self.last_wait[eng][sem] = val
        self.items[eng].append(lambda e, S, sem=sem, val=val: e.wait_ge(S[sem], val))

    def op(self, eng, build, inc=None, n=1):
        if inc is None:
            self.items[eng].append(lambda e, S, b=build: b(e))
            return None
        self.cnt[inc] += n
        v = self.cnt[inc]
        self.items[eng].append(lambda e, S, b=build, s=inc, m=n: b(e).then_inc(S[s], m))
        return v


def place(nc, base_addr, name, shape, dtype, offset):
    h = nc.virtual_sbuf_tensor(name, shape, dtype)
    nc._tensor(h.name, h.shape, h.dtype, type="SB")
    mloc = nc.lookup_mloc(h)
    mloc.allocated = True
    mloc.addr = base_addr + offset
    return h


def build_program(debug=False):
    nc = bass.Bass("TRN2", target_bir_lowering=False)

    d_x = nc.dram_tensor("xT", [C, T], F32, kind="ExternalInput").ap()
    d_win = nc.dram_tensor("winTM", [FF // 128, 128, C], F32, kind="ExternalInput").ap()
    d_conv = nc.dram_tensor("convTM", [KI, 128, INTER], F32, kind="ExternalInput").ap()
    d_wout = nc.dram_tensor("woutTM", [KC, 128, INTER], F32, kind="ExternalInput").ap()
    d_w1 = nc.dram_tensor("w1TM", [KF, 128, C], F32, kind="ExternalInput").ap()
    d_w2 = nc.dram_tensor("w2TM", [KC, 128, FF], F32, kind="ExternalInput").ap()
    d_wmodB = nc.dram_tensor("wmodB", [32, 128, 4096], F32, kind="ExternalInput").ap()
    d_wmodF = nc.dram_tensor("wmodF", [32, 128, 2048], F32, kind="ExternalInput").ap()
    d_c = nc.dram_tensor("cT", [128, 16], F32, kind="ExternalInput").ap()
    d_D = nc.dram_tensor("dT", [128, 32], F32, kind="ExternalInput").ap()
    d_out = nc.dram_tensor("outT", [C, T], F32, kind="ExternalOutput").ap()
    d_x2 = nc.dram_tensor("x2spill", [C, T], F32,
                          kind="ExternalOutput" if debug else "Internal").ap()
    if debug:
        d_dh = nc.dram_tensor("dbg_h", [128, 16384], F32, kind="ExternalOutput").ap()
        d_dxh = nc.dram_tensor("dbg_xh", [128, 32768], F32, kind="ExternalOutput").ap()
        d_dsz = nc.dram_tensor("dbg_sz", [128, 32768], F32, kind="ExternalOutput").ap()
        d_dyz = nc.dram_tensor("dbg_yz", [128, 32768], F32, kind="ExternalOutput").ap()
        d_dh2 = nc.dram_tensor("dbg_h2", [128, 16384], F32, kind="ExternalOutput").ap()
        d_da = nc.dram_tensor("dbg_a", [128, 65536], F32, kind="ExternalOutput").ap()
        d_dmodt = nc.dram_tensor("dbg_modt", [128, 96], F32, kind="ExternalOutput").ap()
        d_dpar = nc.dram_tensor("dbg_par", [128, 80], F32, kind="ExternalOutput").ap()
        d_dbc = nc.dram_tensor("dbg_bc", [128, 2048], F32, kind="ExternalOutput").ap()
    d_mod = nc.dram_tensor("modscr", [96, 128], F32, kind="Internal").ap()

    P = Sched()
    CP = {}

    OFF = {}
    cur = 0

    def A(name, nbytes):
        nonlocal cur
        OFF[name] = cur
        cur += (nbytes + 31) // 32 * 32

    A("wbuf", 16384)
    A("rowbuf", 2048)
    A("rowbuf2", 2048)
    A("ones_row", 512)
    A("ones_col", 32)
    A("silu_cb", 32)
    A("cd", 64)
    A("modt", 384)
    A("par", 320)
    A("magic", 32)
    A("dt", 128)
    A("m96", 512)
    A("RA", 65536)
    A("RB", 65536)
    A("RC", 32768)
    A("WM", 24576)
    assert cur <= 211000, cur

    with (
        nc.sbuf_tensor("arena", [128, cur], U8) as arena,
        nc.psum_tensor("mm_ps", [128, 2048], F32) as mm_ps,
        nc.psum_tensor("bc_ps", [128, 1024], F32) as bc_ps,
        nc.psum_tensor("row_ps", [1, 512], F32) as row_ps,
        nc.psum_tensor("mod_ps", [1, 512], F32) as mod_ps,
        nc.Block() as block,
    ):
        from contextlib import ExitStack
        _es = ExitStack()
        SEM_NAMES = ["sPE", "sV", "sA", "qC", "qD", "qX", "qR", "qSp", "qDbg",
                     "qW0", "qW1", "qW2", "qW3", "qM0", "qM1", "qM2",
                     "qJ0", "qJ1", "qRS0", "qRS1", "qO0", "qO1"]
        SEM = {nm: _es.enter_context(nc.semaphore(nm)) for nm in SEM_NAMES}
        base = nc.lookup_mloc(arena).addr

        wbuf = place(nc, base, "v_wbuf", [128, 8192], BF16, OFF["wbuf"])
        rstd_bc = place(nc, base, "v_rstd_bc", [128, 1024], F32, OFF["RB"] + 28672)
        mrstd_bc = place(nc, base, "v_mrstd_bc", [128, 1024], F32, OFF["RB"] + 32768)
        rowb = place(nc, base, "v_rowb", [1, 512], F32, OFF["rowbuf"])
        rowb2 = place(nc, base, "v_rowb2", [1, 512], F32, OFF["rowbuf2"])
        ones_row = place(nc, base, "v_ones_row", [1, 128], F32, OFF["ones_row"])
        ones_col = place(nc, base, "v_ones_col", [128, 16], BF16, OFF["ones_col"])
        silu_cb = place(nc, base, "v_silu_cb", [128, 16], BF16, OFF["silu_cb"])
        cd = place(nc, base, "v_cd", [128, 16], F32, OFF["cd"])
        modt = place(nc, base, "v_modt", [128, 96], F32, OFF["modt"])
        par = place(nc, base, "v_par", [128, 80], F32, OFF["par"])
        magic = place(nc, base, "v_magic", [128, 1], F32, OFF["magic"])
        Dt = place(nc, base, "v_dt", [128, 32], F32, OFF["dt"])
        m96 = place(nc, base, "v_m96", [96, 128], F32, OFF["m96"])
        RA_x = place(nc, base, "v_RA_x", [128, 16384], F32, OFF["RA"])
        RA_xh = place(nc, base, "v_RA_xh", [128, 32768], BF16, OFF["RA"])
        RA_x2 = place(nc, base, "v_RA_x2", [128, 16384], F32, OFF["RA"])
        RA_alo = place(nc, base, "v_RA_alo", [128, 32768], BF16, OFF["RA"])
        RB_sz = place(nc, base, "v_RB_sz", [128, 32768], BF16, OFF["RB"])
        RB_sum = place(nc, base, "v_RB_sum", [1, 1024], F32, OFF["RB"])
        RB_sq = place(nc, base, "v_RB_sq", [1, 1024], F32, OFF["RB"] + 4096)
        RB_rstd = place(nc, base, "v_RB_rstd", [1, 1024], F32, OFF["RB"] + 8192)
        RB_mrstd = place(nc, base, "v_RB_mrstd", [1, 1024], F32, OFF["RB"] + 12288)
        RB_htmp = place(nc, base, "v_RB_htmp", [128, 2048], F32, OFF["RB"] + 16384)
        RB_xbf = place(nc, base, "v_RB_xbf", [128, 8192], BF16, OFF["RB"] + 36864)
        RC_h = place(nc, base, "v_RC_h", [128, 16384], BF16, OFF["RC"])
        WM_b = [place(nc, base, "v_WM_b%d" % s, [128, 4096], BF16,
                      OFF["WM"] + 8192 * s) for s in range(3)]
        WM_f = [place(nc, base, "v_WM_f%d" % s, [128, 2048], F32,
                      OFF["WM"] + 8192 * s) for s in range(3)]
        WM_g = place(nc, base, "v_WM_g", [128, 2048], F32, OFF["WM"])
        WM_st = place(nc, base, "v_WM_st", [128, 4096], BF16, OFF["WM"])
        WM_out = place(nc, base, "v_WM_out", [128, 2048], F32, OFF["WM"] + 8192)
        WM_jit = place(nc, base, "v_WM_jit", [128, 2048], F32, OFF["WM"] + 16384)

        def col(t, j):
            return t[:, j : j + 1]

        bank_free = [None] * 4
        unit_free = [None] * 4
        group_ctr = [0]
        unit_ctr = [0]

        def mm_bank_ap(b):
            return mm_ps[:, 512 * b : 512 * b + 512]

        def pe_wait(sem, cnt):
            P.wait("tensor", sem, cnt)

        def stage(wdram, Ktot, Mch, m_off, rhs_ap, rhs_ready, evac, hook=None,
                  posthook=None, npieces=1):
            Kch = Ktot // 128
            upg = Ktot // 2048
            assert upg in (1, 2, 4) and upg % npieces == 0
            upp = upg // npieces
            kpp = Kch // npieces
            ring_slots = 4 // upp
            total_pieces = Mch * npieces
            piece_info = []
            issued = [0]

            def issue_piece(p):
                u0 = unit_ctr[0] % 4
                for uu in range(u0, u0 + upp):
                    if unit_free[uu] is not None:
                        P.wait("gpsimd", *unit_free[uu])
                m = p // npieces
                pc = p % npieces
                src = wdram[m_off + m, :, pc * kpp * 128 : (pc + 1) * kpp * 128]
                dst = wbuf[:, u0 * 2048 : u0 * 2048 + kpp * 128]
                sem = "qW%d" % u0
                wv = P.op("gpsimd", lambda e, s=src, d=dst: e.dma_start(out=d, in_=s),
                          inc=sem, n=16)
                piece_info.append((u0, sem, wv))
                unit_ctr[0] += upp

            def ensure(upto):
                while issued[0] < min(total_pieces, upto):
                    issue_piece(issued[0])
                    issued[0] += 1

            ensure(npieces + max(0, ring_slots - npieces))
            for m in range(Mch):
                if hook is not None:
                    hook(m)
                ensure((m + 1) * npieces + max(0, ring_slots - npieces))
                g = group_ctr[0]
                group_ctr[0] += 2
                b0, b1 = g % 4, (g + 1) % 4
                for b in (b0, b1):
                    if bank_free[b] is not None:
                        pe_wait(*bank_free[b])
                banks = (b0, b1)
                v_final = None
                for k in range(Kch):
                    pc = k // kpp
                    u0, wsem, wv = piece_info[m * npieces + pc]
                    pe_wait(wsem, wv)
                    lhs = wbuf[:, u0 * 2048 + (k % kpp) * 128 :
                               u0 * 2048 + (k % kpp) * 128 + 128]
                    piece_end = (k + 1) % kpp == 0
                    for n in range(NT):
                        r = rhs_ready(k, n)
                        if r is not None:
                            pe_wait(*r)
                        is_inc = piece_end and n == NT - 1
                        v = P.op("tensor",
                                 lambda e, o=mm_bank_ap(banks[n]), l=lhs,
                                 r2=rhs_ap(k, n), st=(k == 0), sp=(k == Kch - 1):
                                 e.matmul(o, l, r2, start=st, stop=sp),
                                 inc="sPE" if is_inc else None)
                        if is_inc:
                            v_final = v
                    if piece_end:
                        for uu in range(u0, u0 + upp):
                            unit_free[uu] = ("sPE", v_final)
                assert b1 == b0 + 1
                f0 = evac(m, mm_ps[:, 512 * b0 : 512 * b0 + 1024], v_final)
                bank_free[b0] = f0
                bank_free[b1] = f0
                if posthook is not None:
                    posthook(m)

        P.op("vector", lambda e: e.memset(ones_col[:, :], 1.0))
        P.op("vector", lambda e: e.memset(ones_row[:, :], 1.0))
        v_const = P.op("vector", lambda e: e.memset(magic[:, :], MAGIC), inc="sV")

        x_c = P.op("sync", lambda e: e.dma_start(out=cd[:, :], in_=d_c), inc="qC", n=16)
        x_D = P.op("sync", lambda e: e.dma_start(out=Dt[:, :], in_=d_D), inc="qD", n=16)
        x_x = None
        for gq in range(4):
            src = d_x[512 * gq : 512 * gq + 512, :].rearrange("(kc p) t -> p kc t", p=128)
            dst = RA_x[:, 4096 * gq : 4096 * gq + 4096].rearrange("p (kc t) -> p kc t", t=T)
            x_x = P.op("sync", lambda e, s=src, d=dst: e.dma_start(out=d, in_=s),
                       inc="qX", n=16)

        P.wait("scalar", "qC", x_c)
        a_silu = P.op("scalar", lambda e: e.activation(cd[:, :], cd[:, :], AF.Silu),
                      inc="sA")
        P.wait("scalar", "sA", a_silu)
        a_siluc = P.op("scalar", lambda e: e.copy(silu_cb[:, :], cd[:, :]), inc="sA")

        def stats(srcf32, between=None):
            ev = None
            for n in range(NT):
                for sq in (0, 1):
                    av = None
                    for k in range(KC):
                        xin = srcf32[:, k * 1024 + n * 512 : k * 1024 + n * 512 + 512]
                        cv = RB_xbf[:, k * 512 : k * 512 + 512]
                        if sq == 0:
                            av = P.op("scalar", lambda e, o=cv, i=xin: e.copy(o, i),
                                      inc="sA" if k == KC - 1 else None)
                        else:
                            av = P.op("scalar",
                                      lambda e, o=cv, i=xin: e.activation(o, i, AF.Square),
                                      inc="sA" if k == KC - 1 else None)
                    if between is not None:
                        between()
                    P.wait("tensor", "sA", av)
                    pv = None
                    for k in range(KC):
                        cv = RB_xbf[:, k * 512 : k * 512 + 512]
                        pv = P.op("tensor",
                                  lambda e, r=cv, st=(k == 0), sp=(k == KC - 1):
                                  e.matmul(row_ps[:, :], ones_col[:, 0:1], r,
                                           start=st, stop=sp),
                                  inc="sPE" if k == KC - 1 else None)
                    P.wait("scalar", "sPE", pv)
                    P.wait("vector", "sPE", pv)
                    dst = (RB_sum if sq == 0 else RB_sq)[0:1, n * 512 : n * 512 + 512]
                    ev = P.op("vector", lambda e, d=dst: e.tensor_copy(d, row_ps[:, :]),
                              inc="sV")
                    P.wait("tensor", "sV", ev)
            return ev

        P.wait("scalar", "qX", x_x)

        PIECES = []
        for i in range(16):
            PIECES.append((256 * i, 256, "b"))
        for i in range(16):
            PIECES.append((6144 + 256 * i, 256, "b"))
        for i in range(16):
            PIECES.append((4096 + 128 * i, 128, "f"))
        for i in range(16):
            PIECES.append((10240 + 128 * i, 128, "f"))

        PIECE_SUB = []
        nb = nf = 0
        for (_j0, _w, _k) in PIECES:
            if _k == "b":
                PIECE_SUB.append(nb); nb += 1
            else:
                PIECE_SUB.append(nf); nf += 1

        mod_state = {"mm": [0, 0, 0], "store": [0, 0], "idx": 0, "last_mm": 0,
                     "dma_idx": 0, "dma_v": {}}

        def mod_piece_dma():
            i = mod_state["dma_idx"]
            if i >= len(PIECES):
                return
            mod_state["dma_idx"] += 1
            j0, wdt, kind = PIECES[i]
            s = i % 3
            if mod_state["mm"][s]:
                P.wait("gpsimd", "sPE", mod_state["mm"][s])
            if kind == "b":
                srcw = d_wmodB[PIECE_SUB[i], :, :]
                slot = WM_b[s][:, :]
            else:
                srcw = d_wmodF[PIECE_SUB[i], :, :]
                slot = WM_f[s][:, :]
            msem = "qM%d" % s
            mod_state["dma_v"][i] = (msem, P.op(
                "gpsimd", lambda e, s_=srcw, d_=slot: e.dma_start(out=d_, in_=s_),
                inc=msem, n=16))

        def mod_piece():
            if mod_state["dma_idx"] <= mod_state["idx"]:
                mod_piece_dma()
            i = mod_state["idx"]
            mod_state["idx"] += 1
            j0, wdt, kind = PIECES[i]
            s = i % 3
            pe_wait(*mod_state["dma_v"][i])
            pe_wait("sA", a_siluc)
            pv = None
            for kc in range(KC):
                if kind == "b":
                    lhs = silu_cb[:, kc : kc + 1]
                    rhs = WM_b[s][:, kc * wdt : kc * wdt + wdt]
                else:
                    lhs = cd[:, kc : kc + 1]
                    rhs = WM_f[s][:, kc * wdt : kc * wdt + wdt]
                pv = P.op("tensor",
                          lambda e, l=lhs, r=rhs, st=(kc == 0), sp=(kc == KC - 1),
                          w_=wdt: e.matmul(mod_ps[0:1, 0:w_], l, r, start=st, stop=sp),
                          inc="sPE" if kc == KC - 1 else None)
            mod_state["mm"][s] = pv
            mod_state["last_mm"] = pv
            rs = i % 2
            rb = rowb if rs == 0 else rowb2
            if mod_state["store"][rs]:
                P.wait("scalar", "qRS%d" % rs, mod_state["store"][rs])
            P.wait("scalar", "sPE", pv)
            ev = P.op("scalar", lambda e, r=rb, w_=wdt: e.copy(
                r[0:1, 0:w_], mod_ps[0:1, 0:w_]), inc="sA")
            P.wait("tensor", "sA", ev)
            P.wait("sync", "sA", ev)
            J0 = j0 // 128
            nrow = wdt // 128
            sv = P.op("sync",
                      lambda e, r=rb[0:1, 0:wdt].rearrange("p (a c) -> p a c", c=128),
                      d=d_mod[J0 : J0 + nrow, :].rearrange("a c -> () a c"):
                      e.dma_start(out=d, in_=r),
                      inc="qRS%d" % rs, n=16)
            mod_state["store"][rs] = sv

        mod_piece_dma()
        mod_piece_dma()

        def p0_between():
            for _ in range(4):
                mod_piece_dma()
                mod_piece()

        stats1 = stats(RA_x, between=p0_between)
        CP["saca_rs0"] = P.cnt["qRS0"]
        CP["saca_rs1"] = P.cnt["qRS1"]



        def row_math():
            v1 = P.op("vector", lambda e: e.tensor_scalar(
                RB_sum[:, :], RB_sum[:, :], 1.0 / 2048.0, None, ALU.mult), inc="sV")
            v2 = P.op("vector", lambda e: e.tensor_scalar(
                RB_sq[:, :], RB_sq[:, :], 1.0 / 2048.0, EPS, ALU.mult, ALU.add),
                inc="sV")
            P.wait("vector", "sV", v1)
            v3 = P.op("vector", lambda e: e.tensor_tensor(
                RB_rstd[:, :], RB_sum[:, :], RB_sum[:, :], ALU.mult), inc="sV")
            P.wait("vector", "sV", max(v2, v3))
            v4 = P.op("vector", lambda e: e.tensor_tensor(
                RB_sq[:, :], RB_sq[:, :], RB_rstd[:, :], ALU.subtract), inc="sV")
            P.wait("scalar", "sV", v4)
            a1 = P.op("scalar", lambda e: e.activation(
                RB_rstd[:, :], RB_sq[:, :], AF.Ln), inc="sA")
            P.wait("scalar", "sA", a1)
            a2 = P.op("scalar", lambda e: e.activation(
                RB_rstd[:, :], RB_rstd[:, :], AF.Exp, scale=-0.5), inc="sA")
            P.wait("vector", "sA", a2)
            v5 = P.op("vector", lambda e: e.tensor_tensor(
                RB_mrstd[:, :], RB_sum[:, :], RB_rstd[:, :], ALU.mult), inc="sV")
            return v5

        P.wait("vector", "sV", stats1)
        rows1 = row_math()

        def bcast(src_row, dst_bc, ready_v):
            pe_wait("sV", ready_v)
            pv = None
            for n in range(NT):
                pv = P.op("tensor",
                          lambda e, r=src_row[0:1, n * 512 : n * 512 + 512],
                          o=bc_ps[:, n * 512 : n * 512 + 512]:
                          e.matmul(o, ones_row[0:1, :], r, start=True, stop=True),
                          inc="sPE")
            P.wait("vector", "sPE", pv)
            ev = P.op("vector", lambda e, d=dst_bc: e.tensor_copy(d[:, :], bc_ps[:, :]),
                      inc="sV")
            P.wait("tensor", "sV", ev)
            return ev

        bc1 = bcast(RB_rstd, rstd_bc, rows1)
        bc2 = bcast(RB_mrstd, mrstd_bc, rows1)

        def make_ln(srcf32):
            lv = []
            for k in range(KC):
                ht = RB_htmp[:, (k % 2) * 1024 : (k % 2) * 1024 + 1024]
                xin = srcf32[:, k * 1024 : k * 1024 + 1024]
                t1 = P.op("vector", lambda e, o=ht, i=xin: e.tensor_tensor(
                    o, i, rstd_bc[:, :], ALU.mult), inc="sV")
                P.wait("vector", "sV", t1)
                t2 = P.op("vector",
                          lambda e, o=RC_h[:, k * 1024 : k * 1024 + 1024], i=ht:
                          e.tensor_tensor(o, i, mrstd_bc[:, :], ALU.subtract),
                          inc="sV")
                lv.append(t2)
            return lv

        def make_h(lv, shift_col, scale_col):
            hv = []
            for k in range(KC):
                P.wait("scalar", "sV", lv[k])
                t3 = P.op("scalar",
                          lambda e, o=RC_h[:, k * 1024 : k * 1024 + 1024],
                          s1=scale_col(k), s2=shift_col(k):
                          e.activation(o, o, AF.Identity, bias=s2, scale=s1),
                          inc="sA")
                hv.append(t3)
            return hv

        P.wait("vector", "sV", max(bc1, bc2))
        lv1 = make_ln(RA_x)
        P.wait("sync", "qRS0", CP["saca_rs0"])
        P.wait("sync", "qRS1", CP["saca_rs1"])
        r1 = P.op("sync", lambda e: e.dma_start(out=m96[0:32, :], in_=d_mod[0:32, :]),
                  inc="qR", n=16)
        P.wait("vector", "qR", r1)
        tv = None
        for pb in range(4):
            tv = P.op("vector",
                      lambda e, o=modt[32 * pb : 32 * pb + 32, 0:32],
                      i=m96[0:32, 32 * pb : 32 * pb + 32]: e.transpose(o, i),
                      inc="sV")
        P.wait("vector", "sV", tv)
        v_cap1 = P.op("vector", lambda e: e.tensor_scalar(
            par[:, 0:16], modt[:, 16:32], 1.0, None, ALU.add), inc="sV")
        P.wait("scalar", "sV", v_cap1)
        hv1 = make_h(lv1, lambda k: col(modt, k), lambda k: col(par, k))
        CP["h1_done"] = P.cnt["sA"]

        def h_rhs(k, n):
            return RC_h[:, k * 1024 + n * 512 : k * 1024 + n * 512 + 512]

        first_a1x = [True]

        def evac_a1x(m, ps, mmv):
            P.wait("scalar", "sPE", mmv)
            if first_a1x[0]:
                P.wait("scalar", "sA", CP["h1_done"])
                first_a1x[0] = False
            o = RA_xh[:, m * 1024 : m * 1024 + 1024]
            av = P.op("scalar", lambda e, oo=o, pp=ps: e.copy(oo, pp), inc="sA")
            return ("sA", av)

        def a1_hook(m):
            if m % 2 == 0:
                mod_piece_dma()
                if mod_state["idx"] < mod_state["dma_idx"] - 1 or \
                   mod_state["dma_idx"] == len(PIECES):
                    if mod_state["idx"] < len(PIECES):
                        mod_piece()

        stage(d_win, C, 32, 0, h_rhs, lambda k, n: ("sA", hv1[k]), evac_a1x,
              hook=a1_hook)
        CP["a1x_evac"] = P.cnt["sA"]

        def evac_a1z(m, ps, mmv):
            P.wait("scalar", "sPE", mmv)
            o = RB_sz[:, m * 1024 : m * 1024 + 1024]
            av = P.op("scalar", lambda e, oo=o, pp=ps: e.activation(oo, pp, AF.Silu),
                      inc="sA")
            return ("sA", av)

        stage(d_win, C, 32, 32, h_rhs, lambda k, n: ("sA", hv1[k]), evac_a1z,
              hook=a1_hook)
        CP["a1z_evac"] = P.cnt["sA"]


        dbg_gate = {}

        def dump(dram, sb_ap, gates):
            for sem, cnt in gates:
                P.wait("gpsimd", sem, cnt)
            return P.op("gpsimd", lambda e, d=dram, s=sb_ap: e.dma_start(out=d, in_=s),
                        inc="qDbg", n=16)

        if debug:
            dump(d_dh, RC_h[:, :], [("sV", CP["h1_done"])])
            dump(d_dxh, RA_xh[:, :], [("sA", CP["a1x_evac"])])
            dbg_gate["sz"] = dump(d_dsz, RB_sz[:, :], [("sA", CP["a1z_evac"])])

        def xh_rhs(k, n):
            return RA_xh[:, k * 1024 + n * 512 : k * 1024 + n * 512 + 512]

        first_a2 = [True]

        def evac_a2(m, ps, mmv):
            P.wait("vector", "sPE", mmv)
            if first_a2[0]:
                P.wait("vector", "qD", x_D)
                P.wait("vector", "sA", CP["a1z_evac"])
                if debug:
                    P.wait("vector", "qDbg", dbg_gate["sz"])
                first_a2[0] = False
            o = RB_sz[:, m * 1024 : m * 1024 + 1024]
            vv = P.op("vector",
                      lambda e, oo=o, pp=ps, dc=col(Dt, m):
                      e.scalar_tensor_tensor(oo, pp, dc, oo, ALU.mult, ALU.mult),
                      inc="sV")
            return ("sV", vv)

        def a2_hook(m):
            if m % 2 == 0:
                mod_piece_dma()
                if mod_state["idx"] < len(PIECES):
                    mod_piece()

        stage(d_conv, INTER, KI, 0, xh_rhs, lambda k, n: ("sA", CP["a1x_evac"]),
              evac_a2, hook=a2_hook)
        CP["a2_evac"] = P.cnt["sV"]

        P.wait("sync", "qRS0", P.cnt["qRS0"])
        P.wait("sync", "qRS1", P.cnt["qRS1"])
        r2 = P.op("sync", lambda e: e.dma_start(out=m96[32:96, :], in_=d_mod[32:96, :]),
                  inc="qR", n=16)
        P.wait("vector", "qR", r2)
        tv2 = None
        for pb in range(4):
            for jb in (1, 2):
                tv2 = P.op("vector",
                           lambda e, o=modt[32 * pb : 32 * pb + 32,
                                            32 * jb : 32 * jb + 32],
                           i=m96[32 * jb : 32 * jb + 32, 32 * pb : 32 * pb + 32]:
                           e.transpose(o, i),
                           inc="sV")
        P.wait("vector", "sV", tv2)
        P.op("vector", lambda e: e.tensor_scalar(
            par[:, 32:48], modt[:, 64:80], 1.0, None, ALU.add), inc="sV")
        P.wait("scalar", "sV", tv2)
        sg1 = P.op("scalar", lambda e: e.activation(
            par[:, 64:80], modt[:, 32:48], AF.Sign), inc="sA")
        ab1 = P.op("scalar", lambda e: e.activation(
            par[:, 16:32], modt[:, 32:48], AF.Abs), inc="sA")
        P.wait("vector", "sA", ab1)
        am1 = P.op("vector", lambda e: e.tensor_scalar(
            par[:, 16:32], par[:, 16:32], 0.25, None, ALU.max), inc="sV")
        P.wait("vector", "sV", am1)
        ga_v = P.op("vector", lambda e: e.tensor_tensor(
            par[:, 16:32], par[:, 16:32], par[:, 64:80], ALU.mult), inc="sV")
        P.wait("scalar", "sV", ga_v)
        sg2 = P.op("scalar", lambda e: e.activation(
            par[:, 64:80], modt[:, 80:96], AF.Sign), inc="sA")
        ab2 = P.op("scalar", lambda e: e.activation(
            par[:, 48:64], modt[:, 80:96], AF.Abs), inc="sA")
        P.wait("vector", "sA", ab2)
        am2 = P.op("vector", lambda e: e.tensor_scalar(
            par[:, 48:64], par[:, 48:64], 0.25, 0.5, ALU.max, ALU.mult), inc="sV")
        P.wait("vector", "sV", am2)
        g2 = P.op("vector", lambda e: e.tensor_tensor(
            par[:, 48:64], par[:, 48:64], par[:, 64:80], ALU.mult), inc="sV")
        CP["params2"] = P.cnt["sV"]

        def yz_rhs(k, n):
            return RB_sz[:, k * 1024 + n * 512 : k * 1024 + n * 512 + 512]

        xjit_v = {}
        xjit_slot_free = {}

        def xjit_dma(m):
            slot = m % 2
            if slot in xjit_slot_free:
                P.wait("sync", "sV", xjit_slot_free[slot])
            P.wait("sync", "sPE", mod_state["last_mm"])
            dst = WM_jit[:, slot * 1024 : slot * 1024 + 1024]
            xjit_v[m] = P.op("sync",
                             lambda e, s=d_x[128 * m : 128 * m + 128, :], d=dst:
                             e.dma_start(out=d, in_=s),
                             inc="qJ%d" % slot, n=16)

        xjit_dma(0)
        xjit_dma(1)

        st_rows = [row_ps[0:1, 0:512], mod_ps[0:1, 0:512],
                   bc_ps[0:1, 0:512], bc_ps[0:1, 512:1024]]
        st_cv = [[WM_st[:, 0:512], WM_st[:, 512:1024]],
                 [WM_st[:, 1024:1536], WM_st[:, 1536:2048]]]
        st_sq = [[WM_st[:, 2048:2560], WM_st[:, 2560:3072]],
                 [WM_st[:, 3072:3584], WM_st[:, 3584:4096]]]
        st_state = {"pv": [0, 0, 0, 0], "cv": {}}

        def a3_stats_convs(m):
            P.wait("scalar", "sV", a3_x2v[m])
            if m >= 2:
                P.wait("scalar", "sPE", max(st_state["pv"]))
            last = None
            for n in range(NT):
                xin = RA_x2[:, m * 1024 + n * 512 : m * 1024 + n * 512 + 512]
                P.op("scalar", lambda e, o=st_cv[m % 2][n], i=xin: e.copy(o, i))
                last = P.op("scalar",
                            lambda e, o=st_sq[m % 2][n], i=xin:
                            e.activation(o, i, AF.Square),
                            inc="sA" if n == NT - 1 else None)
            st_state["cv"][m] = last

        def a3_stats_mms(m):
            P.wait("tensor", "sA", st_state["cv"][m])
            for n in range(NT):
                st_state["pv"][n] = P.op(
                    "tensor", lambda e, r=st_cv[m % 2][n], o=st_rows[n],
                    st=(m == 0), sp=(m == KC - 1):
                    e.matmul(o, ones_col[:, 0:1], r, start=st, stop=sp),
                    inc="sPE")
            for n in range(NT):
                st_state["pv"][2 + n] = P.op(
                    "tensor", lambda e, r=st_sq[m % 2][n], o=st_rows[2 + n],
                    st=(m == 0), sp=(m == KC - 1):
                    e.matmul(o, ones_col[:, 0:1], r, start=st, stop=sp),
                    inc="sPE")

        def a3_post(m):
            if m + 2 < KC:
                xjit_dma(m + 2)
            a3_stats_convs(m)
            if m >= 1:
                a3_stats_mms(m - 1)
            if m == KC - 1:
                a3_stats_mms(m)

        a3_x2v = {}

        def evac_a3(m, ps, mmv):
            P.wait("vector", "sPE", mmv)
            P.wait("vector", "qJ%d" % (m % 2), xjit_v[m])
            P.wait("vector", "sV", CP["params2"])
            slot = m % 2
            xj = WM_jit[:, slot * 1024 : slot * 1024 + 1024]
            o = RA_x2[:, m * 1024 : m * 1024 + 1024]
            vv = P.op("vector",
                      lambda e, oo=o, pp=ps, gc=col(par, 16 + m), xx=xj:
                      e.scalar_tensor_tensor(oo, pp, gc, xx, ALU.mult, ALU.add),
                      inc="sV")
            xjit_slot_free[slot] = vv
            a3_x2v[m] = vv
            return ("sV", vv)

        stage(d_wout, INTER, KC, 0, yz_rhs, lambda k, n: ("sV", CP["a2_evac"]),
              evac_a3, posthook=a3_post)
        CP["a3_evac"] = P.cnt["sV"]
        CP["a3_mm"] = P.cnt["sPE"]

        if debug:
            dbg_gate["yz"] = dump(d_dyz, RB_sz[:, :],
                                  [("sV", CP["a2_evac"]), ("sPE", CP["a3_mm"])])
            dump(d_dbc[:, 0:1024], rstd_bc[:, :], [("sV", CP["h1_done"])])
            dump(d_dbc[:, 1024:2048], mrstd_bc[:, :], [])

        if debug:
            P.wait("scalar", "qDbg", dbg_gate["yz"])
            P.wait("vector", "qDbg", dbg_gate["yz"])
        P.wait("vector", "sPE", max(st_state["pv"]))
        for nn in range(NT):
            P.op("vector", lambda e, d=RB_sum[0:1, nn * 512 : nn * 512 + 512],
                 s=st_rows[nn]: e.tensor_copy(d, s), inc="sV")
        for nn in range(NT):
            P.op("vector", lambda e, d=RB_sq[0:1, nn * 512 : nn * 512 + 512],
                 s=st_rows[2 + nn]: e.tensor_copy(d, s), inc="sV")
        stats2 = P.cnt["sV"]
        P.wait("tensor", "sV", stats2)
        P.wait("vector", "sV", stats2)
        rows2 = row_math()
        bc3 = bcast(RB_rstd, rstd_bc, rows2)
        bc4 = bcast(RB_mrstd, mrstd_bc, rows2)
        P.wait("vector", "sV", max(bc3, bc4))
        lv2 = make_ln(RA_x2)
        P.wait("scalar", "sV", CP["params2"])
        hv2 = make_h(lv2, lambda k: col(modt, 48 + k), lambda k: col(par, 32 + k))
        CP["h2_done"] = P.cnt["sA"]

        spill_v = None
        for gq in range(4):
            P.wait("sync", "sV", a3_x2v[4 * gq + 3])
            src = RA_x2[:, 4096 * gq : 4096 * gq + 4096].rearrange(
                "p (kc t) -> p kc t", t=T)
            dst = d_x2[512 * gq : 512 * gq + 512, :].rearrange(
                "(kc p) t -> p kc t", p=128)
            spill_v = P.op("sync", lambda e, s=src, d=dst: e.dma_start(out=d, in_=s),
                           inc="qSp", n=16)
        CP["spill"] = spill_v

        gelu_free = {}
        first_b1 = [True]
        a_ready = {}
        GX = [WM_g[:, 0:1024], WM_jit[:, 0:1024]]
        GU = [WM_g[:, 1024:2048], WM_jit[:, 1024:2048]]

        def evac_b1(m, ps, mmv):
            t_slot = m % 2
            X = GX[t_slot]
            U = GU[t_slot]
            P.wait("scalar", "sPE", mmv)
            if first_b1[0]:
                P.wait("scalar", "sA", CP["h2_done"])
                P.wait("scalar", "sV", CP["a3_evac"])
                P.wait("vector", "qSp", CP["spill"])
                first_b1[0] = False
            if t_slot in gelu_free:
                P.wait("scalar", "sV", gelu_free[t_slot])
            e1 = P.op("scalar", lambda e, o=X, pp=ps: e.activation(
                o, pp, AF.Identity, bias=magic[:, :], scale=10.0), inc="sA")
            P.wait("vector", "sA", e1)
            v1 = P.op("vector", lambda e, o=X: e.tensor_scalar(
                o, o, MAGIC, QMAX, ALU.subtract, ALU.min), inc="sV")
            P.wait("vector", "sV", v1)
            v2 = P.op("vector", lambda e, o=X: e.tensor_scalar(
                o, o, QMIN, 0.1, ALU.max, ALU.mult), inc="sV")
            P.wait("scalar", "sV", v2)
            s2 = P.op("scalar", lambda e, o=U, i=X: e.activation(o, i, AF.Square),
                      inc="sA")
            P.wait("vector", "sA", s2)
            v3 = P.op("vector", lambda e, o=U, i=X: e.tensor_tensor(
                o, o, i, ALU.mult), inc="sV")
            P.wait("vector", "sV", v3)
            v4 = P.op("vector", lambda e, o=U, i=X: e.scalar_tensor_tensor(
                o, o, GC, i, ALU.mult, ALU.add), inc="sV")
            P.wait("scalar", "sV", v4)
            s3 = P.op("scalar", lambda e, o=U: e.activation(o, o, AF.Tanh, scale=GK),
                      inc="sA")
            P.wait("vector", "sA", s3)
            if m < 32:
                ao = RA_alo[:, m * 1024 : m * 1024 + 1024]
            else:
                ao = RB_sz[:, (m - 32) * 1024 : (m - 32) * 1024 + 1024]
            v5 = P.op("vector", lambda e, o=ao, u=U, i=X: e.scalar_tensor_tensor(
                o, u, 1.0, i, ALU.add, ALU.mult), inc="sV")
            gelu_free[t_slot] = v5
            a_ready[m] = v5
            return ("sA", e1)

        P.wait("scalar", "sV", v_const)
        stage(d_w1, C, KF, 0, h_rhs, lambda k, n: ("sA", hv2[k]), evac_b1)
        CP["b1_evac"] = P.cnt["sV"]

        def a_rhs(k, n):
            if k < 32:
                return RA_alo[:, k * 1024 + n * 512 : k * 1024 + n * 512 + 512]
            return RB_sz[:, (k - 32) * 1024 + n * 512 : (k - 32) * 1024 + n * 512 + 512]

        x2jit_v = {}
        x2jit_slot_free = {}
        first_x2jit = [True]

        def x2jit_dma(m):
            slot = m % 2
            if first_x2jit[0]:
                P.wait("sync", "qSp", CP["spill"])
                P.wait("sync", "sV", CP["b1_evac"])
                first_x2jit[0] = False
            if slot in x2jit_slot_free:
                P.wait("sync", "sV", x2jit_slot_free[slot])
            dst = WM_jit[:, slot * 1024 : slot * 1024 + 1024]
            x2jit_v[m] = P.op("sync",
                              lambda e, s=d_x2[128 * m : 128 * m + 128, :], d=dst:
                              e.dma_start(out=d, in_=s),
                              inc="qJ%d" % slot, n=16)

        x2jit_dma(0)
        x2jit_dma(1)

        def b2_post(m):
            if m + 2 < KC:
                x2jit_dma(m + 2)

        out_free = {}

        def evac_b2(m, ps, mmv):
            slot = m % 2
            P.wait("vector", "sPE", mmv)
            P.wait("vector", "qJ%d" % slot, x2jit_v[m])
            if slot in out_free:
                P.wait("vector", "qO%d" % slot, out_free[slot])
            xj = WM_jit[:, slot * 1024 : slot * 1024 + 1024]
            o = WM_out[:, slot * 1024 : slot * 1024 + 1024]
            vv = P.op("vector",
                      lambda e, oo=o, pp=ps, gc=col(par, 48 + m), xx=xj:
                      e.scalar_tensor_tensor(oo, pp, gc, xx, ALU.mult, ALU.add),
                      inc="sV")
            x2jit_slot_free[slot] = vv
            P.wait("sync", "sV", vv)
            sv = P.op("sync",
                      lambda e, s=WM_out[:, slot * 1024 : slot * 1024 + 1024],
                      d=d_out[128 * m : 128 * m + 128, :]:
                      e.dma_start(out=d, in_=s),
                      inc="qO%d" % slot, n=16)
            out_free[slot] = sv
            return ("sV", vv)

        stage(d_w2, FF, KC, 0, a_rhs, lambda k, n: ("sV", a_ready[k]),
              evac_b2, posthook=b2_post, npieces=2)

        if debug:
            dump(d_dh2, RC_h[:, :], [("sV", CP["h2_done"])])
            dump(d_da[:, 0:32768], RA_alo[:, :], [("sV", CP["b1_evac"])])
            dump(d_da[:, 32768:65536], RB_sz[:, :], [])
            dump(d_dmodt, modt[:, :], [])
            dump(d_dpar, par[:, :], [])

        P.wait("sync", "qO0", P.cnt["qO0"])
        P.wait("sync", "qO1", P.cnt["qO1"])

        for engname in ENGS:
            fns = P.items[engname]
            getattr(block, engname)(lambda eng, fns=fns: [f(eng, SEM) for f in fns])

    return nc


_NC_CACHE = None


def _get_nc():
    global _NC_CACHE
    if _NC_CACHE is None:
        _NC_CACHE = build_program()
    return _NC_CACHE


def kernel(
    L0_attn_hidden_states,
    L0_attn_W_in,
    L0_attn_conv_w,
    L0_attn_W_x,
    L0_attn_W_dt,
    L0_attn_b_dt,
    L0_attn_D,
    L0_attn_W_out,
    L0_ffn_W1,
    L0_ffn_W2,
    L0_c,
    L0_wmod,
):
    f = lambda a: np.ascontiguousarray(np.asarray(a, dtype=np.float32))

    def tm(w):
        wT = np.asarray(w, dtype=np.float32).T
        K, M = wT.shape
        return np.ascontiguousarray(
            wT.reshape(K // 128, 128, M // 128, 128).transpose(2, 1, 0, 3)
            .reshape(M // 128, 128, K))

    xT = f(np.asarray(L0_attn_hidden_states, dtype=np.float32).reshape(8192, 2048).T)
    winT = tm(L0_attn_W_in)
    convT = tm(L0_attn_conv_w)
    woutT = tm(L0_attn_W_out)
    w1T = tm(L0_ffn_W1)
    w2T = tm(L0_ffn_W2)
    wmod = f(L0_wmod)
    PIECES_H = ([(256 * i, 256) for i in range(16)]
                + [(6144 + 256 * i, 256) for i in range(16)])
    PIECESF_H = ([(4096 + 128 * i, 128) for i in range(16)]
                 + [(10240 + 128 * i, 128) for i in range(16)])
    wmodB = np.stack([
        np.ascontiguousarray(
            wmod[:, j0:j0 + w].reshape(16, 128, w).transpose(1, 0, 2).reshape(128, 16 * w))
        for j0, w in PIECES_H])
    wmodF = np.stack([
        np.ascontiguousarray(
            wmod[:, j0:j0 + w].reshape(16, 128, w).transpose(1, 0, 2).reshape(128, 16 * w))
        for j0, w in PIECESF_H])
    cT = f(np.asarray(L0_c, dtype=np.float32).reshape(16, 128).T)
    dT = f(np.asarray(L0_attn_D, dtype=np.float32).reshape(32, 128).T)

    nc = _get_nc()
    in_maps = []
    for i in range(NCORES):
        in_maps.append({
            "xT": np.ascontiguousarray(xT[:, T * i : T * (i + 1)]),
            "winTM": winT,
            "convTM": convT,
            "woutTM": woutT,
            "w1TM": w1T,
            "w2TM": w2T,
            "wmodB": wmodB,
            "wmodF": wmodF,
            "cT": cT,
            "dT": dT,
        })
    res = run_bass_kernel_spmd(nc, in_maps, core_ids=list(range(NCORES)))
    out = np.concatenate([r["outT"] for r in res.results], axis=1)
    return np.ascontiguousarray(out.T).reshape(1, 8192, 2048)
